# version 18
# baseline (speedup 1.0000x reference)
import numpy as np
import ml_dtypes
import concourse.bass as bass
import concourse.mybir as mybir
from concourse.bass_utils import run_bass_kernel_spmd

F32 = mybir.dt.float32
BF16 = mybir.dt.bfloat16
I32 = mybir.dt.int32
AF = mybir.ActivationFunctionType
ALU = mybir.AluOpType
AX = mybir.AxisListType

P = 128
T = 2048
TH = 1024
DC = 8
NCORES = 8
LN_EPS = 1e-5
QS = 1.0 / 127.0
BF = ml_dtypes.bfloat16


def _ceil(a, b):
    return (a + b - 1) // b


def view(buf, *shape, off=0):
    n = int(np.prod(shape))
    ap = buf.ap()[:, off:off + n]
    if len(shape) == 1:
        return ap
    pat = "p (" + " ".join(f"d{i}" for i in range(len(shape))) + ") -> p " + \
          " ".join(f"d{i}" for i in range(len(shape)))
    return ap.rearrange(pat, **{f"d{i}": s for i, s in enumerate(shape)})


class StopBuild(Exception):
    pass


def build_program(stop_after=None, dbg=None):
    nc = bass.Bass("TRN2", target_bir_lowering=False)

    x_in = nc.dram_tensor("x", [1024, T], F32, kind="ExternalInput").ap()
    y_out = nc.dram_tensor("y", [1024, T], F32, kind="ExternalOutput").ap()

    def din(name, shape, dt=F32):
        return nc.dram_tensor(name, shape, dt, kind="ExternalInput").ap()

    wq = din("wq", [1024, 1024]); wk = din("wk", [1024, 1024])
    wv = din("wv", [1024, 1024]); wo = din("wo", [1024, 1024])
    wg1 = din("wg1", [1024, 128]); wg2 = din("wg2", [128, 1024])
    gn_col = din("gn_col", [128, 8])
    w0g = din("w0g", [1024, 3072], I32)
    w0u = din("w0u", [1024, 3072], I32)
    w0d = din("w0d", [3072, 1024], I32)
    pl = din("pl", [1024, 1024]); pg = din("pg", [1024, 1024])
    wlin = din("wlin", [1024, 3072]); wgin = din("wgin", [1024, 3072])
    wlout = din("wlout", [1024, 1024]); wgout = din("wgout", [1024, 1024])
    blq_col = din("blq_col", [128, 8]); blk_col = din("blk_col", [128, 8])
    bgq_col = din("bgq_col", [128, 8]); bgk_col = din("bgk_col", [128, 8])
    blv_row = din("blv_row", [1, 1024]); bgv_row = din("bgv_row", [1, 1024])
    blout_row = din("blout_row", [1, 1024]); bgout_row = din("bgout_row", [1, 1024])
    fusw = din("fusw", [1, 2])
    g1_col = din("g1_col", [128, 8]); b1_col = din("b1_col", [128, 8])
    g2_col = din("g2_col", [128, 8]); b2_col = din("b2_col", [128, 8])
    fc1q = din("fc1q", [1024, 4096], I32)
    fc2q = din("fc2q", [4096, 1024], I32)
    s1_rows = din("s1_rows", [32, 1024]); s2_rows = din("s2_rows", [32, 1024])
    fb1_col = din("fb1_col", [128, 32]); fb2_row = din("fb2_row", [1, 1024])
    fb2_col = din("fb2_col", [128, 8])

    jj, ii = np.meshgrid(np.arange(P), np.arange(P), indexing="ij")
    ident_h = nc.inline_tensor(np.eye(P, dtype=BF), name="c_ident").ap()
    gmask_h = nc.inline_tensor(np.tile((ii >= jj).astype(BF), (1, 4)), name="c_gmask").ap()
    l1m_h = nc.inline_tensor((jj > ii).astype(BF), name="c_l1m").ap()
    onesr_h = nc.inline_tensor(np.ones((1, 512), dtype=BF), name="c_onesr").ap()
    onesc_h = nc.inline_tensor(np.ones((1, 128), dtype=BF), name="c_onesc").ap()
    sumc_h = nc.inline_tensor(np.ones((128, 1), dtype=BF), name="c_sumc").ap()
    _cb = np.zeros((128, 128), dtype=BF); _cb[64, :] = 1.0
    conb_h = nc.inline_tensor(_cb, name="c_conb").ap()
    _fconsts = [LN_EPS, -0.5, 1.0 / 1024.0, QS, 0.125, 1.0, 0.0]
    fcon_h = nc.inline_tensor(np.tile(np.asarray(_fconsts, np.float32), (P, 1)),
                              name="c_fcon").ap()

    A = nc.alloc_sbuf_tensor
    XRES = A("XRES", [P, DC, T], F32)
    BN = 8320
    B1 = A("B1", [P, BN], BF16)
    B2 = A("B2", [P, BN], BF16)
    B3 = A("B3", [P, BN], BF16)
    B4 = A("B4", [P, BN], BF16)
    B5 = A("B5", [P, BN], BF16)
    WS = A("WS", [P, 4096], F32)
    WB = A("WB", [P, 8192], BF16)
    SGLA = A("SGLA", [P, DC, 128], F32)
    SM = A("SM", [P, 640], F32)
    SB2 = A("SB2", [P, 2048], BF16)
    BROW = A("BROW", [1, 2048], BF16)
    RSTG = A("RSTG", [1, 3072], F32)
    CID = A("CID", [P, 128], BF16)
    CGM = A("CGM", [P, 512], BF16)
    CL1 = A("CL1", [P, 128], BF16)
    CONR = A("CONR", [1, 512], BF16)
    CONC = A("CONC", [1, 128], BF16)
    CSUM = A("CSUM", [P, 1], BF16)
    CONB = A("CONB", [P, 128], BF16)
    CFC = A("CFC", [P, len(_fconsts)], F32)
    for _j, _v in enumerate(_fconsts):
        nc.const_aps.register(F32, float(_v), CFC.ap()[:, _j:_j + 1])

    PSALL = nc.alloc_psum_tensor("psall", [P, 4096], F32)

    class _PSView:
        def __init__(self, i):
            self.i = i

        def ap(self):
            return PSALL.ap()[:, self.i * 512:(self.i + 1) * 512]

    PS = [_PSView(i) for i in range(8)]

    def psb(i):
        return PS[i].ap().bitcast(BF16)

    def psw(i, n):
        return PSALL.ap()[:, i * 512:(i + n) * 512]

    _blkn = [0]
    _origBlock = nc.Block

    def _Block(tag):
        _blkn[0] += 1
        return _origBlock(f"{tag}{_blkn[0]}")
    nc.Block = _Block

    sems = {}
    cnt = {}

    def S(name):
        if name not in sems:
            sems[name] = nc.alloc_semaphore(name)
            cnt[name] = 0
        return sems[name]

    def inc(instr, name, k=1):
        instr.then_inc(S(name), k)
        cnt[name] += k
        return cnt[name]

    def val(name):
        S(name)
        return cnt[name]

    smoff = {}
    _smn = [0]

    def sm(name, width=8):
        if name not in smoff:
            smoff[name] = _smn[0]
            _smn[0] += width
            assert _smn[0] <= 640, "SM overflow"
        return SM.ap()[:, smoff[name]:smoff[name] + width]

    broff = {}
    _brn = [0]

    def brow(name, width=1024):
        if name not in broff:
            broff[name] = _brn[0]
            _brn[0] += width
            assert _brn[0] <= 2048, "BROW overflow"
        return BROW.ap()[:, broff[name]:broff[name] + width]

    with nc.Block("init") as blk:
        @blk.sync
        def _(sy):
            inc(sy.dma_start(out=XRES.ap(), in_=x_in.rearrange("(c p) t -> p c t", p=P)), "dm", 16)
            for dst, src in [(CID, ident_h), (CGM, gmask_h), (CL1, l1m_h),
                             (CONR, onesr_h), (CONC, onesc_h), (CSUM, sumc_h),
                             (CFC, fcon_h), (CONB, conb_h)]:
                inc(sy.dma_start(out=dst.ap(), in_=src), "dm", 16)
            for name, h in [("gn", gn_col), ("blq", blq_col), ("blk", blk_col),
                            ("bgq", bgq_col), ("bgk", bgk_col), ("g1", g1_col),
                            ("b1", b1_col), ("g2", g2_col), ("b2", b2_col)]:
                inc(sy.dma_start(out=sm(name, 8), in_=h), "dm", 16)
            inc(sy.dma_start(out=sm("fb1", 32), in_=fb1_col), "dm", 16)
            inc(sy.dma_start(out=sm("fb2c", 8), in_=fb2_col), "dm", 16)

        @blk.vector
        def _(v):
            v.wait_ge(S("dm"), val("dm"))
            v.memset(SGLA.ap(), 0.0)
            v.tensor_scalar_mul(out=sm("blq8"), in0=sm("blq"), scalar1=0.125)
            v.tensor_scalar_mul(out=sm("bgq8"), in0=sm("bgq"), scalar1=0.125)
            v.tensor_scalar_mul(out=sm("hfb2", 8), in0=sm("fb2c", 8), scalar1=0.5)

    def ensure_row(slot, hbm_row):
        with nc.Block("rowcvt") as blk:
            @blk.sync
            def _(sy):
                inc(sy.dma_start(out=RSTG.ap()[0:1, 0:1024], in_=hbm_row), "dm", 16)

            @blk.vector
            def _(v):
                v.wait_ge(S("dm"), val("dm"))
                v.tensor_copy(out=brow(slot), in_=RSTG.ap()[0:1, 0:1024])

    def dump(tag, buf, f32=False):
        if dbg != tag:
            return
        with nc.Block("dbg") as blk:
            @blk.sync
            def _(sy):
                yv = y_out.rearrange("(a b) t -> a (b t)", a=128)
                src = buf.ap() if f32 else buf.ap().bitcast(F32)
                n = src.shape[1] if len(src.shape) == 2 else int(np.prod(src.shape[1:]))
                srcf = src if len(src.shape) == 2 else src.rearrange("p a b -> p (a b)")
                inc(sy.dma_start(out=yv[:, 0:n], in_=srcf), "dm", 16)
                sy.wait_ge(S("dm"), val("dm"))
        raise StopBuild

    def ln_1024(src_f32, src_bf, out_view, scr1, scr2, gamma=None, beta=None,
                pf=None):
        srcx = src_f32 if src_f32 is not None else src_bf
        with nc.Block("ln") as blk:
            if pf is not None:
                @blk.sync
                def _(sy):
                    if callable(pf):
                        pf(sy)
                    else:
                        pf_emit(sy, *pf)

            @blk.vector
            def _(v):
                v.wait_ge(S("dm"), val("dm"))
                if src_f32 is not None:
                    v.tensor_copy(out=scr1, in_=src_f32)
                inc(v.drain(), "lnv", 1)

            @blk.scalar
            def _(s):
                s.activation(out=scr2, in_=srcx, func=AF.Square)
                inc(s.drain(), "lns", 1)

            @blk.tensor
            def _(t):
                t.wait_ge(S("lnv"), val("lnv"))
                t.wait_ge(S("lns"), val("lns"))
                xb = scr1 if src_f32 is not None else src_bf
                for b in range(2):
                    c0, c1 = b * 512, (b + 1) * 512
                    for c in range(DC):
                        t.matmul(PS[0 + b].ap()[0:1, 0:512], CSUM.ap(),
                                 xb[:, c, c0:c1], start=(c == 0), stop=(c == DC - 1))
                    for c in range(DC):
                        t.matmul(PS[2 + b].ap()[0:1, 0:512], CSUM.ap(),
                                 scr2[:, c, c0:c1], start=(c == 0), stop=(c == DC - 1))
                inc(t.drain(), "lnt", 1)

            @blk.vector
            def _(v):
                v.wait_ge(S("lnt"), val("lnt"))
                mu = RSTG.ap()[0:1, 1024:2048]
                var = RSTG.ap()[0:1, 2048:3072]
                for b in range(2):
                    bs = slice(b * 512, (b + 1) * 512)
                    v.tensor_scalar_mul(out=mu[:, bs], in0=PS[0 + b].ap()[0:1, 0:512],
                                        scalar1=1.0 / 1024.0)
                v.drain()
                for b in range(2):
                    bs = slice(b * 512, (b + 1) * 512)
                    v.tensor_mul(out=var[:, bs], in0=mu[:, bs], in1=mu[:, bs])
                    v.drain()
                    v.scalar_tensor_tensor(out=var[:, bs], in0=PS[2 + b].ap()[0:1, 0:512],
                                           scalar=1.0 / 1024.0, in1=var[:, bs],
                                           op0=ALU.mult, op1=ALU.subtract)
                inc(v.drain(), "lnv", 1)

            @blk.scalar
            def _(s):
                s.wait_ge(S("lnv"), val("lnv"))
                rstd = RSTG.ap()[0:1, 2048:3072]
                s.activation(out=rstd, in_=rstd, func=AF.Ln, bias=LN_EPS)
                s.drain()
                s.activation(out=rstd, in_=rstd, func=AF.Exp, scale=-0.5)
                inc(s.drain(), "lns", 1)

            @blk.vector
            def _(v):
                v.wait_ge(S("lns"), val("lns"))
                mu = RSTG.ap()[0:1, 1024:2048]
                rstd = RSTG.ap()[0:1, 2048:3072]
                arow = RSTG.ap().bitcast(BF16)[0:1, 0:1024]
                brw = RSTG.ap().bitcast(BF16)[0:1, 1024:2048]
                v.tensor_copy(out=arow, in_=rstd)
                v.scalar_tensor_tensor(out=brw, in0=mu, scalar=-1.0, in1=rstd,
                                       op0=ALU.mult, op1=ALU.mult)
                inc(v.drain(), "lnv", 1)

            @blk.tensor
            def _(t):
                t.wait_ge(S("lnv"), val("lnv"))
                arow = RSTG.ap().bitcast(BF16)[0:1, 0:1024]
                brw = RSTG.ap().bitcast(BF16)[0:1, 1024:2048]
                for b in range(2):
                    bs = slice(b * 512, (b + 1) * 512)
                    t.matmul(PS[4 + b].ap()[:, 0:512], CONC.ap()[0:1, :],
                             arow[:, bs], start=True, stop=True)
                    t.matmul(PS[6 + b].ap()[:, 0:512], CONC.ap()[0:1, :],
                             brw[:, bs], start=True, stop=True)
                inc(t.drain(), "lnt", 1)

            @blk.vector
            def _(v):
                v.wait_ge(S("lnt"), val("lnt"))
                for b in range(2):
                    c0, c1 = b * 512, (b + 1) * 512
                    abc = PS[4 + b].ap()[:, 0:512].unsqueeze(1).broadcast_to([P, DC, 512])
                    bbc = PS[6 + b].ap()[:, 0:512].unsqueeze(1).broadcast_to([P, DC, 512])
                    v.tensor_mul(out=scr2[:, :, c0:c1], in0=srcx[:, :, c0:c1], in1=abc)
                    v.drain()
                    v.tensor_add(out=out_view[:, :, c0:c1], in0=scr2[:, :, c0:c1], in1=bbc)
                if gamma is not None:
                    v.drain()
                    for c in range(DC):
                        v.tensor_scalar(out=out_view[:, c, :], in0=out_view[:, c, :],
                                        scalar1=gamma[:, c:c + 1], scalar2=beta[:, c:c + 1],
                                        op0=ALU.mult, op1=ALU.add)
                inc(v.drain(), "lnv", 1)

    def load_w_fp32(w_hbm, of0, nofe, wb_off=0, scale_cols=None, kc_n=DC):
        per = 2048 // kc_n
        o = 0
        while o < nofe:
            cur = min(per, nofe - o)
            with nc.Block("ldw") as blk:
                @blk.sync
                def _(sy, o=o, cur=cur):
                    src = w_hbm[:, of0 + o:of0 + o + cur].rearrange(
                        "(kc p) of -> p kc of", p=P)
                    inc(sy.dma_start(out=view(WS, kc_n, cur), in_=src), "dm", 16)

                @blk.vector
                def _(v, o=o, cur=cur):
                    v.wait_ge(S("dm"), val("dm"))
                    dst = view(WB, kc_n, nofe, off=wb_off)
                    if scale_cols is None:
                        v.tensor_copy(out=dst[:, :, o:o + cur], in_=view(WS, kc_n, cur))
                    else:
                        for kc in range(kc_n):
                            v.tensor_scalar_mul(out=dst[:, kc, o:o + cur],
                                                in0=view(WS, kc_n, cur)[:, kc, :],
                                                scalar1=scale_cols[:, kc:kc + 1])
                    inc(v.drain(), "wv", 1)
            o += cur

    def n_units(nofe, kc_n):
        return _ceil(nofe, 2048 // kc_n)

    def pf_emit(sy, w_hbm, nofe, kc_n, war=None):
        unit_ofe = 2048 // kc_n
        nu = min(2, n_units(nofe, kc_n))
        for u in range(nu):
            if war is not None and war[u] is not None:
                sy.wait_ge(S("pwv"), war[u])
            o0, o1 = u * unit_ofe, min(nofe, (u + 1) * unit_ofe)
            src = w_hbm[:, o0:o1].rearrange("(kc p) of -> p kc of", p=P)
            inc(sy.dma_start(out=view(WS, 2, kc_n, unit_ofe)[:, u, :, 0:o1 - o0],
                             in_=src), f"dmp{u}", 16)
        return nu

    def fm_proj(src_bf, nt, nof, evac, evac_eng="v", bias_row=None, kc_n=DC,
                wbv=None, of_base=0):
        wbv_ = wbv if wbv is not None else view(WB, kc_n, nof * 128)
        tiles = [(ofc, tb) for ofc in range(nof) for tb in range(_ceil(nt, 512))]
        mm_b = val("pmm"); ev_b = val("pev")
        with nc.Block("proj") as blk:
            @blk.tensor
            def _(t):
                t.wait_ge(S("wv"), val("wv"))
                t.wait_ge(S("dm"), val("dm"))
                for i, (ofc, tb) in enumerate(tiles):
                    bank = i % 2
                    ts, te = tb * 512, min(nt, (tb + 1) * 512)
                    if i >= 2:
                        t.wait_ge(S("pev"), ev_b + i - 1)
                    last = None
                    for kc in range(kc_n):
                        last = t.matmul(PS[bank].ap()[:, 0:te - ts],
                                        wbv_[:, kc, ofc * 128:(ofc + 1) * 128],
                                        src_bf[:, kc, ts:te], start=(kc == 0),
                                        stop=(kc == kc_n - 1 and bias_row is None))
                    if bias_row is not None:
                        last = t.matmul(PS[bank].ap()[:, 0:te - ts],
                                        bias_row[0:1, (of_base + ofc) * 128:(of_base + ofc + 1) * 128],
                                        CONR.ap()[0:1, 0:te - ts], start=False, stop=True)
                    inc(last, "pmm", 1)

            def emit_evac(eng):
                for i, (ofc, tb) in enumerate(tiles):
                    bank = i % 2
                    ts, te = tb * 512, min(nt, (tb + 1) * 512)
                    eng.wait_ge(S("pmm"), mm_b + i + 1)
                    ei = evac(eng, PS[bank].ap()[:, 0:te - ts], ofc, ts, te)
                    inc(ei, "pev", 1)

            if evac_eng == "v":
                @blk.vector
                def _(v):
                    emit_evac(v)
            else:
                @blk.scalar
                def _(s):
                    emit_evac(s)

    def proj_fp32(w_hbm, nofe, src_bf, nt, evac, evac_eng="v", bias_row=None,
                  kc_n=DC, of_base=0, scale_cols=None, pre=0, pf_next=None):
        unit_ofe = 2048 // kc_n
        units = []
        o = 0
        while o < nofe:
            units.append((o, min(nofe, o + unit_ofe)))
            o = units[-1][1]
        ntb = _ceil(nt, 512)
        tcounts = []
        tot = 0
        for (o0, o1) in units:
            tot += ((o1 - o0) // 128) * ntb
            tcounts.append(tot)
        b_p = [val("dmp0") - (16 if pre >= 1 else 0),
               val("dmp1") - (16 if pre >= 2 else 0)]
        b_wv = val("pwv"); b_mm = val("pmm"); b_ev = val("pev")
        b_dmt = val("dm")
        with nc.Block("projp") as blk:
            @blk.sync
            def _(sy):
                for u, (o0, o1) in enumerate(units):
                    if u < pre:
                        continue
                    if u >= 2:
                        sy.wait_ge(S("pwv"), b_wv + u - 1)
                    src = w_hbm[:, o0:o1].rearrange("(kc p) of -> p kc of", p=P)
                    inc(sy.dma_start(out=view(WS, 2, kc_n, unit_ofe)[:, u % 2, :, 0:o1 - o0],
                                     in_=src), f"dmp{u % 2}", 16)
                if pf_next is not None:
                    U = len(units)
                    war = [b_wv + ((U - 1) // 2) * 2 + 1, None]
                    if U >= 2:
                        war[1] = b_wv + ((U - 2) // 2) * 2 + 2
                    pf_emit(sy, *pf_next, war=war)

            def unit_tiles(u):
                o0, o1 = units[u]
                base = tcounts[u - 1] if u > 0 else 0
                out = []
                i = base
                for ofr in range((o1 - o0) // 128):
                    for tb in range(ntb):
                        out.append((i, o0 // 128 + ofr, tb))
                        i += 1
                return out

            def emit_ev_unit(eng, u):
                for i, ofc, tb in unit_tiles(u):
                    bank = i % 8
                    ts, te = tb * 512, min(nt, (tb + 1) * 512)
                    eng.wait_ge(S("pmm"), b_mm + i + 1)
                    ei = evac(eng, PS[bank].ap()[:, 0:te - ts], ofc, ts, te)
                    inc(ei, "pev", 1)

            @blk.vector
            def _(v):
                for step in range(len(units) + 1):
                    if step < len(units):
                        u = step
                        o0, o1 = units[u]
                        v.wait_ge(S(f"dmp{u % 2}"), b_p[u % 2] + 16 * (u // 2 + 1))
                        if u >= 2:
                            v.wait_ge(S("pmm"), b_mm + tcounts[u - 2])
                        dstw = view(WB, 2, kc_n, unit_ofe)[:, u % 2]
                        srcw = view(WS, 2, kc_n, unit_ofe)[:, u % 2]
                        if scale_cols is None:
                            v.tensor_copy(out=dstw[:, :, 0:o1 - o0], in_=srcw[:, :, 0:o1 - o0])
                        else:
                            for kc in range(kc_n):
                                v.tensor_scalar_mul(out=dstw[:, kc, 0:o1 - o0],
                                                    in0=srcw[:, kc, 0:o1 - o0],
                                                    scalar1=scale_cols[:, kc:kc + 1])
                        inc(v.drain(), "pwv", 1)
                    if evac_eng == "v" and step >= 1:
                        emit_ev_unit(v, step - 1)

            @blk.tensor
            def _(t):
                t.wait_ge(S("dm"), b_dmt)
                i = 0
                for u, (o0, o1) in enumerate(units):
                    t.wait_ge(S("pwv"), b_wv + u + 1)
                    for ofr in range((o1 - o0) // 128):
                        ofc = o0 // 128 + ofr
                        for tb in range(ntb):
                            bank = i % 8
                            ts, te = tb * 512, min(nt, (tb + 1) * 512)
                            if i >= 8:
                                t.wait_ge(S("pev"), b_ev + i - 7)
                            last = None
                            for kc in range(kc_n):
                                last = t.matmul(PS[bank].ap()[:, 0:te - ts],
                                                view(WB, 2, kc_n, unit_ofe)[:, u % 2, kc, ofr * 128:(ofr + 1) * 128],
                                                src_bf[:, kc, ts:te], start=(kc == 0),
                                                stop=(kc == kc_n - 1 and bias_row is None))
                            if bias_row is not None:
                                last = t.matmul(PS[bank].ap()[:, 0:te - ts],
                                                bias_row[0:1, (of_base + ofc) * 128:(of_base + ofc + 1) * 128],
                                                CONR.ap()[0:1, 0:te - ts], start=False, stop=True)
                            inc(last, "pmm", 1)
                            i += 1

            if evac_eng == "s":
                @blk.scalar
                def _(se):
                    for u in range(len(units)):
                        emit_ev_unit(se, u)

    def tm_proj(src_bf, vdst, bias_name, w_hbm, pre=0, pf_next=None):
        ensure_row("rowA", bias_name)
        b_mm = val("pmm"); b_ev = val("pev"); b_wv = val("pwv")
        b_p = [val("dmp0") - (16 if pre >= 1 else 0),
               val("dmp1") - (16 if pre >= 2 else 0)]
        with nc.Block("tmproj") as blk:
            @blk.sync
            def _(sy):
                for u in range(4):
                    if u < pre:
                        continue
                    if u >= 2:
                        sy.wait_ge(S("pwv"), b_wv + u - 1)
                    src = w_hbm[:, u * 256:(u + 1) * 256].rearrange(
                        "(kc p) of -> p kc of", p=P)
                    inc(sy.dma_start(out=view(WS, 2, DC, 256)[:, u % 2],
                                     in_=src), f"dmp{u % 2}", 16)
                if pf_next is not None:
                    pf_emit(sy, *pf_next, war=[b_wv + 3, b_wv + 4])

            @blk.vector
            def _(v):
                for u in range(4):
                    v.wait_ge(S(f"dmp{u % 2}"), b_p[u % 2] + 16 * (u // 2 + 1))
                    v.tensor_copy(out=view(WB, DC, 1024)[:, :, u * 256:(u + 1) * 256],
                                  in_=view(WS, 2, DC, 256)[:, u % 2])
                    inc(v.drain(), "pwv", 1)
                i = 0
                for ofb in range(2):
                    for tt in range(8):
                        bank = i % 4
                        v.wait_ge(S("pmm"), b_mm + i + 1)
                        ei = None
                        for hh in range(8):
                            h = ofb * 8 + hh
                            ei = v.tensor_copy(out=vdst[:, tt, h, 0:64],
                                               in_=PS[bank].ap()[:, hh * 64:(hh + 1) * 64])
                        inc(ei, "pev", 1)
                        i += 1

            @blk.tensor
            def _(t):
                t.wait_ge(S("gv"), val("gv"))
                t.wait_ge(S("dm"), val("dm"))
                i = 0
                for ofb in range(2):
                    t.wait_ge(S("pwv"), b_wv + 2 * (ofb + 1))
                    for tt in range(8):
                        bank = i % 4
                        if i >= 4:
                            t.wait_ge(S("pev"), b_ev + i - 3)
                        for kc in range(DC):
                            t.matmul(PS[bank].ap()[:, 0:512],
                                     src_bf[:, kc, tt * 128:(tt + 1) * 128],
                                     view(WB, DC, 1024)[:, kc, ofb * 512:(ofb + 1) * 512],
                                     start=(kc == 0), stop=False)
                        last = t.matmul(PS[bank].ap()[:, 0:512], CONC.ap()[0:1, :],
                                        brow("rowA")[0:1, ofb * 512:(ofb + 1) * 512],
                                        start=False, stop=True)
                        inc(last, "pmm", 1)
                        i += 1

    def l0_attn():
        vq = view(B2, DC, TH)
        vE = view(B3, DC, TH)
        vvt = view(B4, DC, TH)
        vgs = view(B5, DC, TH)
        vP = view(B1, DC, TH)
        for half in range(2):
            t0 = half * TH
            xsl = XRES.ap()[:, :, t0:t0 + TH]
            ln_1024(xsl, None, vP, view(B2, DC, TH), view(B3, DC, TH),
                    pf=(wq, 1024, DC))
            dump("ln0", B1)

            def ev_q(e, ps, ofc, ts, te):
                return e.activation(out=vq[:, ofc, ts:te], in_=ps, func=AF.Silu)
            proj_fp32(wq, 1024, vP, TH, ev_q, evac_eng="s", pre=2,
                      pf_next=(wg1, 128, DC))
            dump("q", B2)

            def ev_g1(e, ps, ofc, ts, te):
                return e.tensor_copy(out=SB2.ap()[:, ts:te], in_=ps)
            proj_fp32(wg1, 128, vP, TH, ev_g1, evac_eng="v", pre=1,
                      pf_next=(wg2, 1024, 1))

            def ev_g2(e, ps, ofc, ts, te):
                return e.activation(out=vgs[:, ofc, ts:te], in_=ps, func=AF.Silu)
            g1v = SB2.ap()[:, 0:TH].rearrange("p (a t) -> p a t", a=1)
            proj_fp32(wg2, 1024, g1v, TH, ev_g2, evac_eng="s", kc_n=1, pre=1,
                      pf_next=(wk, 1024, DC))

            def ev_k(e, ps, ofc, ts, te):
                return e.activation(out=vE[:, ofc, ts:te], in_=ps, func=AF.Exp)
            proj_fp32(wk, 1024, vP, TH, ev_k, evac_eng="s", pre=2,
                      pf_next=(wv, 1024, DC))
            dump("E", B3)

            b_mm = val("pmm"); b_ev = val("pev"); b_wv = val("pwv")
            b_p = [val("dmp0") - 16, val("dmp1") - 16]
            with nc.Block("gla_v") as blk:
                @blk.sync
                def _(sy):
                    for u in range(2, 4):
                        sy.wait_ge(S("pwv"), b_wv + u - 1)
                        src = wv[:, u * 256:(u + 1) * 256].rearrange(
                            "(kc p) of -> p kc of", p=P)
                        inc(sy.dma_start(out=view(WS, 2, DC, 256)[:, u % 2],
                                         in_=src), f"dmp{u % 2}", 16)
                    pf_emit(sy, wo, 1024, DC, war=[b_wv + 3, b_wv + 4])

                @blk.vector
                def _(v):
                    for u in range(4):
                        v.wait_ge(S(f"dmp{u % 2}"), b_p[u % 2] + 16 * (u // 2 + 1))
                        v.tensor_copy(out=view(WB, DC, 1024)[:, :, u * 256:(u + 1) * 256],
                                      in_=view(WS, 2, DC, 256)[:, u % 2])
                        inc(v.drain(), "pwv", 1)
                    i = 0
                    for ofb in range(2):
                        for tt in range(8):
                            bank = i % 2
                            v.wait_ge(S("pmm"), b_mm + i + 1)
                            ei = v.tensor_copy(out=vvt[:, tt, ofb * 512:(ofb + 1) * 512],
                                               in_=PS[bank].ap()[:, 0:512])
                            inc(ei, "pev", 1)
                            i += 1

                @blk.tensor
                def _(t):
                    i = 0
                    for ofb in range(2):
                        t.wait_ge(S("pwv"), b_wv + 2 * (ofb + 1))
                        for tt in range(8):
                            bank = i % 2
                            if i >= 2:
                                t.wait_ge(S("pev"), b_ev + i - 1)
                            last = None
                            for kc in range(DC):
                                last = t.matmul(PS[bank].ap()[:, 0:512],
                                                vP[:, kc, tt * 128:(tt + 1) * 128],
                                                view(WB, DC, 1024)[:, kc, ofb * 512:(ofb + 1) * 512],
                                                start=(kc == 0), stop=(kc == DC - 1))
                            inc(last, "pmm", 1)
                            i += 1

            dump("v", B4)
            dump("gsw", B5)

            with nc.Block("gla_fac") as blk:
                @blk.vector
                def _(v, half=half):
                    v.wait_ge(S("pev"), val("pev"))
                    for h in range(DC):
                        init = 0.0 if half == 0 else sm("Pcar")[:, h:h + 1]
                        v.tensor_tensor_scan(out=vP[:, h, :], data0=vE[:, h, :],
                                             data1=vE[:, h, :], initial=init,
                                             op0=ALU.add, op1=ALU.bypass)
                    v.drain()
                    pbp4 = sm("Pbp", 64).rearrange("p (h n) -> p h n", h=8).unsqueeze(3)
                    pls4 = sm("Pls", 64).rearrange("p (h n) -> p h n", h=8).unsqueeze(3)
                    vP4c = view(B1, DC, 8, 128)
                    v.tensor_copy(out=pls4, in_=vP4c[:, :, :, 127:128])
                    v.tensor_copy(out=pbp4[:, :, 1:8, :], in_=vP4c[:, :, 0:7, 127:128])
                    if half == 0:
                        v.tensor_copy(out=pbp4[:, :, 0:1, :], in_=vP4c[:, :, 0:1, 0:1])
                    else:
                        v.tensor_copy(out=pbp4[:, :, 0:1, :],
                                      in_=sm("Pcar").unsqueeze(2).unsqueeze(3))
                    v.drain()
                    v.tensor_copy(out=sm("Pcar").unsqueeze(2).unsqueeze(3),
                                  in_=vP4c[:, :, 7:8, 127:128])
                    v.drain()
                    v.reciprocal(out=sm("rPbp", 64), in_=sm("Pbp", 64))
                    v.reciprocal(out=sm("rPls", 64), in_=sm("Pls", 64))
                    v.drain()
                    v.tensor_mul(out=sm("ebl", 64), in0=sm("Pbp", 64), in1=sm("rPls", 64))
                    inc(v.drain(), "gv", 1)

                @blk.scalar
                def _(s):
                    s.wait_ge(S("gv"), val("gv"))
                    s.activation(out=vP, in_=vP, func=AF.Ln)
                    s.activation(out=sm("lnPbp", 64), in_=sm("Pbp", 64), func=AF.Ln)
                    inc(s.drain(), "gs", 1)

                @blk.vector
                def _(v):
                    v.wait_ge(S("gs"), val("gs"))
                    lnpbp = sm("lnPbp", 64).rearrange("p (h n) -> p h n", h=8)
                    rep = lnpbp.unsqueeze(3).broadcast_to([P, 8, 8, 128])
                    vP4 = view(B1, DC, 8, 128)
                    v.tensor_sub(out=vP4, in0=rep, in1=vP4)
                    inc(v.drain(), "gv", 1)

                @blk.scalar
                def _(s):
                    s.wait_ge(S("gv"), val("gv"))
                    s.activation(out=vP, in_=vP, func=AF.Exp)
                    inc(s.drain(), "gs", 1)

                @blk.vector
                def _(v, half=half):
                    v.wait_ge(S("gs"), val("gs"))
                    v.tensor_mul(out=vq, in0=vq, in1=vP)
                    rpb = sm("rPbp", 64).rearrange("p (h n) -> p h n", h=8)
                    rep = rpb.unsqueeze(3).broadcast_to([P, 8, 8, 128])
                    vE4 = view(B3, DC, 8, 128)
                    v.tensor_mul(out=vE4, in0=vE4, in1=rep)
                    if half == 0:
                        v.memset(SB2.ap()[:, 1024:2048], 0.0)
                    inc(v.drain(), "gv", 1)

            dump("fac", B1)
            dump("q2", B2)
            dump("k2", B3)

            etm = SB2.ap()[:, 0:1024].rearrange("p (h d) -> p h d", h=8)
            vq4o = view(B2, DC, 8, 128)
            sbf = SB2.ap()[:, 1024:2048].rearrange("p (h d) -> p h d", h=8)
            amask = view(WB, 8, 128)
            ebl = sm("ebl", 64)
            b_mm = val("g_mm"); b_tr = val("g_tr"); b_ds = val("g_ds")
            b_om = val("g_om"); b_ae = val("g_ae"); b_te = val("g_te")
            b_su = val("g_su"); b_oe = val("g_oe")
            with nc.Block("gla_chunks") as blk:
                @blk.tensor
                def _(t):
                    t.wait_ge(S("gv"), val("gv"))
                    for n in range(8):
                        nsl = slice(n * 128, (n + 1) * 128)
                        t.wait_ge(S("g_ae"), b_ae + 2 * n)
                        last = None
                        for h in range(8):
                            last = t.matmul(PS[h // 4].ap()[:, (h % 4) * 128:(h % 4 + 1) * 128],
                                            vE[:, h, nsl], vq[:, h, nsl],
                                            start=True, stop=True)
                        inc(last, "g_mm", 1)
                        t.wait_ge(S("g_te"), b_te + n)
                        for h in range(8):
                            last = t.transpose(psb(2)[:, h * 128:(h + 1) * 128],
                                               vE[:, h, nsl], CID.ap())
                        inc(last, "g_tr", 1)
                        t.wait_ge(S("g_su"), b_su + n)
                        t.wait_ge(S("g_te"), b_te + n + 1)
                        for h in range(8):
                            last = t.matmul(PS[3 + h // 4].ap()[:, (h % 4) * 128:(h % 4 + 1) * 128],
                                            etm[:, h, :], vvt[:, n, h * 128:(h + 1) * 128],
                                            start=True, stop=True)
                        inc(last, "g_ds", 1)
                        t.wait_ge(S("g_ae"), b_ae + 2 * (n + 1))
                        t.wait_ge(S("g_oe"), b_oe + n)
                        for h in range(8):
                            reg = PS[5 + h // 4].ap()[:, (h % 4) * 128:(h % 4 + 1) * 128]
                            t.matmul(reg, sbf[:, h, :], vq[:, h, nsl], start=True, stop=False)
                            last = t.matmul(reg, vvt[:, n, h * 128:(h + 1) * 128],
                                            amask[:, h, :], start=False, stop=True)
                        inc(last, "g_om", 1)

                @blk.vector
                def _(v):
                    for n in range(8):
                        v.wait_ge(S("g_mm"), b_mm + n + 1)
                        v.wait_ge(S("g_om"), b_om + n)
                        for g in range(2):
                            v.tensor_mul(
                                out=amask[:, 4 * g:4 * (g + 1), :].rearrange("p a b -> p (a b)"),
                                in0=PS[g].ap()[:, 0:512], in1=CGM.ap())
                        inc(v.drain(), "g_ae", 2)
                        v.wait_ge(S("g_ds"), b_ds + n + 1)
                        v.wait_ge(S("g_om"), b_om + n + 1)
                        for g in range(2):
                            v.tensor_add(out=SGLA.ap().rearrange("p h d -> p (h d)")[:, g * 512:(g + 1) * 512],
                                         in0=PS[3 + g].ap()[:, 0:512],
                                         in1=SGLA.ap().rearrange("p h d -> p (h d)")[:, g * 512:(g + 1) * 512])
                        v.drain()
                        for h in range(8):
                            gi = h * 8 + n
                            v.tensor_scalar_mul(out=SGLA.ap()[:, h, :], in0=SGLA.ap()[:, h, :],
                                                scalar1=ebl[:, gi:gi + 1])
                        v.drain()
                        v.tensor_copy(out=SB2.ap()[:, 1024:2048],
                                      in_=SGLA.ap().rearrange("p h d -> p (h d)"))
                        inc(v.drain(), "g_su", 1)

                @blk.scalar
                def _(s):
                    for n in range(8):
                        nsl = slice(n * 128, (n + 1) * 128)
                        s.wait_ge(S("g_tr"), b_tr + n + 1)
                        s.wait_ge(S("g_ds"), b_ds + n)
                        s.copy(out=SB2.ap()[:, 0:1024], in_=psb(2)[:, 0:1024])
                        inc(s.drain(), "g_te", 1)
                        s.wait_ge(S("g_om"), b_om + n + 1)
                        ei = s.activation(out=vq[:, :, nsl].transpose([0, 2, 1]) if False
                                          else vq4o[:, :, n, :],
                                          in_=psw(5, 2), func=AF.Copy)
                        inc(ei, "g_oe", 1)

            dump("o", B2)
            dump("sm", SM, f32=True)
            dump("S", SGLA, f32=True)
            dump("am", WB)
            dump("sb2", SB2)

            vosq = view(B3, DC, TH)
            with nc.Block("gla_rms") as blk:
                @blk.scalar
                def _(s):
                    s.wait_ge(S("g_oe"), val("g_oe"))
                    s.activation(out=vosq, in_=vq, func=AF.Square)
                    inc(s.drain(), "gs", 1)

                @blk.tensor
                def _(t):
                    t.wait_ge(S("gs"), val("gs"))
                    for b in range(2):
                        for c in range(DC):
                            t.matmul(PS[0 + b].ap()[0:1, 0:512], CSUM.ap(),
                                     vosq[:, c, b * 512:(b + 1) * 512],
                                     start=(c == 0), stop=(c == DC - 1))
                    inc(t.drain(), "gt", 1)

                @blk.scalar
                def _(s):
                    s.wait_ge(S("gt"), val("gt"))
                    rr = RSTG.ap()[0:1, 2048:3072]
                    for b in range(2):
                        s.activation(out=rr[:, b * 512:(b + 1) * 512],
                                     in_=PS[b].ap()[0:1, 0:512], func=AF.Ln,
                                     scale=1.0 / 1024.0, bias=LN_EPS)
                    s.drain()
                    s.activation(out=rr, in_=rr, func=AF.Exp, scale=-0.5)
                    inc(s.drain(), "gs", 1)

                @blk.vector
                def _(v):
                    v.wait_ge(S("gs"), val("gs"))
                    ab = RSTG.ap().bitcast(BF16)[0:1, 0:1024]
                    v.tensor_copy(out=ab, in_=RSTG.ap()[0:1, 2048:3072])
                    inc(v.drain(), "gv", 1)

                @blk.tensor
                def _(t):
                    t.wait_ge(S("gv"), val("gv"))
                    ab = RSTG.ap().bitcast(BF16)[0:1, 0:1024]
                    for b in range(2):
                        t.matmul(PS[2 + b].ap()[:, 0:512], CONC.ap()[0:1, :],
                                 ab[:, b * 512:(b + 1) * 512], start=True, stop=True)
                    inc(t.drain(), "gt", 1)

                @blk.vector
                def _(v):
                    v.wait_ge(S("gt"), val("gt"))
                    for b in range(2):
                        c0, c1 = b * 512, (b + 1) * 512
                        rbc = PS[2 + b].ap()[:, 0:512].unsqueeze(1).broadcast_to([P, DC, 512])
                        v.tensor_mul(out=vosq[:, :, c0:c1], in0=vq[:, :, c0:c1], in1=rbc)
                    v.drain()
                    v.tensor_mul(out=vosq, in0=vosq, in1=vgs)
                    inc(v.drain(), "gv", 1)

            def ev_wo(e, ps, ofc, ts, te):
                return e.tensor_add(out=XRES.ap()[:, ofc, t0 + ts:t0 + te],
                                    in0=ps, in1=XRES.ap()[:, ofc, t0 + ts:t0 + te])
            proj_fp32(wo, 1024, vosq, TH, ev_wo, evac_eng="v", scale_cols=sm("gn"),
                      pre=2)

    def l0_ffn():
        vln = view(B1, DC, TH)

        def h_ap(hc):
            buf = [B2, B3, B4][hc // 8]
            return view(buf, 8, TH)[:, hc % 8, :]

        for half in range(2):
            t0 = half * TH
            xsl = XRES.ap()[:, :, t0:t0 + TH]
            def pf_gu(sy):
                for wi, wt in enumerate([w0g, w0u]):
                    src = wt[:, 0:128].rearrange("(kc p) of -> p kc of", p=P)
                    inc(sy.dma_start(
                        out=view(WS, 2, 2, DC, 128)[:, 0, wi].bitcast(I32),
                        in_=src), "dmp0", 16)
            ln_1024(xsl, None, vln, view(B2, DC, TH), view(B3, DC, TH), pf=pf_gu)

            b_wv = val("fwv"); b_mm = val("fmm")
            b_sl = val("fsl"); b_ev = val("fev")
            b_p = [val("dmp0") - 32, val("dmp1")]
            with nc.Block("l0f_gu") as blk:
                @blk.sync
                def _(sy):
                    for hc in range(1, 24):
                        if hc >= 2:
                            sy.wait_ge(S("fmm"), b_mm + 4 * (hc - 1))
                        for wi, wt in enumerate([w0g, w0u]):
                            src = wt[:, hc * 128:(hc + 1) * 128].rearrange(
                                "(kc p) of -> p kc of", p=P)
                            inc(sy.dma_start(
                                out=view(WS, 2, 2, DC, 128)[:, hc % 2, wi].bitcast(I32),
                                in_=src), f"dmp{hc % 2}", 16)
                    for kc2 in range(2):
                        sy.wait_ge(S("fwv"), b_wv + 23 + kc2)
                        src = w0d[kc2 * 128:(kc2 + 1) * 128, 0:512]
                        inc(sy.dma_start(out=view(WS, 2, 512)[:, kc2, :].bitcast(I32),
                                         in_=src), f"dmp{kc2}", 16)

                @blk.vector
                def _(v):
                    for i in range(25):
                        if i < 24:
                            hc = i
                            v.wait_ge(S(f"dmp{hc % 2}"), b_p[hc % 2] + 32 * (hc // 2 + 1))
                            if hc >= 2:
                                v.wait_ge(S("fmm"), b_mm + 4 * (hc - 1))
                            for wi in range(2):
                                v.tensor_copy(out=view(WB, 2, 2, DC, 128)[:, hc % 2, wi],
                                              in_=view(WS, 2, 2, DC, 128)[:, hc % 2, wi].bitcast(I32))
                            inc(v.drain(), "fwv", 1)
                        if i >= 1:
                            hc = i - 1
                            for tb in range(2):
                                v.wait_ge(S("fsl"), b_sl + 2 * hc + tb + 1)
                                v.wait_ge(S("fmm"), b_mm + 4 * hc + 2 + tb + 1)
                                ei = v.tensor_mul(out=h_ap(hc)[:, tb * 512:(tb + 1) * 512],
                                                  in0=PS[4 * (hc % 2) + 2 + tb].ap()[:, 0:512],
                                                  in1=SB2.ap()[:, (2 * (hc % 2) + tb) * 512:(2 * (hc % 2) + tb + 1) * 512])
                                inc(ei, "fev", 1)

                @blk.tensor
                def _(t):
                    for hc in range(24):
                        t.wait_ge(S("fwv"), b_wv + hc + 1)
                        if hc >= 2:
                            t.wait_ge(S("fsl"), b_sl + 2 * (hc - 1))
                            t.wait_ge(S("fev"), b_ev + 2 * (hc - 1))
                        last = None
                        for wi in range(2):
                            for tb in range(2):
                                bank = 4 * (hc % 2) + 2 * wi + tb
                                for kc in range(DC):
                                    last = t.matmul(PS[bank].ap()[:, 0:512],
                                                    view(WB, 2, 2, DC, 128)[:, hc % 2, wi, kc, :],
                                                    vln[:, kc, tb * 512:(tb + 1) * 512],
                                                    start=(kc == 0), stop=(kc == DC - 1))
                                inc(last, "fmm", 1)

                @blk.scalar
                def _(s):
                    for hc in range(24):
                        for tb in range(2):
                            s.wait_ge(S("fmm"), b_mm + 4 * hc + tb + 1)
                            s.activation(out=SB2.ap()[:, (2 * (hc % 2) + tb) * 512:(2 * (hc % 2) + tb + 1) * 512],
                                         in_=PS[4 * (hc % 2) + tb].ap()[:, 0:512], func=AF.Silu,
                                         scale=QS)
                            inc(s.drain(), "fsl", 1)


            for ofh in range(2):
                def wd_ap(kc):
                    if kc < 16:
                        return view(B1, 16, 512)[:, kc, :]
                    return view(WB, 8, 512)[:, kc - 16, :]
                b_wv = val("fwv"); b_mm = val("fmm")
                b_p = [val("dmp0") - 16, val("dmp1") - 16]
                with nc.Block("l0f_dn") as blk:
                    @blk.sync
                    def _(sy, ofh=ofh):
                        for kc in range(2, 24):
                            sy.wait_ge(S("fwv"), b_wv + kc - 1)
                            src = w0d[kc * 128:(kc + 1) * 128, ofh * 512:(ofh + 1) * 512]
                            inc(sy.dma_start(out=view(WS, 2, 512)[:, kc % 2, :].bitcast(I32),
                                             in_=src), f"dmp{kc % 2}", 16)
                        if ofh == 0:
                            for kc2 in range(2):
                                sy.wait_ge(S("fwv"), b_wv + 23 + kc2)
                                src = w0d[kc2 * 128:(kc2 + 1) * 128, 512:1024]
                                inc(sy.dma_start(out=view(WS, 2, 512)[:, kc2, :].bitcast(I32),
                                                 in_=src), f"dmp{kc2}", 16)

                    @blk.vector
                    def _(v):
                        for kc in range(24):
                            v.wait_ge(S(f"dmp{kc % 2}"), b_p[kc % 2] + 16 * (kc // 2 + 1))
                            v.tensor_copy(out=wd_ap(kc),
                                          in_=view(WS, 2, 512)[:, kc % 2, :].bitcast(I32))
                            inc(v.drain(), "fwv", 1)
                        for tb in range(2):
                            v.wait_ge(S("fmm"), b_mm + tb + 1)
                            ei = None
                            for j in range(4):
                                ofc = ofh * 4 + j
                                dst = XRES.ap()[:, ofc, t0 + tb * 512:t0 + (tb + 1) * 512]
                                ei = v.scalar_tensor_tensor(out=dst,
                                                            in0=PS[4 * tb + j].ap()[:, 0:512],
                                                            scalar=QS * QS, in1=dst,
                                                            op0=ALU.mult, op1=ALU.add)
                            inc(ei, "fev", 1)

                    @blk.tensor
                    def _(t):
                        for tb in range(2):
                            last = None
                            for kc in range(24):
                                if tb == 0:
                                    t.wait_ge(S("fwv"), b_wv + kc + 1)
                                for j in range(4):
                                    last = t.matmul(PS[4 * tb + j].ap()[:, 0:512],
                                                    wd_ap(kc)[:, j * 128:(j + 1) * 128],
                                                    h_ap(kc)[:, tb * 512:(tb + 1) * 512],
                                                    start=(kc == 0), stop=(kc == 23))
                            inc(last, "fmm", 1)

    def l1_attn():
        with nc.Block("l1_fw") as blk:
            @blk.sync
            def _(sy):
                inc(sy.dma_start(out=RSTG.ap()[0:1, 0:2], in_=fusw), "dm", 16)

            @blk.scalar
            def _(s):
                s.wait_ge(S("dm"), val("dm"))
                s.activation(out=RSTG.ap()[0:1, 1024:1026], in_=RSTG.ap()[0:1, 0:2], func=AF.Exp)
                inc(s.drain(), "gs", 1)

            @blk.vector
            def _(v):
                v.wait_ge(S("gs"), val("gs"))
                v.reduce_sum(out=RSTG.ap()[0:1, 1026:1027], in_=RSTG.ap()[0:1, 1024:1026], axis=AX.X)
                v.drain()
                v.reciprocal(out=RSTG.ap()[0:1, 1027:1028], in_=RSTG.ap()[0:1, 1026:1027])
                v.drain()
                v.tensor_scalar_mul(out=RSTG.ap().bitcast(BF16)[0:1, 0:2],
                                    in0=RSTG.ap()[0:1, 1024:1026],
                                    scalar1=RSTG.ap()[0:1, 1027:1028])
                inc(v.drain(), "gv", 1)

            @blk.tensor
            def _(t):
                t.wait_ge(S("gv"), val("gv"))
                wb2 = RSTG.ap().bitcast(BF16)[0:1, 0:2]
                t.matmul(PS[0].ap()[:, 0:1], CONC.ap()[0:1, :], wb2[:, 0:1],
                         start=True, stop=True)
                t.matmul(PS[0].ap()[:, 1:2], CONC.ap()[0:1, :], wb2[:, 1:2],
                         start=True, stop=True)
                inc(t.drain(), "gt", 1)

            @blk.vector
            def _(v):
                v.wait_ge(S("gt"), val("gt"))
                v.tensor_copy(out=sm("w0", 1), in_=PS[0].ap()[:, 0:1])
                v.tensor_copy(out=sm("w1", 1), in_=PS[0].ap()[:, 1:2])
                inc(v.drain(), "gv", 1)

        for w in range(2):
            t0 = w * TH
            xsl = XRES.ap()[:, :, t0:t0 + TH]
            vln2 = view(B1, DC, TH)
            ln_1024(xsl, None, vln2, view(B3, DC, TH), view(B4, DC, TH),
                    pf=(pl, 1024, DC))
            dump("l1ln", B1)

            def ev_cp(dst):
                def f(e, ps, ofc, ts, te):
                    return e.tensor_copy(out=dst[:, ofc, ts:te], in_=ps)
                return f

            vlqs = view(B2, DC, TH)
            proj_fp32(pl, 1024, vln2, TH, ev_cp(vlqs), evac_eng="v", pre=2,
                      pf_next=(wlin[:, 0:1024], 1024, DC))
            dump("lqs", B2)

            vlq = view(B3, DC, TH); vlk = view(B4, DC, TH)
            vlv = view(B5, 8, 16, 65)
            def ev_q8(e, ps, ofc, ts, te):
                return e.activation(out=vlq[:, ofc, ts:te], in_=ps, func=AF.Identity,
                                    scale=0.125, bias=sm("blq8")[:, ofc:ofc + 1])
            proj_fp32(wlin[:, 0:1024], 1024, vlqs, TH, ev_q8, evac_eng="s", pre=2,
                      pf_next=(wlin[:, 1024:2048], 1024, DC))
            def ev_kb(e, ps, ofc, ts, te):
                return e.activation(out=vlk[:, ofc, ts:te], in_=ps, func=AF.Identity,
                                    bias=sm("blk")[:, ofc:ofc + 1])
            proj_fp32(wlin[:, 1024:2048], 1024, vlqs, TH, ev_kb, evac_eng="s", pre=2,
                      pf_next=(wlin[:, 2048:3072], 1024, DC))
            with nc.Block("l1_lvo") as blk:
                @blk.vector
                def _(v):
                    v.memset(vlv[:, :, :, 64:65], 1.0)
                    inc(v.drain(), "gv", 1)
            tm_proj(vlqs, vlv, blv_row, wlin[:, 2048:3072], pre=2,
                    pf_next=(wlout, 1024, DC))
            dump("lq", B3)
            dump("lk", B4)
            dump("lv", B5)

            vlo = view(B2, DC, TH)
            b_sc = val("l_sc"); b_se = val("l_se"); b_av = val("l_av")
            b_nr = val("l_nr"); b_rb = val("l_rb"); b_no = val("l_no")
            b_dm = val("dm"); b_st = val("l_st")
            with nc.Block("l1_local") as blk:
                @blk.tensor
                def _(t):
                    t.wait_ge(S("pev"), val("pev"))
                    for lw in range(8):
                        lsl = slice(lw * 128, (lw + 1) * 128)
                        t.wait_ge(S("l_no"), b_no + lw * 4)
                        last = None
                        for h in range(16):
                            r0 = (h % 2) * 64
                            reg = PS[h // 4].ap()[:, (h % 4) * 128:(h % 4 + 1) * 128]
                            t.matmul(reg, vlk[r0:r0 + 64, h // 2, lsl],
                                     vlq[r0:r0 + 64, h // 2, lsl], start=True, stop=False)
                            last = t.matmul(reg, CID.ap(), CL1.ap(), start=False, stop=True)
                            if h % 4 == 3:
                                inc(last, "l_sc", 1)
                        t.wait_ge(S("l_se"), b_se + lw * 4 + 4)
                        for h in range(16):
                            reg = PS[4 + h // 4].ap()[:, (h % 4) * 128:(h % 4 + 1) * 128]
                            last = t.matmul(reg[0:65, :], vlv[:, lw, h, :],
                                            SB2.ap()[:, h * 128:(h + 1) * 128],
                                            start=True, stop=True)
                            if h % 4 == 3:
                                inc(last, "l_av", 1)
                        t.wait_ge(S("l_nr"), b_nr + lw + 1)
                        last = None
                        for g in range(4):
                            last = t.matmul(PS[g].ap()[0:64, 0:512], CONB.ap()[64:65, 0:64],
                                            SB2.ap()[64:65, g * 512:(g + 1) * 512],
                                            start=True, stop=True)
                        inc(last, "l_rb", 1)

                @blk.scalar
                def _(s):
                    for lw in range(8):
                        s.wait_ge(S("dm"), b_dm + 16 * lw)
                        s.wait_ge(S("l_sc"), b_sc + lw * 4 + 2)
                        s.activation(out=SB2.ap()[:, 0:1024], in_=psw(0, 2), func=AF.Exp)
                        s.wait_ge(S("l_sc"), b_sc + lw * 4 + 4)
                        s.activation(out=SB2.ap()[:, 1024:2048], in_=psw(2, 2), func=AF.Exp)
                        inc(s.drain(), "l_se", 4)
                        s.wait_ge(S("l_av"), b_av + lw * 4 + 4)
                        for g in range(4):
                            s.activation(out=SB2.ap()[64:65, g * 512:(g + 1) * 512],
                                         in_=PS[4 + g].ap()[64:65, 0:512], func=AF.Ln)
                        s.drain()
                        with nc.allow_low_precision(reason="softmax recip row bf16"):
                            s.activation(out=SB2.ap()[64:65, 0:2048],
                                         in_=SB2.ap()[64:65, 0:2048], func=AF.Exp,
                                         scale=-1.0)
                        inc(s.drain(), "l_nr", 1)

                @blk.vector
                def _(v):
                    wrbc = view(WB, 2048)
                    for lw in range(8):
                        v.wait_ge(S("l_rb"), b_rb + lw + 1)
                        v.wait_ge(S("dm"), b_dm + 16 * lw)
                        for g in range(4):
                            v.tensor_copy(out=wrbc[0:64, g * 512:(g + 1) * 512],
                                          in_=PS[g].ap()[0:64, 0:512])
                        v.drain()
                        ei = None
                        for h in range(16):
                            g = h // 4
                            src = PS[4 + g].ap()[0:64, (h % 4) * 128:(h % 4 + 1) * 128]
                            rb = wrbc[0:64, h * 128:(h + 1) * 128]
                            if h % 2 == 0:
                                dst = vlo[0:64, h // 2, lw * 128:(lw + 1) * 128]
                            else:
                                dst = SB2.ap()[0:64, (h // 2) * 128:(h // 2 + 1) * 128]
                            ei = v.tensor_mul(out=dst, in0=src, in1=rb)
                        inc(ei, "l_no", 4)
                        inc(v.drain(), "l_st", 1)

                @blk.sync
                def _(sy):
                    vlo4 = view(B2, DC, 8, 128)
                    stg = SB2.ap()[0:64, 0:1024].rearrange("p (c t) -> p c t", c=8)
                    for lw in range(8):
                        sy.wait_ge(S("l_st"), b_st + lw + 1)
                        inc(sy.dma_start(out=vlo4[64:128, :, lw, :], in_=stg), "dm", 16)

            dump("lo", B2)
            vfus = view(B3, DC, TH)
            ensure_row("rowB", blout_row)

            def ev_lo(e, ps, ofc, ts, te):
                return e.activation(out=vfus[:, ofc, ts:te], in_=ps, func=AF.Identity,
                                    scale=sm("w0", 1))
            proj_fp32(wlout, 1024, vlo, TH, ev_lo, evac_eng="s", bias_row=brow("rowB"),
                      pre=2, pf_next=(pg, 1024, DC))

            dump("fusl", B3)
            vgqs = view(B4, DC, TH)
            proj_fp32(pg, 1024, vln2, TH, ev_cp(vgqs), evac_eng="v", pre=2,
                      pf_next=(wgin[:, 0:1024], 1024, DC))

            vgq = view(B1, DC, TH); vgk = view(B2, DC, TH)
            vgv = view(B5, 8, 16, 65)
            def ev_gq(e, ps, ofc, ts, te):
                return e.activation(out=vgq[:, ofc, ts:te], in_=ps, func=AF.Identity,
                                    scale=0.125, bias=sm("bgq8")[:, ofc:ofc + 1])
            proj_fp32(wgin[:, 0:1024], 1024, vgqs, TH, ev_gq, evac_eng="s", pre=2,
                      pf_next=(wgin[:, 1024:2048], 1024, DC))
            def ev_gk(e, ps, ofc, ts, te):
                return e.activation(out=vgk[:, ofc, ts:te], in_=ps, func=AF.Identity,
                                    bias=sm("bgk")[:, ofc:ofc + 1])
            proj_fp32(wgin[:, 1024:2048], 1024, vgqs, TH, ev_gk, evac_eng="s", pre=2,
                      pf_next=(wgin[:, 2048:3072], 1024, DC))
            with nc.Block("l1_gvo") as blk:
                @blk.vector
                def _(v):
                    v.memset(vgv[:, :, :, 64:65], 1.0)
                    inc(v.drain(), "gv", 1)
            tm_proj(vgqs, vgv, bgv_row, wgin[:, 2048:3072], pre=2,
                    pf_next=(wgout, 1024, DC))

            vet = view(B4, 8, TH)
            b_gsc = val("ga_sc"); b_gse = val("ga_se"); b_gav = val("ga_av")
            b_gnr = val("ga_nr"); b_grb = val("ga_rb"); b_gno = val("ga_no")
            b_gdm = val("dm"); b_gst = val("ga_st")
            with nc.Block("l1_global") as blk:
                @blk.tensor
                def _(t):
                    t.wait_ge(S("pev"), val("pev"))
                    for h in range(16):
                        c = h // 2
                        r0 = (h % 2) * 64

                        def emit_av(jt):
                            last = None
                            for ih in range(2):
                                last = t.matmul(PS[2 + ih].ap()[0:65, 0:512],
                                                vgv[:, jt, h, :],
                                                vet[:, jt, ih * 512:(ih + 1) * 512],
                                                start=(jt == 0), stop=(jt == 7))
                            return last

                        t.wait_ge(S("ga_no"), b_gno + h)
                        for jt in range(8):
                            gidx = h * 8 + jt
                            bank = (jt % 2) * 6
                            if gidx >= 2:
                                t.wait_ge(S("ga_se"), b_gse + (gidx - 1) * 2)
                            last = None
                            for ih in range(2):
                                reg = PS[bank + ih].ap()[:, 0:512]
                                ops = [("sc", None)]
                                dlo = jt * 128
                                s0, s1 = ih * 512, (ih + 1) * 512
                                if dlo > s0:
                                    ops.append(("full", min(dlo, s1) - s0))
                                if dlo < s1 and dlo + 128 > s0:
                                    ops.append(("diag", dlo - s0))
                                nops = len(ops)
                                for oi, (kind, arg) in enumerate(ops):
                                    st = (oi == 0)
                                    sp = (oi == nops - 1)
                                    if kind == "sc":
                                        last = t.matmul(reg,
                                                        vgk[r0:r0 + 64, c, jt * 128:(jt + 1) * 128],
                                                        vgq[r0:r0 + 64, c, ih * 512:(ih + 1) * 512],
                                                        start=st, stop=sp)
                                    elif kind == "full":
                                        last = t.matmul(reg[:, 0:arg], CONC.ap()[0:1, :],
                                                        CONR.ap()[0:1, 0:arg],
                                                        start=st, stop=sp)
                                    else:
                                        last = t.matmul(reg[:, arg:arg + 128], CID.ap(),
                                                        CL1.ap(), start=st, stop=sp)
                            inc(last, "ga_sc", 1)
                        t.wait_ge(S("ga_se"), b_gse + (h * 8 + 8) * 2)
                        for jt in range(8):
                            lav = emit_av(jt)
                        inc(lav, "ga_av", 2)
                        t.wait_ge(S("ga_nr"), b_gnr + h + 1)
                        last = None
                        for ih in range(2):
                            last = t.matmul(PS[4 + ih].ap()[0:64, 0:512], CONB.ap()[64:65, 0:64],
                                            SB2.ap()[64:65, ih * 512:(ih + 1) * 512],
                                            start=True, stop=True)
                        inc(last, "ga_rb", 1)

                @blk.scalar
                def _(s):
                    for h in range(16):
                        for jt in range(8):
                            gidx = h * 8 + jt
                            bank = (jt % 2) * 6
                            s.wait_ge(S("ga_sc"), b_gsc + gidx + 1)
                            s.activation(out=vet[:, jt, 0:1024],
                                         in_=psw(bank, 2), func=AF.Exp)
                            inc(s.drain(), "ga_se", 2)
                        s.wait_ge(S("ga_av"), b_gav + h * 2 + 2)
                        for ih in range(2):
                            s.activation(out=SB2.ap()[64:65, ih * 512:(ih + 1) * 512],
                                         in_=PS[2 + ih].ap()[64:65, 0:512], func=AF.Ln)
                        s.drain()
                        with nc.allow_low_precision(reason="softmax recip row bf16"):
                            s.activation(out=SB2.ap()[64:65, 0:1024],
                                         in_=SB2.ap()[64:65, 0:1024], func=AF.Exp,
                                         scale=-1.0)
                        inc(s.drain(), "ga_nr", 1)

                @blk.vector
                def _(v):
                    wrbc = view(WB, 2048)
                    for h in range(16):
                        c = h // 2
                        v.wait_ge(S("ga_rb"), b_grb + h + 1)
                        if h % 2 == 1:
                            v.wait_ge(S("dm"), b_gdm + 16 * (h // 2))
                        for ih in range(2):
                            v.tensor_copy(out=wrbc[0:64, ih * 512:(ih + 1) * 512],
                                          in_=PS[4 + ih].ap()[0:64, 0:512])
                        v.drain()
                        ei = None
                        for ih in range(2):
                            if h % 2 == 0:
                                dst = vgq[0:64, c, ih * 512:(ih + 1) * 512]
                            else:
                                dst = SB2.ap()[0:64, ih * 512:(ih + 1) * 512]
                            ei = v.tensor_mul(out=dst,
                                              in0=PS[2 + ih].ap()[0:64, 0:512],
                                              in1=wrbc[0:64, ih * 512:(ih + 1) * 512])
                        inc(ei, "ga_no", 1)
                        if h % 2 == 1:
                            inc(v.drain(), "ga_st", 1)

                @blk.sync
                def _(sy):
                    vgq4 = view(B1, DC, TH)
                    for ho in range(8):
                        sy.wait_ge(S("ga_st"), b_gst + ho + 1)
                        inc(sy.dma_start(out=vgq4[64:128, ho, :],
                                         in_=SB2.ap()[0:64, 0:1024]), "dm", 16)

            ensure_row("rowB", bgout_row)
            dump("go", B1)
            def ev_go(e, ps, ofc, ts, te):
                dst = vfus[:, ofc, ts:te]
                return e.scalar_tensor_tensor(out=dst, in0=ps, scalar=sm("w1", 1),
                                              in1=dst, op0=ALU.mult, op1=ALU.add)
            proj_fp32(wgout, 1024, vgq, TH, ev_go, evac_eng="v", bias_row=brow("rowB"),
                      pre=2)

            dump("fusg", B3)
            vd = view(B1, DC, TH)
            ln_1024(None, vfus, vd, None, view(B4, DC, TH),
                    gamma=sm("g1"), beta=sm("b1"))
            with nc.Block("l1_res") as blk:
                @blk.vector
                def _(v):
                    v.wait_ge(S("lnv"), val("lnv"))
                    v.tensor_add(out=xsl, in0=vd, in1=xsl)
                    inc(v.drain(), "gv", 1)

    def l1_ffn():
        def u_ap(hc):
            buf = [B3, B4, B5, B1][hc // 8]
            return view(buf, 8, TH)[:, hc % 8, :]

        for half in range(2):
            t0 = half * TH
            xsl = XRES.ap()[:, :, t0:t0 + TH]
            vy = view(B1, DC, TH)
            ln_1024(xsl, None, vy, view(B3, DC, TH), view(B4, DC, TH))
            with nc.Block("l1f_res1") as blk:
                @blk.vector
                def _(v):
                    v.wait_ge(S("lnv"), val("lnv"))
                    v.tensor_add(out=xsl, in0=vy, in1=xsl)
                    inc(v.drain(), "gv", 1)
            vh = view(B2, DC, TH)

            def pf_fc1(sy):
                src = fc1q[:, 0:128].rearrange("(kc p) of -> p kc of", p=P)
                inc(sy.dma_start(out=view(WS, 2, DC, 128)[:, 0].bitcast(I32),
                                 in_=src), "dmp0", 16)
            ln_1024(None, vy, vh, None, view(B4, DC, TH),
                    gamma=sm("g2"), beta=sm("b2"), pf=pf_fc1)

            b_wv = val("fwv"); b_mm = val("fmm")
            b_sl = val("fsl"); b_sr = val("fsr"); b_gt = val("fgt")
            b_p = [val("dmp0") - 16, val("dmp1")]
            with nc.Block("l1f_fc1") as blk:
                @blk.sync
                def _(sy):
                    for hc in range(32):
                        if hc >= 2:
                            sy.wait_ge(S("fwv"), b_wv + hc - 1)
                        if hc >= 1:
                            src = fc1q[:, hc * 128:(hc + 1) * 128].rearrange(
                                "(kc p) of -> p kc of", p=P)
                            inc(sy.dma_start(out=view(WS, 2, DC, 128)[:, hc % 2].bitcast(I32),
                                             in_=src), f"dmp{hc % 2}", 16)
                        inc(sy.dma_start(out=RSTG.ap()[0:1, (hc % 2) * 1024:(hc % 2) * 1024 + 1024],
                                         in_=s1_rows[hc:hc + 1, :]), f"dmp{hc % 2}", 16)
                    sy.wait_ge(S("fsr"), b_sr + 31)
                    src = fc2q[0:128, 0:256]
                    inc(sy.dma_start(out=view(WS, 4, 256)[:, 0].bitcast(I32),
                                     in_=src), "dmq0", 16)
                    inc(sy.dma_start(out=RSTG.ap()[0:1, 2048:2304],
                                     in_=s2_rows[0:1, 0:256]), "dmq0", 16)

                @blk.vector
                def _(v):
                    for i in range(33):
                        if i < 32:
                            hc = i
                            v.wait_ge(S(f"dmp{hc % 2}"), b_p[hc % 2] + 32 * (hc // 2 + 1))
                            if hc >= 2:
                                v.wait_ge(S("fmm"), b_mm + 2 * (hc - 1))
                            v.tensor_scalar_add(out=view(WB, 2, DC, 128)[:, hc % 2],
                                                in0=view(WS, 2, DC, 128)[:, hc % 2].bitcast(I32),
                                                scalar1=-128.0)
                            v.tensor_copy(out=BROW.ap()[0:1, (hc % 2) * 1024:(hc % 2) * 1024 + 1024],
                                          in_=RSTG.ap()[0:1, (hc % 2) * 1024:(hc % 2) * 1024 + 1024])
                            inc(v.drain(), "fsr", 1)
                        if i >= 1:
                            hc = i - 1
                            v.wait_ge(S("fgt"), b_gt + hc + 1)
                            wbd = view(WB, 2, 1024)[:, hc % 2]
                            v.tensor_mul(out=wbd, in0=wbd, in1=psw(6, 2)[:, 0:1024])
                            inc(v.drain(), "fwv", 1)

                @blk.tensor
                def _(t):
                    for hc in range(32):
                        t.wait_ge(S("fsr"), b_sr + hc + 1)
                        if hc >= 1:
                            t.wait_ge(S("fwv"), b_wv + hc)
                        srow = BROW.ap()[0:1, (hc % 2) * 1024:(hc % 2) * 1024 + 1024]
                        for bb in range(2):
                            t.matmul(PS[6 + bb].ap()[:, 0:512],
                                     CONC.ap()[0:1, :], srow[:, bb * 512:(bb + 1) * 512],
                                     start=True, stop=True)
                        inc(t.drain(), "fgt", 1)
                        t.wait_ge(S("fwv"), b_wv + hc + 1)
                        if hc >= 2:
                            t.wait_ge(S("fsl"), b_sl + 2 * (hc - 1))
                        last = None
                        for tb in range(2):
                            for kc in range(DC):
                                last = t.matmul(PS[2 * (hc % 2) + tb].ap()[:, 0:512],
                                                view(WB, 2, DC, 128)[:, hc % 2, kc, :],
                                                vh[:, kc, tb * 512:(tb + 1) * 512],
                                                start=(kc == 0), stop=(kc == DC - 1))
                            inc(last, "fmm", 1)

                @blk.scalar
                def _(s):
                    for hc in range(32):
                        for tb in range(2):
                            s.wait_ge(S("fmm"), b_mm + 2 * hc + tb + 1)
                            s.activation(out=u_ap(hc)[:, tb * 512:(tb + 1) * 512],
                                         in_=PS[2 * (hc % 2) + tb].ap()[:, 0:512], func=AF.Silu,
                                         bias=sm("fb1", 32)[:, hc:hc + 1])
                            inc(s.drain(), "fsl", 1)

            for ofq in range(4):
                ocs0 = ofq * 256
                b_wv = val("fwv"); b_mm = val("fmm")
                b_sr = val("fsr"); b_gt = val("fgt"); b_sl = val("fsl")
                b_pe = val("fpe")
                b_q = [val("dmq0") - 32] + [val(f"dmq{j}") for j in range(1, 4)]
                with nc.Block("l1f_fc2") as blk:
                    @blk.sync
                    def _(sy, ofq=ofq):
                        for kc in range(1, 32):
                            if kc >= 4:
                                sy.wait_ge(S("fwv"), b_wv + kc - 3)
                            src = fc2q[kc * 128:(kc + 1) * 128, ocs0:ocs0 + 256]
                            inc(sy.dma_start(out=view(WS, 4, 256)[:, kc % 4].bitcast(I32),
                                             in_=src), f"dmq{kc % 4}", 16)
                            inc(sy.dma_start(out=RSTG.ap()[0:1, 2048 + (kc % 4) * 256:2048 + (kc % 4) * 256 + 256],
                                             in_=s2_rows[kc:kc + 1, ocs0:ocs0 + 256]),
                                f"dmq{kc % 4}", 16)
                        if ofq < 3:
                            sy.wait_ge(S("fsr"), b_sr + 29)
                            nx0 = (ofq + 1) * 256
                            src = fc2q[0:128, nx0:nx0 + 256]
                            inc(sy.dma_start(out=view(WS, 4, 256)[:, 0].bitcast(I32),
                                             in_=src), "dmq0", 16)
                            inc(sy.dma_start(out=RSTG.ap()[0:1, 2048:2304],
                                             in_=s2_rows[0:1, nx0:nx0 + 256]), "dmq0", 16)

                    @blk.vector
                    def _(v):
                        for i in range(33):
                            if i < 32:
                                kc = i
                                v.wait_ge(S(f"dmq{kc % 4}"), b_q[kc % 4] + 32 * (kc // 4 + 1))
                                if kc >= 4:
                                    v.wait_ge(S("fpe"), b_pe + kc - 3)
                                v.tensor_scalar_add(out=view(WB, 4, 256, off=4096)[:, kc % 4],
                                                    in0=view(WS, 4, 256)[:, kc % 4].bitcast(I32),
                                                    scalar1=-128.0)
                                v.tensor_copy(out=BROW.ap()[0:1, (kc % 4) * 256:(kc % 4) * 256 + 256],
                                              in_=RSTG.ap()[0:1, 2048 + (kc % 4) * 256:2048 + (kc % 4) * 256 + 256])
                                inc(v.drain(), "fsr", 1)
                            if i >= 1:
                                kc = i - 1
                                v.wait_ge(S("fgt"), b_gt + kc + 1)
                                wbd = view(WB, 4, 256, off=4096)[:, kc % 4]
                                v.tensor_mul(out=wbd, in0=wbd,
                                             in1=PS[6 + (kc % 4) // 2].ap()[:, (kc % 2) * 256:(kc % 2) * 256 + 256])
                                inc(v.drain(), "fwv", 1)

                    @blk.tensor
                    def _(t):
                        for kc in range(32):
                            t.wait_ge(S("fsr"), b_sr + kc + 1)
                            if kc >= 4:
                                t.wait_ge(S("fwv"), b_wv + kc - 3)
                            srow = BROW.ap()[0:1, (kc % 4) * 256:(kc % 4) * 256 + 256]
                            t.matmul(PS[6 + (kc % 4) // 2].ap()[:, (kc % 2) * 256:(kc % 2) * 256 + 256],
                                     CONC.ap()[0:1, :], srow, start=True, stop=True)
                            inc(t.drain(), "fgt", 1)
                            t.wait_ge(S("fwv"), b_wv + kc + 1)
                            last = None
                            for jj2 in range(2):
                                for tb in range(2):
                                    last = t.matmul(PS[2 * jj2 + tb].ap()[:, 0:512],
                                                    view(WB, 4, 256, off=4096)[:, kc % 4, jj2 * 128:(jj2 + 1) * 128],
                                                    u_ap(kc)[:, tb * 512:(tb + 1) * 512],
                                                    start=(kc == 0), stop=(kc == 31))
                            inc(last, "fpe", 1)
                        inc(t.drain(), "fmm", 1)

                    @blk.scalar
                    def _(s, ofq=ofq):
                        s.wait_ge(S("fmm"), b_mm + 1)
                        ei = None
                        for jj2 in range(2):
                            ofc = ofq * 2 + jj2
                            for tb in range(2):
                                ei = s.activation(out=SB2.ap()[:, (2 * jj2 + tb) * 512:(2 * jj2 + tb + 1) * 512],
                                                  in_=PS[2 * jj2 + tb].ap()[:, 0:512],
                                                  func=AF.Identity, scale=0.5,
                                                  bias=sm("hfb2", 8)[:, ofc:ofc + 1])
                        inc(s.drain(), "fsl", 1)

                    @blk.vector
                    def _(v, ofq=ofq):
                        v.wait_ge(S("fsl"), b_sl + 1)
                        ei = None
                        for jj2 in range(2):
                            ofc = ofq * 2 + jj2
                            for tb in range(2):
                                dst = XRES.ap()[:, ofc, t0 + tb * 512:t0 + (tb + 1) * 512]
                                ei = v.tensor_add(out=dst,
                                                  in0=SB2.ap()[:, (2 * jj2 + tb) * 512:(2 * jj2 + tb + 1) * 512],
                                                  in1=dst)
                        inc(ei, "fev", 1)

    seq = [("L0A", l0_attn), ("L0F", l0_ffn), ("L1A", l1_attn), ("L1F", l1_ffn)]
    try:
        for name, fn in seq:
            fn()
            if stop_after == name:
                break
    except StopBuild:
        return nc

    with nc.Block("store") as blk:
        @blk.sync
        def _(sy):
            inc(sy.dma_start(out=y_out.rearrange("(c p) t -> p c t", p=P),
                             in_=XRES.ap()), "dm", 16)
            sy.wait_ge(S("dm"), val("dm"))

    return nc


def _prep_weights(inp):
    C = np.ascontiguousarray
    d = {}
    d["wq"] = C(inp["L0_attn_Wq"].T)
    d["wk"] = C(inp["L0_attn_Wk"].T)
    d["wv"] = C(inp["L0_attn_Wv"].T)
    d["wo"] = C(inp["L0_attn_Wo"].T)
    d["wg1"] = C(inp["L0_attn_Wg1"].T)
    d["wg2"] = C(inp["L0_attn_Wg2"].T)
    d["gn_col"] = C(inp["L0_attn_g_norm_w"].reshape(8, 128).T)
    d["w0g"] = C(inp["L0_ffn_w_gate"].T)
    d["w0u"] = C(inp["L0_ffn_w_up"].T)
    d["w0d"] = C(inp["L0_ffn_w_down"].T)
    d["pl"] = C(inp["L1_attn_proj_local"])
    d["pg"] = C(inp["L1_attn_proj_global"])
    d["wlin"] = C(inp["L1_attn_Wl_in"].T)
    d["wgin"] = C(inp["L1_attn_Wg_in"].T)
    d["wlout"] = C(inp["L1_attn_Wl_out"].T)
    d["wgout"] = C(inp["L1_attn_Wg_out"].T)
    bl = inp["L1_attn_bl_in"]; bg = inp["L1_attn_bg_in"]
    d["blq_col"] = C(bl[0:1024].reshape(8, 128).T)
    d["blk_col"] = C(bl[1024:2048].reshape(8, 128).T)
    d["blv_row"] = C(bl[2048:3072].reshape(1, 1024))
    d["bgq_col"] = C(bg[0:1024].reshape(8, 128).T)
    d["bgk_col"] = C(bg[1024:2048].reshape(8, 128).T)
    d["bgv_row"] = C(bg[2048:3072].reshape(1, 1024))
    d["blout_row"] = C(inp["L1_attn_bl_out"].reshape(1, 1024))
    d["bgout_row"] = C(inp["L1_attn_bg_out"].reshape(1, 1024))
    d["fusw"] = C(inp["L1_attn_fusion_w"].reshape(1, 2))
    d["g1_col"] = C(inp["L1_attn_ln_gamma"].reshape(8, 128).T)
    d["b1_col"] = C(inp["L1_attn_ln_beta"].reshape(8, 128).T)
    d["g2_col"] = C(inp["L1_ffn_ln_gamma"].reshape(8, 128).T)
    d["b2_col"] = C(inp["L1_ffn_ln_beta"].reshape(8, 128).T)
    d["fc1q"] = C(inp["L1_ffn_fc1_q"])
    d["fc2q"] = C(inp["L1_ffn_fc2_q"])
    s1 = inp["L1_ffn_fc1_scales"]
    d["s1_rows"] = C(np.stack([s1[:, hc * 128:(hc + 1) * 128].ravel()
                               for hc in range(32)]))
    d["s2_rows"] = C(inp["L1_ffn_fc2_scales"])
    d["fb1_col"] = C(inp["L1_ffn_fc1_bias"].reshape(32, 128).T)
    d["fb2_row"] = C(inp["L1_ffn_fc2_bias"].reshape(1, 1024))
    d["fb2_col"] = C(inp["L1_ffn_fc2_bias"].reshape(8, 128).T)
    return d


def kernel(_stop_after=None, _cores=None, **inputs):
    x = np.asarray(inputs["L0_attn_hidden_states"], dtype=np.float32)
    wd = _prep_weights({k: np.asarray(v) for k, v in inputs.items()})
    nc = build_program(stop_after=_stop_after)
    ncores = NCORES if _cores is None else _cores
    in_maps = []
    for b in range(ncores):
        m = dict(wd)
        m["x"] = np.ascontiguousarray(x[b].T)
        in_maps.append(m)
    res = run_bass_kernel_spmd(nc, in_maps, core_ids=list(range(ncores)))
    out = np.stack([np.asarray(r["y"], dtype=np.float32).T for r in res.results])
    if ncores < NCORES:
        pad = np.zeros((NCORES - ncores, T, 1024), np.float32)
        out = np.concatenate([out, pad], axis=0)
    return out


# version 19
# speedup vs baseline: 1.0065x; 1.0065x over previous
import numpy as np
import ml_dtypes
import concourse.bass as bass
import concourse.mybir as mybir
from concourse.bass_utils import run_bass_kernel_spmd

F32 = mybir.dt.float32
BF16 = mybir.dt.bfloat16
I32 = mybir.dt.int32
AF = mybir.ActivationFunctionType
ALU = mybir.AluOpType
AX = mybir.AxisListType

P = 128
T = 2048
TH = 1024
DC = 8
NCORES = 8
LN_EPS = 1e-5
QS = 1.0 / 127.0
BF = ml_dtypes.bfloat16


def _ceil(a, b):
    return (a + b - 1) // b


def view(buf, *shape, off=0):
    n = int(np.prod(shape))
    ap = buf.ap()[:, off:off + n]
    if len(shape) == 1:
        return ap
    pat = "p (" + " ".join(f"d{i}" for i in range(len(shape))) + ") -> p " + \
          " ".join(f"d{i}" for i in range(len(shape)))
    return ap.rearrange(pat, **{f"d{i}": s for i, s in enumerate(shape)})


class StopBuild(Exception):
    pass


def build_program(stop_after=None, dbg=None):
    nc = bass.Bass("TRN2", target_bir_lowering=False)

    x_in = nc.dram_tensor("x", [1024, T], F32, kind="ExternalInput").ap()
    y_out = nc.dram_tensor("y", [1024, T], F32, kind="ExternalOutput").ap()

    def din(name, shape, dt=F32):
        return nc.dram_tensor(name, shape, dt, kind="ExternalInput").ap()

    wq = din("wq", [1024, 1024]); wk = din("wk", [1024, 1024])
    wv = din("wv", [1024, 1024]); wo = din("wo", [1024, 1024])
    wg1 = din("wg1", [1024, 128]); wg2 = din("wg2", [128, 1024])
    gn_col = din("gn_col", [128, 8])
    w0g = din("w0g", [1024, 3072], I32)
    w0u = din("w0u", [1024, 3072], I32)
    w0d = din("w0d", [3072, 1024], I32)
    pl = din("pl", [1024, 1024]); pg = din("pg", [1024, 1024])
    wlin = din("wlin", [1024, 3072]); wgin = din("wgin", [1024, 3072])
    wlout = din("wlout", [1024, 1024]); wgout = din("wgout", [1024, 1024])
    blq_col = din("blq_col", [128, 8]); blk_col = din("blk_col", [128, 8])
    bgq_col = din("bgq_col", [128, 8]); bgk_col = din("bgk_col", [128, 8])
    blv_row = din("blv_row", [1, 1024]); bgv_row = din("bgv_row", [1, 1024])
    blout_row = din("blout_row", [1, 1024]); bgout_row = din("bgout_row", [1, 1024])
    fusw = din("fusw", [1, 2])
    g1_col = din("g1_col", [128, 8]); b1_col = din("b1_col", [128, 8])
    g2_col = din("g2_col", [128, 8]); b2_col = din("b2_col", [128, 8])
    fc1q = din("fc1q", [1024, 4096], I32)
    fc2q = din("fc2q", [4096, 1024], I32)
    s1_rows = din("s1_rows", [32, 1024]); s2_rows = din("s2_rows", [32, 1024])
    fb1_col = din("fb1_col", [128, 32]); fb2_row = din("fb2_row", [1, 1024])
    fb2_col = din("fb2_col", [128, 8])

    jj, ii = np.meshgrid(np.arange(P), np.arange(P), indexing="ij")
    ident_h = nc.inline_tensor(np.eye(P, dtype=BF), name="c_ident").ap()
    gmask_h = nc.inline_tensor(np.tile((ii >= jj).astype(BF), (1, 4)), name="c_gmask").ap()
    l1m_h = nc.inline_tensor((jj > ii).astype(BF), name="c_l1m").ap()
    onesr_h = nc.inline_tensor(np.ones((1, 512), dtype=BF), name="c_onesr").ap()
    onesc_h = nc.inline_tensor(np.ones((1, 128), dtype=BF), name="c_onesc").ap()
    sumc_h = nc.inline_tensor(np.ones((128, 1), dtype=BF), name="c_sumc").ap()
    _cb = np.zeros((128, 128), dtype=BF); _cb[64, :] = 1.0
    conb_h = nc.inline_tensor(_cb, name="c_conb").ap()
    _fconsts = [LN_EPS, -0.5, 1.0 / 1024.0, QS, 0.125, 1.0, 0.0]
    fcon_h = nc.inline_tensor(np.tile(np.asarray(_fconsts, np.float32), (P, 1)),
                              name="c_fcon").ap()

    A = nc.alloc_sbuf_tensor
    XRES = A("XRES", [P, DC, T], F32)
    BN = 8320
    B1 = A("B1", [P, BN], BF16)
    B2 = A("B2", [P, BN], BF16)
    B3 = A("B3", [P, BN], BF16)
    B4 = A("B4", [P, BN], BF16)
    B5 = A("B5", [P, BN], BF16)
    WS = A("WS", [P, 4096], F32)
    WB = A("WB", [P, 8192], BF16)
    SGLA = A("SGLA", [P, DC, 128], F32)
    SM = A("SM", [P, 640], F32)
    SB2 = A("SB2", [P, 2048], BF16)
    BROW = A("BROW", [1, 2048], BF16)
    RSTG = A("RSTG", [1, 3072], F32)
    CID = A("CID", [P, 128], BF16)
    CGM = A("CGM", [P, 512], BF16)
    CL1 = A("CL1", [P, 128], BF16)
    CONR = A("CONR", [1, 512], BF16)
    CONC = A("CONC", [1, 128], BF16)
    CSUM = A("CSUM", [P, 1], BF16)
    CONB = A("CONB", [P, 128], BF16)
    CFC = A("CFC", [P, len(_fconsts)], F32)
    for _j, _v in enumerate(_fconsts):
        nc.const_aps.register(F32, float(_v), CFC.ap()[:, _j:_j + 1])

    PSALL = nc.alloc_psum_tensor("psall", [P, 4096], F32)

    class _PSView:
        def __init__(self, i):
            self.i = i

        def ap(self):
            return PSALL.ap()[:, self.i * 512:(self.i + 1) * 512]

    PS = [_PSView(i) for i in range(8)]

    def psb(i):
        return PS[i].ap().bitcast(BF16)

    def psw(i, n):
        return PSALL.ap()[:, i * 512:(i + n) * 512]

    _blkn = [0]
    _origBlock = nc.Block

    def _Block(tag):
        _blkn[0] += 1
        return _origBlock(f"{tag}{_blkn[0]}")
    nc.Block = _Block

    sems = {}
    cnt = {}

    def S(name):
        if name not in sems:
            sems[name] = nc.alloc_semaphore(name)
            cnt[name] = 0
        return sems[name]

    def inc(instr, name, k=1):
        instr.then_inc(S(name), k)
        cnt[name] += k
        return cnt[name]

    def val(name):
        S(name)
        return cnt[name]

    smoff = {}
    _smn = [0]

    def sm(name, width=8):
        if name not in smoff:
            smoff[name] = _smn[0]
            _smn[0] += width
            assert _smn[0] <= 640, "SM overflow"
        return SM.ap()[:, smoff[name]:smoff[name] + width]

    broff = {}
    _brn = [0]

    def brow(name, width=1024):
        if name not in broff:
            broff[name] = _brn[0]
            _brn[0] += width
            assert _brn[0] <= 2048, "BROW overflow"
        return BROW.ap()[:, broff[name]:broff[name] + width]

    with nc.Block("init") as blk:
        @blk.sync
        def _(sy):
            xr = x_in.rearrange("(c p) t -> p c t", p=P)
            inc(sy.dma_start(out=XRES.ap()[:, :, 0:TH], in_=xr[:, :, 0:TH]), "dm", 16)
            inc(sy.dma_start(out=XRES.ap()[:, :, TH:2 * TH], in_=xr[:, :, TH:2 * TH]),
                "dmx", 16)
            for dst, src in [(CID, ident_h), (CGM, gmask_h), (CL1, l1m_h),
                             (CONR, onesr_h), (CONC, onesc_h), (CSUM, sumc_h),
                             (CFC, fcon_h), (CONB, conb_h)]:
                inc(sy.dma_start(out=dst.ap(), in_=src), "dm", 16)
            for name, h in [("gn", gn_col), ("blq", blq_col), ("blk", blk_col),
                            ("bgq", bgq_col), ("bgk", bgk_col), ("g1", g1_col),
                            ("b1", b1_col), ("g2", g2_col), ("b2", b2_col)]:
                inc(sy.dma_start(out=sm(name, 8), in_=h), "dm", 16)
            inc(sy.dma_start(out=sm("fb1", 32), in_=fb1_col), "dm", 16)
            inc(sy.dma_start(out=sm("fb2c", 8), in_=fb2_col), "dm", 16)

        @blk.vector
        def _(v):
            v.wait_ge(S("dm"), val("dm"))
            v.memset(SGLA.ap(), 0.0)
            v.tensor_scalar_mul(out=sm("blq8"), in0=sm("blq"), scalar1=0.125)
            v.tensor_scalar_mul(out=sm("bgq8"), in0=sm("bgq"), scalar1=0.125)
            v.tensor_scalar_mul(out=sm("hfb2", 8), in0=sm("fb2c", 8), scalar1=0.5)

    def ensure_row(slot, hbm_row):
        with nc.Block("rowcvt") as blk:
            @blk.sync
            def _(sy):
                inc(sy.dma_start(out=RSTG.ap()[0:1, 0:1024], in_=hbm_row), "dm", 16)

            @blk.vector
            def _(v):
                v.wait_ge(S("dm"), val("dm"))
                v.tensor_copy(out=brow(slot), in_=RSTG.ap()[0:1, 0:1024])

    def dump(tag, buf, f32=False):
        if dbg != tag:
            return
        with nc.Block("dbg") as blk:
            @blk.sync
            def _(sy):
                yv = y_out.rearrange("(a b) t -> a (b t)", a=128)
                src = buf.ap() if f32 else buf.ap().bitcast(F32)
                n = src.shape[1] if len(src.shape) == 2 else int(np.prod(src.shape[1:]))
                srcf = src if len(src.shape) == 2 else src.rearrange("p a b -> p (a b)")
                inc(sy.dma_start(out=yv[:, 0:n], in_=srcf), "dm", 16)
                sy.wait_ge(S("dm"), val("dm"))
        raise StopBuild

    def ln_1024(src_f32, src_bf, out_view, scr1, scr2, gamma=None, beta=None,
                pf=None, xw=None):
        srcx = src_f32 if src_f32 is not None else src_bf
        with nc.Block("ln") as blk:
            if pf is not None:
                @blk.sync
                def _(sy):
                    if callable(pf):
                        pf(sy)
                    else:
                        pf_emit(sy, *pf)

            @blk.vector
            def _(v):
                v.wait_ge(S("dm"), val("dm"))
                if xw is not None:
                    for nm, tg in xw:
                        v.wait_ge(S(nm), tg)
                if src_f32 is not None:
                    v.tensor_copy(out=scr1, in_=src_f32)
                inc(v.drain(), "lnv", 1)

            @blk.scalar
            def _(s):
                if xw is not None:
                    for nm, tg in xw:
                        s.wait_ge(S(nm), tg)
                s.activation(out=scr2, in_=srcx, func=AF.Square)
                inc(s.drain(), "lns", 1)

            @blk.tensor
            def _(t):
                t.wait_ge(S("lnv"), val("lnv"))
                t.wait_ge(S("lns"), val("lns"))
                xb = scr1 if src_f32 is not None else src_bf
                for b in range(2):
                    c0, c1 = b * 512, (b + 1) * 512
                    for c in range(DC):
                        t.matmul(PS[0 + b].ap()[0:1, 0:512], CSUM.ap(),
                                 xb[:, c, c0:c1], start=(c == 0), stop=(c == DC - 1))
                    for c in range(DC):
                        t.matmul(PS[2 + b].ap()[0:1, 0:512], CSUM.ap(),
                                 scr2[:, c, c0:c1], start=(c == 0), stop=(c == DC - 1))
                inc(t.drain(), "lnt", 1)

            @blk.vector
            def _(v):
                v.wait_ge(S("lnt"), val("lnt"))
                mu = RSTG.ap()[0:1, 1024:2048]
                var = RSTG.ap()[0:1, 2048:3072]
                for b in range(2):
                    bs = slice(b * 512, (b + 1) * 512)
                    v.tensor_scalar_mul(out=mu[:, bs], in0=PS[0 + b].ap()[0:1, 0:512],
                                        scalar1=1.0 / 1024.0)
                v.drain()
                for b in range(2):
                    bs = slice(b * 512, (b + 1) * 512)
                    v.tensor_mul(out=var[:, bs], in0=mu[:, bs], in1=mu[:, bs])
                    v.drain()
                    v.scalar_tensor_tensor(out=var[:, bs], in0=PS[2 + b].ap()[0:1, 0:512],
                                           scalar=1.0 / 1024.0, in1=var[:, bs],
                                           op0=ALU.mult, op1=ALU.subtract)
                inc(v.drain(), "lnv", 1)

            @blk.scalar
            def _(s):
                s.wait_ge(S("lnv"), val("lnv"))
                rstd = RSTG.ap()[0:1, 2048:3072]
                s.activation(out=rstd, in_=rstd, func=AF.Ln, bias=LN_EPS)
                s.drain()
                s.activation(out=rstd, in_=rstd, func=AF.Exp, scale=-0.5)
                inc(s.drain(), "lns", 1)

            @blk.vector
            def _(v):
                v.wait_ge(S("lns"), val("lns"))
                mu = RSTG.ap()[0:1, 1024:2048]
                rstd = RSTG.ap()[0:1, 2048:3072]
                arow = RSTG.ap().bitcast(BF16)[0:1, 0:1024]
                brw = RSTG.ap().bitcast(BF16)[0:1, 1024:2048]
                v.tensor_copy(out=arow, in_=rstd)
                v.scalar_tensor_tensor(out=brw, in0=mu, scalar=-1.0, in1=rstd,
                                       op0=ALU.mult, op1=ALU.mult)
                inc(v.drain(), "lnv", 1)

            @blk.tensor
            def _(t):
                t.wait_ge(S("lnv"), val("lnv"))
                arow = RSTG.ap().bitcast(BF16)[0:1, 0:1024]
                brw = RSTG.ap().bitcast(BF16)[0:1, 1024:2048]
                for b in range(2):
                    bs = slice(b * 512, (b + 1) * 512)
                    t.matmul(PS[4 + b].ap()[:, 0:512], CONC.ap()[0:1, :],
                             arow[:, bs], start=True, stop=True)
                    t.matmul(PS[6 + b].ap()[:, 0:512], CONC.ap()[0:1, :],
                             brw[:, bs], start=True, stop=True)
                inc(t.drain(), "lnt", 1)

            @blk.vector
            def _(v):
                v.wait_ge(S("lnt"), val("lnt"))
                for b in range(2):
                    c0, c1 = b * 512, (b + 1) * 512
                    abc = PS[4 + b].ap()[:, 0:512].unsqueeze(1).broadcast_to([P, DC, 512])
                    bbc = PS[6 + b].ap()[:, 0:512].unsqueeze(1).broadcast_to([P, DC, 512])
                    v.tensor_mul(out=scr2[:, :, c0:c1], in0=srcx[:, :, c0:c1], in1=abc)
                    v.drain()
                    v.tensor_add(out=out_view[:, :, c0:c1], in0=scr2[:, :, c0:c1], in1=bbc)
                if gamma is not None:
                    v.drain()
                    for c in range(DC):
                        v.tensor_scalar(out=out_view[:, c, :], in0=out_view[:, c, :],
                                        scalar1=gamma[:, c:c + 1], scalar2=beta[:, c:c + 1],
                                        op0=ALU.mult, op1=ALU.add)
                inc(v.drain(), "lnv", 1)

    def load_w_fp32(w_hbm, of0, nofe, wb_off=0, scale_cols=None, kc_n=DC):
        per = 2048 // kc_n
        o = 0
        while o < nofe:
            cur = min(per, nofe - o)
            with nc.Block("ldw") as blk:
                @blk.sync
                def _(sy, o=o, cur=cur):
                    src = w_hbm[:, of0 + o:of0 + o + cur].rearrange(
                        "(kc p) of -> p kc of", p=P)
                    inc(sy.dma_start(out=view(WS, kc_n, cur), in_=src), "dm", 16)

                @blk.vector
                def _(v, o=o, cur=cur):
                    v.wait_ge(S("dm"), val("dm"))
                    dst = view(WB, kc_n, nofe, off=wb_off)
                    if scale_cols is None:
                        v.tensor_copy(out=dst[:, :, o:o + cur], in_=view(WS, kc_n, cur))
                    else:
                        for kc in range(kc_n):
                            v.tensor_scalar_mul(out=dst[:, kc, o:o + cur],
                                                in0=view(WS, kc_n, cur)[:, kc, :],
                                                scalar1=scale_cols[:, kc:kc + 1])
                    inc(v.drain(), "wv", 1)
            o += cur

    def n_units(nofe, kc_n):
        return _ceil(nofe, 2048 // kc_n)

    def pf_emit(sy, w_hbm, nofe, kc_n, war=None):
        unit_ofe = 2048 // kc_n
        nu = min(2, n_units(nofe, kc_n))
        for u in range(nu):
            if war is not None and war[u] is not None:
                sy.wait_ge(S("pwv"), war[u])
            o0, o1 = u * unit_ofe, min(nofe, (u + 1) * unit_ofe)
            src = w_hbm[:, o0:o1].rearrange("(kc p) of -> p kc of", p=P)
            inc(sy.dma_start(out=view(WS, 2, kc_n, unit_ofe)[:, u, :, 0:o1 - o0],
                             in_=src), f"dmp{u}", 16)
        return nu

    def fm_proj(src_bf, nt, nof, evac, evac_eng="v", bias_row=None, kc_n=DC,
                wbv=None, of_base=0):
        wbv_ = wbv if wbv is not None else view(WB, kc_n, nof * 128)
        tiles = [(ofc, tb) for ofc in range(nof) for tb in range(_ceil(nt, 512))]
        mm_b = val("pmm"); ev_b = val("pev")
        with nc.Block("proj") as blk:
            @blk.tensor
            def _(t):
                t.wait_ge(S("wv"), val("wv"))
                t.wait_ge(S("dm"), val("dm"))
                for i, (ofc, tb) in enumerate(tiles):
                    bank = i % 2
                    ts, te = tb * 512, min(nt, (tb + 1) * 512)
                    if i >= 2:
                        t.wait_ge(S("pev"), ev_b + i - 1)
                    last = None
                    for kc in range(kc_n):
                        last = t.matmul(PS[bank].ap()[:, 0:te - ts],
                                        wbv_[:, kc, ofc * 128:(ofc + 1) * 128],
                                        src_bf[:, kc, ts:te], start=(kc == 0),
                                        stop=(kc == kc_n - 1 and bias_row is None))
                    if bias_row is not None:
                        last = t.matmul(PS[bank].ap()[:, 0:te - ts],
                                        bias_row[0:1, (of_base + ofc) * 128:(of_base + ofc + 1) * 128],
                                        CONR.ap()[0:1, 0:te - ts], start=False, stop=True)
                    inc(last, "pmm", 1)

            def emit_evac(eng):
                for i, (ofc, tb) in enumerate(tiles):
                    bank = i % 2
                    ts, te = tb * 512, min(nt, (tb + 1) * 512)
                    eng.wait_ge(S("pmm"), mm_b + i + 1)
                    ei = evac(eng, PS[bank].ap()[:, 0:te - ts], ofc, ts, te)
                    inc(ei, "pev", 1)

            if evac_eng == "v":
                @blk.vector
                def _(v):
                    emit_evac(v)
            else:
                @blk.scalar
                def _(s):
                    emit_evac(s)

    def proj_fp32(w_hbm, nofe, src_bf, nt, evac, evac_eng="v", bias_row=None,
                  kc_n=DC, of_base=0, scale_cols=None, pre=0, pf_next=None):
        unit_ofe = 2048 // kc_n
        units = []
        o = 0
        while o < nofe:
            units.append((o, min(nofe, o + unit_ofe)))
            o = units[-1][1]
        ntb = _ceil(nt, 512)
        tcounts = []
        tot = 0
        for (o0, o1) in units:
            tot += ((o1 - o0) // 128) * ntb
            tcounts.append(tot)
        b_p = [val("dmp0") - (16 if pre >= 1 else 0),
               val("dmp1") - (16 if pre >= 2 else 0)]
        b_wv = val("pwv"); b_mm = val("pmm"); b_ev = val("pev")
        b_dmt = val("dm")
        with nc.Block("projp") as blk:
            @blk.sync
            def _(sy):
                for u, (o0, o1) in enumerate(units):
                    if u < pre:
                        continue
                    if u >= 2:
                        sy.wait_ge(S("pwv"), b_wv + u - 1)
                    src = w_hbm[:, o0:o1].rearrange("(kc p) of -> p kc of", p=P)
                    inc(sy.dma_start(out=view(WS, 2, kc_n, unit_ofe)[:, u % 2, :, 0:o1 - o0],
                                     in_=src), f"dmp{u % 2}", 16)
                if pf_next is not None:
                    U = len(units)
                    war = [b_wv + ((U - 1) // 2) * 2 + 1, None]
                    if U >= 2:
                        war[1] = b_wv + ((U - 2) // 2) * 2 + 2
                    pf_emit(sy, *pf_next, war=war)

            def unit_tiles(u):
                o0, o1 = units[u]
                base = tcounts[u - 1] if u > 0 else 0
                out = []
                i = base
                for ofr in range((o1 - o0) // 128):
                    for tb in range(ntb):
                        out.append((i, o0 // 128 + ofr, tb))
                        i += 1
                return out

            def emit_ev_unit(eng, u):
                for i, ofc, tb in unit_tiles(u):
                    bank = i % 8
                    ts, te = tb * 512, min(nt, (tb + 1) * 512)
                    eng.wait_ge(S("pmm"), b_mm + i + 1)
                    ei = evac(eng, PS[bank].ap()[:, 0:te - ts], ofc, ts, te)
                    inc(ei, "pev", 1)

            @blk.vector
            def _(v):
                for step in range(len(units) + 1):
                    if step < len(units):
                        u = step
                        o0, o1 = units[u]
                        v.wait_ge(S(f"dmp{u % 2}"), b_p[u % 2] + 16 * (u // 2 + 1))
                        if u >= 2:
                            v.wait_ge(S("pmm"), b_mm + tcounts[u - 2])
                        dstw = view(WB, 2, kc_n, unit_ofe)[:, u % 2]
                        srcw = view(WS, 2, kc_n, unit_ofe)[:, u % 2]
                        if scale_cols is None:
                            v.tensor_copy(out=dstw[:, :, 0:o1 - o0], in_=srcw[:, :, 0:o1 - o0])
                        else:
                            for kc in range(kc_n):
                                v.tensor_scalar_mul(out=dstw[:, kc, 0:o1 - o0],
                                                    in0=srcw[:, kc, 0:o1 - o0],
                                                    scalar1=scale_cols[:, kc:kc + 1])
                        inc(v.drain(), "pwv", 1)
                    if evac_eng == "v" and step >= 1:
                        emit_ev_unit(v, step - 1)

            @blk.tensor
            def _(t):
                t.wait_ge(S("dm"), b_dmt)
                i = 0
                for u, (o0, o1) in enumerate(units):
                    t.wait_ge(S("pwv"), b_wv + u + 1)
                    for ofr in range((o1 - o0) // 128):
                        ofc = o0 // 128 + ofr
                        for tb in range(ntb):
                            bank = i % 8
                            ts, te = tb * 512, min(nt, (tb + 1) * 512)
                            if i >= 8:
                                t.wait_ge(S("pev"), b_ev + i - 7)
                            last = None
                            for kc in range(kc_n):
                                last = t.matmul(PS[bank].ap()[:, 0:te - ts],
                                                view(WB, 2, kc_n, unit_ofe)[:, u % 2, kc, ofr * 128:(ofr + 1) * 128],
                                                src_bf[:, kc, ts:te], start=(kc == 0),
                                                stop=(kc == kc_n - 1 and bias_row is None))
                            if bias_row is not None:
                                last = t.matmul(PS[bank].ap()[:, 0:te - ts],
                                                bias_row[0:1, (of_base + ofc) * 128:(of_base + ofc + 1) * 128],
                                                CONR.ap()[0:1, 0:te - ts], start=False, stop=True)
                            inc(last, "pmm", 1)
                            i += 1

            if evac_eng == "s":
                @blk.scalar
                def _(se):
                    for u in range(len(units)):
                        emit_ev_unit(se, u)

    def tm_proj(src_bf, vdst, bias_name, w_hbm, pre=0, pf_next=None):
        ensure_row("rowA", bias_name)
        b_mm = val("pmm"); b_ev = val("pev"); b_wv = val("pwv")
        b_p = [val("dmp0") - (16 if pre >= 1 else 0),
               val("dmp1") - (16 if pre >= 2 else 0)]
        with nc.Block("tmproj") as blk:
            @blk.sync
            def _(sy):
                for u in range(4):
                    if u < pre:
                        continue
                    if u >= 2:
                        sy.wait_ge(S("pwv"), b_wv + u - 1)
                    src = w_hbm[:, u * 256:(u + 1) * 256].rearrange(
                        "(kc p) of -> p kc of", p=P)
                    inc(sy.dma_start(out=view(WS, 2, DC, 256)[:, u % 2],
                                     in_=src), f"dmp{u % 2}", 16)
                if pf_next is not None:
                    pf_emit(sy, *pf_next, war=[b_wv + 3, b_wv + 4])

            @blk.vector
            def _(v):
                for u in range(4):
                    v.wait_ge(S(f"dmp{u % 2}"), b_p[u % 2] + 16 * (u // 2 + 1))
                    v.tensor_copy(out=view(WB, DC, 1024)[:, :, u * 256:(u + 1) * 256],
                                  in_=view(WS, 2, DC, 256)[:, u % 2])
                    inc(v.drain(), "pwv", 1)
                i = 0
                for ofb in range(2):
                    for tt in range(8):
                        bank = i % 4
                        v.wait_ge(S("pmm"), b_mm + i + 1)
                        ei = None
                        for hh in range(8):
                            h = ofb * 8 + hh
                            ei = v.tensor_copy(out=vdst[:, tt, h, 0:64],
                                               in_=PS[bank].ap()[:, hh * 64:(hh + 1) * 64])
                        inc(ei, "pev", 1)
                        i += 1

            @blk.tensor
            def _(t):
                t.wait_ge(S("gv"), val("gv"))
                t.wait_ge(S("dm"), val("dm"))
                i = 0
                for ofb in range(2):
                    t.wait_ge(S("pwv"), b_wv + 2 * (ofb + 1))
                    for tt in range(8):
                        bank = i % 4
                        if i >= 4:
                            t.wait_ge(S("pev"), b_ev + i - 3)
                        for kc in range(DC):
                            t.matmul(PS[bank].ap()[:, 0:512],
                                     src_bf[:, kc, tt * 128:(tt + 1) * 128],
                                     view(WB, DC, 1024)[:, kc, ofb * 512:(ofb + 1) * 512],
                                     start=(kc == 0), stop=False)
                        last = t.matmul(PS[bank].ap()[:, 0:512], CONC.ap()[0:1, :],
                                        brow("rowA")[0:1, ofb * 512:(ofb + 1) * 512],
                                        start=False, stop=True)
                        inc(last, "pmm", 1)
                        i += 1

    def l0_attn():
        vq = view(B2, DC, TH)
        vE = view(B3, DC, TH)
        vvt = view(B4, DC, TH)
        vgs = view(B5, DC, TH)
        vP = view(B1, DC, TH)
        for half in range(2):
            t0 = half * TH
            xsl = XRES.ap()[:, :, t0:t0 + TH]
            ln_1024(xsl, None, vP, view(B2, DC, TH), view(B3, DC, TH),
                    pf=(wq, 1024, DC), xw=[("dmx", 16)] if half == 1 else None)
            dump("ln0", B1)

            def ev_q(e, ps, ofc, ts, te):
                return e.activation(out=vq[:, ofc, ts:te], in_=ps, func=AF.Silu)
            proj_fp32(wq, 1024, vP, TH, ev_q, evac_eng="s", pre=2,
                      pf_next=(wg1, 128, DC))
            dump("q", B2)

            def ev_g1(e, ps, ofc, ts, te):
                return e.tensor_copy(out=SB2.ap()[:, ts:te], in_=ps)
            proj_fp32(wg1, 128, vP, TH, ev_g1, evac_eng="v", pre=1,
                      pf_next=(wg2, 1024, 1))

            def ev_g2(e, ps, ofc, ts, te):
                return e.activation(out=vgs[:, ofc, ts:te], in_=ps, func=AF.Silu)
            g1v = SB2.ap()[:, 0:TH].rearrange("p (a t) -> p a t", a=1)
            proj_fp32(wg2, 1024, g1v, TH, ev_g2, evac_eng="s", kc_n=1, pre=1,
                      pf_next=(wk, 1024, DC))

            def ev_k(e, ps, ofc, ts, te):
                return e.activation(out=vE[:, ofc, ts:te], in_=ps, func=AF.Exp)
            proj_fp32(wk, 1024, vP, TH, ev_k, evac_eng="s", pre=2,
                      pf_next=(wv, 1024, DC))
            dump("E", B3)

            b_mm = val("pmm"); b_ev = val("pev"); b_wv = val("pwv")
            b_p = [val("dmp0") - 16, val("dmp1") - 16]
            with nc.Block("gla_v") as blk:
                @blk.sync
                def _(sy):
                    for u in range(2, 4):
                        sy.wait_ge(S("pwv"), b_wv + u - 1)
                        src = wv[:, u * 256:(u + 1) * 256].rearrange(
                            "(kc p) of -> p kc of", p=P)
                        inc(sy.dma_start(out=view(WS, 2, DC, 256)[:, u % 2],
                                         in_=src), f"dmp{u % 2}", 16)
                    pf_emit(sy, wo, 1024, DC, war=[b_wv + 3, b_wv + 4])

                @blk.vector
                def _(v):
                    for u in range(4):
                        v.wait_ge(S(f"dmp{u % 2}"), b_p[u % 2] + 16 * (u // 2 + 1))
                        v.tensor_copy(out=view(WB, DC, 1024)[:, :, u * 256:(u + 1) * 256],
                                      in_=view(WS, 2, DC, 256)[:, u % 2])
                        inc(v.drain(), "pwv", 1)
                    i = 0
                    for ofb in range(2):
                        for tt in range(8):
                            bank = i % 2
                            v.wait_ge(S("pmm"), b_mm + i + 1)
                            ei = v.tensor_copy(out=vvt[:, tt, ofb * 512:(ofb + 1) * 512],
                                               in_=PS[bank].ap()[:, 0:512])
                            inc(ei, "pev", 1)
                            i += 1

                @blk.tensor
                def _(t):
                    i = 0
                    for ofb in range(2):
                        t.wait_ge(S("pwv"), b_wv + 2 * (ofb + 1))
                        for tt in range(8):
                            bank = i % 2
                            if i >= 2:
                                t.wait_ge(S("pev"), b_ev + i - 1)
                            last = None
                            for kc in range(DC):
                                last = t.matmul(PS[bank].ap()[:, 0:512],
                                                vP[:, kc, tt * 128:(tt + 1) * 128],
                                                view(WB, DC, 1024)[:, kc, ofb * 512:(ofb + 1) * 512],
                                                start=(kc == 0), stop=(kc == DC - 1))
                            inc(last, "pmm", 1)
                            i += 1

            dump("v", B4)
            dump("gsw", B5)

            with nc.Block("gla_fac") as blk:
                @blk.vector
                def _(v, half=half):
                    v.wait_ge(S("pev"), val("pev"))
                    for h in range(DC):
                        init = 0.0 if half == 0 else sm("Pcar")[:, h:h + 1]
                        v.tensor_tensor_scan(out=vP[:, h, :], data0=vE[:, h, :],
                                             data1=vE[:, h, :], initial=init,
                                             op0=ALU.add, op1=ALU.bypass)
                    v.drain()
                    pbp4 = sm("Pbp", 64).rearrange("p (h n) -> p h n", h=8).unsqueeze(3)
                    pls4 = sm("Pls", 64).rearrange("p (h n) -> p h n", h=8).unsqueeze(3)
                    vP4c = view(B1, DC, 8, 128)
                    v.tensor_copy(out=pls4, in_=vP4c[:, :, :, 127:128])
                    v.tensor_copy(out=pbp4[:, :, 1:8, :], in_=vP4c[:, :, 0:7, 127:128])
                    if half == 0:
                        v.tensor_copy(out=pbp4[:, :, 0:1, :], in_=vP4c[:, :, 0:1, 0:1])
                    else:
                        v.tensor_copy(out=pbp4[:, :, 0:1, :],
                                      in_=sm("Pcar").unsqueeze(2).unsqueeze(3))
                    v.drain()
                    v.tensor_copy(out=sm("Pcar").unsqueeze(2).unsqueeze(3),
                                  in_=vP4c[:, :, 7:8, 127:128])
                    v.drain()
                    v.reciprocal(out=sm("rPbp", 64), in_=sm("Pbp", 64))
                    v.reciprocal(out=sm("rPls", 64), in_=sm("Pls", 64))
                    v.drain()
                    v.tensor_mul(out=sm("ebl", 64), in0=sm("Pbp", 64), in1=sm("rPls", 64))
                    inc(v.drain(), "gv", 1)

                @blk.scalar
                def _(s):
                    s.wait_ge(S("gv"), val("gv"))
                    s.activation(out=vP, in_=vP, func=AF.Ln)
                    s.activation(out=sm("lnPbp", 64), in_=sm("Pbp", 64), func=AF.Ln)
                    inc(s.drain(), "gs", 1)

                @blk.vector
                def _(v):
                    v.wait_ge(S("gs"), val("gs"))
                    lnpbp = sm("lnPbp", 64).rearrange("p (h n) -> p h n", h=8)
                    rep = lnpbp.unsqueeze(3).broadcast_to([P, 8, 8, 128])
                    vP4 = view(B1, DC, 8, 128)
                    v.tensor_sub(out=vP4, in0=rep, in1=vP4)
                    inc(v.drain(), "gv", 1)

                @blk.scalar
                def _(s):
                    s.wait_ge(S("gv"), val("gv"))
                    s.activation(out=vP, in_=vP, func=AF.Exp)
                    inc(s.drain(), "gs", 1)

                @blk.vector
                def _(v, half=half):
                    v.wait_ge(S("gs"), val("gs"))
                    v.tensor_mul(out=vq, in0=vq, in1=vP)
                    rpb = sm("rPbp", 64).rearrange("p (h n) -> p h n", h=8)
                    rep = rpb.unsqueeze(3).broadcast_to([P, 8, 8, 128])
                    vE4 = view(B3, DC, 8, 128)
                    v.tensor_mul(out=vE4, in0=vE4, in1=rep)
                    if half == 0:
                        v.memset(SB2.ap()[:, 1024:2048], 0.0)
                    inc(v.drain(), "gv", 1)

            dump("fac", B1)
            dump("q2", B2)
            dump("k2", B3)

            etm = SB2.ap()[:, 0:1024].rearrange("p (h d) -> p h d", h=8)
            vq4o = view(B2, DC, 8, 128)
            sbf = SB2.ap()[:, 1024:2048].rearrange("p (h d) -> p h d", h=8)
            amask = view(WB, 8, 128)
            ebl = sm("ebl", 64)
            b_mm = val("g_mm"); b_tr = val("g_tr"); b_ds = val("g_ds")
            b_om = val("g_om"); b_ae = val("g_ae"); b_te = val("g_te")
            b_su = val("g_su"); b_oe = val("g_oe")
            with nc.Block("gla_chunks") as blk:
                @blk.tensor
                def _(t):
                    t.wait_ge(S("gv"), val("gv"))
                    for n in range(8):
                        nsl = slice(n * 128, (n + 1) * 128)
                        t.wait_ge(S("g_ae"), b_ae + 2 * n)
                        last = None
                        for h in range(8):
                            last = t.matmul(PS[h // 4].ap()[:, (h % 4) * 128:(h % 4 + 1) * 128],
                                            vE[:, h, nsl], vq[:, h, nsl],
                                            start=True, stop=True)
                        inc(last, "g_mm", 1)
                        t.wait_ge(S("g_te"), b_te + n)
                        for h in range(8):
                            last = t.transpose(psb(2)[:, h * 128:(h + 1) * 128],
                                               vE[:, h, nsl], CID.ap())
                        inc(last, "g_tr", 1)
                        t.wait_ge(S("g_su"), b_su + n)
                        t.wait_ge(S("g_te"), b_te + n + 1)
                        for h in range(8):
                            last = t.matmul(PS[3 + h // 4].ap()[:, (h % 4) * 128:(h % 4 + 1) * 128],
                                            etm[:, h, :], vvt[:, n, h * 128:(h + 1) * 128],
                                            start=True, stop=True)
                        inc(last, "g_ds", 1)
                        t.wait_ge(S("g_ae"), b_ae + 2 * (n + 1))
                        t.wait_ge(S("g_oe"), b_oe + n)
                        for h in range(8):
                            reg = PS[5 + h // 4].ap()[:, (h % 4) * 128:(h % 4 + 1) * 128]
                            t.matmul(reg, sbf[:, h, :], vq[:, h, nsl], start=True, stop=False)
                            last = t.matmul(reg, vvt[:, n, h * 128:(h + 1) * 128],
                                            amask[:, h, :], start=False, stop=True)
                        inc(last, "g_om", 1)

                @blk.vector
                def _(v):
                    for n in range(8):
                        v.wait_ge(S("g_mm"), b_mm + n + 1)
                        v.wait_ge(S("g_om"), b_om + n)
                        for g in range(2):
                            v.tensor_mul(
                                out=amask[:, 4 * g:4 * (g + 1), :].rearrange("p a b -> p (a b)"),
                                in0=PS[g].ap()[:, 0:512], in1=CGM.ap())
                        inc(v.drain(), "g_ae", 2)
                        v.wait_ge(S("g_ds"), b_ds + n + 1)
                        v.wait_ge(S("g_om"), b_om + n + 1)
                        for g in range(2):
                            v.tensor_add(out=SGLA.ap().rearrange("p h d -> p (h d)")[:, g * 512:(g + 1) * 512],
                                         in0=PS[3 + g].ap()[:, 0:512],
                                         in1=SGLA.ap().rearrange("p h d -> p (h d)")[:, g * 512:(g + 1) * 512])
                        v.drain()
                        for h in range(8):
                            gi = h * 8 + n
                            v.tensor_scalar_mul(out=SGLA.ap()[:, h, :], in0=SGLA.ap()[:, h, :],
                                                scalar1=ebl[:, gi:gi + 1])
                        v.drain()
                        v.tensor_copy(out=SB2.ap()[:, 1024:2048],
                                      in_=SGLA.ap().rearrange("p h d -> p (h d)"))
                        inc(v.drain(), "g_su", 1)

                @blk.scalar
                def _(s):
                    for n in range(8):
                        nsl = slice(n * 128, (n + 1) * 128)
                        s.wait_ge(S("g_tr"), b_tr + n + 1)
                        s.wait_ge(S("g_ds"), b_ds + n)
                        s.copy(out=SB2.ap()[:, 0:1024], in_=psb(2)[:, 0:1024])
                        inc(s.drain(), "g_te", 1)
                        s.wait_ge(S("g_om"), b_om + n + 1)
                        ei = s.activation(out=vq[:, :, nsl].transpose([0, 2, 1]) if False
                                          else vq4o[:, :, n, :],
                                          in_=psw(5, 2), func=AF.Copy)
                        inc(ei, "g_oe", 1)

            dump("o", B2)
            dump("sm", SM, f32=True)
            dump("S", SGLA, f32=True)
            dump("am", WB)
            dump("sb2", SB2)

            vosq = view(B3, DC, TH)
            with nc.Block("gla_rms") as blk:
                @blk.scalar
                def _(s):
                    s.wait_ge(S("g_oe"), val("g_oe"))
                    s.activation(out=vosq, in_=vq, func=AF.Square)
                    inc(s.drain(), "gs", 1)

                @blk.tensor
                def _(t):
                    t.wait_ge(S("gs"), val("gs"))
                    for b in range(2):
                        for c in range(DC):
                            t.matmul(PS[0 + b].ap()[0:1, 0:512], CSUM.ap(),
                                     vosq[:, c, b * 512:(b + 1) * 512],
                                     start=(c == 0), stop=(c == DC - 1))
                    inc(t.drain(), "gt", 1)

                @blk.scalar
                def _(s):
                    s.wait_ge(S("gt"), val("gt"))
                    rr = RSTG.ap()[0:1, 2048:3072]
                    for b in range(2):
                        s.activation(out=rr[:, b * 512:(b + 1) * 512],
                                     in_=PS[b].ap()[0:1, 0:512], func=AF.Ln,
                                     scale=1.0 / 1024.0, bias=LN_EPS)
                    s.drain()
                    s.activation(out=rr, in_=rr, func=AF.Exp, scale=-0.5)
                    inc(s.drain(), "gs", 1)

                @blk.vector
                def _(v):
                    v.wait_ge(S("gs"), val("gs"))
                    ab = RSTG.ap().bitcast(BF16)[0:1, 0:1024]
                    v.tensor_copy(out=ab, in_=RSTG.ap()[0:1, 2048:3072])
                    inc(v.drain(), "gv", 1)

                @blk.tensor
                def _(t):
                    t.wait_ge(S("gv"), val("gv"))
                    ab = RSTG.ap().bitcast(BF16)[0:1, 0:1024]
                    for b in range(2):
                        t.matmul(PS[2 + b].ap()[:, 0:512], CONC.ap()[0:1, :],
                                 ab[:, b * 512:(b + 1) * 512], start=True, stop=True)
                    inc(t.drain(), "gt", 1)

                @blk.vector
                def _(v):
                    v.wait_ge(S("gt"), val("gt"))
                    for b in range(2):
                        c0, c1 = b * 512, (b + 1) * 512
                        rbc = PS[2 + b].ap()[:, 0:512].unsqueeze(1).broadcast_to([P, DC, 512])
                        v.tensor_mul(out=vosq[:, :, c0:c1], in0=vq[:, :, c0:c1], in1=rbc)
                    v.drain()
                    v.tensor_mul(out=vosq, in0=vosq, in1=vgs)
                    inc(v.drain(), "gv", 1)

            def ev_wo(e, ps, ofc, ts, te):
                return e.tensor_add(out=XRES.ap()[:, ofc, t0 + ts:t0 + te],
                                    in0=ps, in1=XRES.ap()[:, ofc, t0 + ts:t0 + te])
            proj_fp32(wo, 1024, vosq, TH, ev_wo, evac_eng="v", scale_cols=sm("gn"),
                      pre=2)

    def l0_ffn():
        vln = view(B1, DC, TH)

        def h_ap(hc):
            buf = [B2, B3, B4][hc // 8]
            return view(buf, 8, TH)[:, hc % 8, :]

        for half in range(2):
            t0 = half * TH
            xsl = XRES.ap()[:, :, t0:t0 + TH]
            def pf_gu(sy):
                for wi, wt in enumerate([w0g, w0u]):
                    src = wt[:, 0:128].rearrange("(kc p) of -> p kc of", p=P)
                    inc(sy.dma_start(
                        out=view(WS, 2, 2, DC, 128)[:, 0, wi].bitcast(I32),
                        in_=src), "dmp0", 16)
            ln_1024(xsl, None, vln, view(B2, DC, TH), view(B3, DC, TH), pf=pf_gu)

            b_wv = val("fwv"); b_mm = val("fmm")
            b_sl = val("fsl"); b_ev = val("fev")
            b_p = [val("dmp0") - 32, val("dmp1")]
            with nc.Block("l0f_gu") as blk:
                @blk.sync
                def _(sy):
                    for hc in range(1, 24):
                        if hc >= 2:
                            sy.wait_ge(S("fmm"), b_mm + 4 * (hc - 1))
                        for wi, wt in enumerate([w0g, w0u]):
                            src = wt[:, hc * 128:(hc + 1) * 128].rearrange(
                                "(kc p) of -> p kc of", p=P)
                            inc(sy.dma_start(
                                out=view(WS, 2, 2, DC, 128)[:, hc % 2, wi].bitcast(I32),
                                in_=src), f"dmp{hc % 2}", 16)
                    for kc2 in range(2):
                        sy.wait_ge(S("fwv"), b_wv + 23 + kc2)
                        src = w0d[kc2 * 128:(kc2 + 1) * 128, 0:512]
                        inc(sy.dma_start(out=view(WS, 2, 512)[:, kc2, :].bitcast(I32),
                                         in_=src), f"dmp{kc2}", 16)

                @blk.vector
                def _(v):
                    for i in range(25):
                        if i < 24:
                            hc = i
                            v.wait_ge(S(f"dmp{hc % 2}"), b_p[hc % 2] + 32 * (hc // 2 + 1))
                            if hc >= 2:
                                v.wait_ge(S("fmm"), b_mm + 4 * (hc - 1))
                            for wi in range(2):
                                v.tensor_copy(out=view(WB, 2, 2, DC, 128)[:, hc % 2, wi],
                                              in_=view(WS, 2, 2, DC, 128)[:, hc % 2, wi].bitcast(I32))
                            inc(v.drain(), "fwv", 1)
                        if i >= 1:
                            hc = i - 1
                            for tb in range(2):
                                v.wait_ge(S("fsl"), b_sl + 2 * hc + tb + 1)
                                v.wait_ge(S("fmm"), b_mm + 4 * hc + 2 + tb + 1)
                                ei = v.tensor_mul(out=h_ap(hc)[:, tb * 512:(tb + 1) * 512],
                                                  in0=PS[4 * (hc % 2) + 2 + tb].ap()[:, 0:512],
                                                  in1=SB2.ap()[:, (2 * (hc % 2) + tb) * 512:(2 * (hc % 2) + tb + 1) * 512])
                                inc(ei, "fev", 1)

                @blk.tensor
                def _(t):
                    for hc in range(24):
                        t.wait_ge(S("fwv"), b_wv + hc + 1)
                        if hc >= 2:
                            t.wait_ge(S("fsl"), b_sl + 2 * (hc - 1))
                            t.wait_ge(S("fev"), b_ev + 2 * (hc - 1))
                        last = None
                        for wi in range(2):
                            for tb in range(2):
                                bank = 4 * (hc % 2) + 2 * wi + tb
                                for kc in range(DC):
                                    last = t.matmul(PS[bank].ap()[:, 0:512],
                                                    view(WB, 2, 2, DC, 128)[:, hc % 2, wi, kc, :],
                                                    vln[:, kc, tb * 512:(tb + 1) * 512],
                                                    start=(kc == 0), stop=(kc == DC - 1))
                                inc(last, "fmm", 1)

                @blk.scalar
                def _(s):
                    for hc in range(24):
                        for tb in range(2):
                            s.wait_ge(S("fmm"), b_mm + 4 * hc + tb + 1)
                            s.activation(out=SB2.ap()[:, (2 * (hc % 2) + tb) * 512:(2 * (hc % 2) + tb + 1) * 512],
                                         in_=PS[4 * (hc % 2) + tb].ap()[:, 0:512], func=AF.Silu,
                                         scale=QS)
                            inc(s.drain(), "fsl", 1)


            for ofh in range(2):
                def wd_ap(kc):
                    if kc < 16:
                        return view(B1, 16, 512)[:, kc, :]
                    return view(WB, 8, 512)[:, kc - 16, :]
                b_wv = val("fwv"); b_mm = val("fmm")
                b_p = [val("dmp0") - 16, val("dmp1") - 16]
                with nc.Block("l0f_dn") as blk:
                    @blk.sync
                    def _(sy, ofh=ofh):
                        for kc in range(2, 24):
                            sy.wait_ge(S("fwv"), b_wv + kc - 1)
                            src = w0d[kc * 128:(kc + 1) * 128, ofh * 512:(ofh + 1) * 512]
                            inc(sy.dma_start(out=view(WS, 2, 512)[:, kc % 2, :].bitcast(I32),
                                             in_=src), f"dmp{kc % 2}", 16)
                        if ofh == 0:
                            for kc2 in range(2):
                                sy.wait_ge(S("fwv"), b_wv + 23 + kc2)
                                src = w0d[kc2 * 128:(kc2 + 1) * 128, 512:1024]
                                inc(sy.dma_start(out=view(WS, 2, 512)[:, kc2, :].bitcast(I32),
                                                 in_=src), f"dmp{kc2}", 16)

                    @blk.vector
                    def _(v):
                        for kc in range(24):
                            v.wait_ge(S(f"dmp{kc % 2}"), b_p[kc % 2] + 16 * (kc // 2 + 1))
                            v.tensor_copy(out=wd_ap(kc),
                                          in_=view(WS, 2, 512)[:, kc % 2, :].bitcast(I32))
                            inc(v.drain(), "fwv", 1)
                        for tb in range(2):
                            v.wait_ge(S("fmm"), b_mm + tb + 1)
                            ei = None
                            for j in range(4):
                                ofc = ofh * 4 + j
                                dst = XRES.ap()[:, ofc, t0 + tb * 512:t0 + (tb + 1) * 512]
                                ei = v.scalar_tensor_tensor(out=dst,
                                                            in0=PS[4 * tb + j].ap()[:, 0:512],
                                                            scalar=QS * QS, in1=dst,
                                                            op0=ALU.mult, op1=ALU.add)
                            inc(ei, "fev", 1)

                    @blk.tensor
                    def _(t):
                        for tb in range(2):
                            last = None
                            for kc in range(24):
                                if tb == 0:
                                    t.wait_ge(S("fwv"), b_wv + kc + 1)
                                for j in range(4):
                                    last = t.matmul(PS[4 * tb + j].ap()[:, 0:512],
                                                    wd_ap(kc)[:, j * 128:(j + 1) * 128],
                                                    h_ap(kc)[:, tb * 512:(tb + 1) * 512],
                                                    start=(kc == 0), stop=(kc == 23))
                            inc(last, "fmm", 1)

    def l1_attn():
        with nc.Block("l1_fw") as blk:
            @blk.sync
            def _(sy):
                inc(sy.dma_start(out=RSTG.ap()[0:1, 0:2], in_=fusw), "dm", 16)

            @blk.scalar
            def _(s):
                s.wait_ge(S("dm"), val("dm"))
                s.activation(out=RSTG.ap()[0:1, 1024:1026], in_=RSTG.ap()[0:1, 0:2], func=AF.Exp)
                inc(s.drain(), "gs", 1)

            @blk.vector
            def _(v):
                v.wait_ge(S("gs"), val("gs"))
                v.reduce_sum(out=RSTG.ap()[0:1, 1026:1027], in_=RSTG.ap()[0:1, 1024:1026], axis=AX.X)
                v.drain()
                v.reciprocal(out=RSTG.ap()[0:1, 1027:1028], in_=RSTG.ap()[0:1, 1026:1027])
                v.drain()
                v.tensor_scalar_mul(out=RSTG.ap().bitcast(BF16)[0:1, 0:2],
                                    in0=RSTG.ap()[0:1, 1024:1026],
                                    scalar1=RSTG.ap()[0:1, 1027:1028])
                inc(v.drain(), "gv", 1)

            @blk.tensor
            def _(t):
                t.wait_ge(S("gv"), val("gv"))
                wb2 = RSTG.ap().bitcast(BF16)[0:1, 0:2]
                t.matmul(PS[0].ap()[:, 0:1], CONC.ap()[0:1, :], wb2[:, 0:1],
                         start=True, stop=True)
                t.matmul(PS[0].ap()[:, 1:2], CONC.ap()[0:1, :], wb2[:, 1:2],
                         start=True, stop=True)
                inc(t.drain(), "gt", 1)

            @blk.vector
            def _(v):
                v.wait_ge(S("gt"), val("gt"))
                v.tensor_copy(out=sm("w0", 1), in_=PS[0].ap()[:, 0:1])
                v.tensor_copy(out=sm("w1", 1), in_=PS[0].ap()[:, 1:2])
                inc(v.drain(), "gv", 1)

        for w in range(2):
            t0 = w * TH
            xsl = XRES.ap()[:, :, t0:t0 + TH]
            vln2 = view(B1, DC, TH)
            ln_1024(xsl, None, vln2, view(B3, DC, TH), view(B4, DC, TH),
                    pf=(pl, 1024, DC))
            dump("l1ln", B1)

            def ev_cp(dst):
                def f(e, ps, ofc, ts, te):
                    return e.tensor_copy(out=dst[:, ofc, ts:te], in_=ps)
                return f

            vlqs = view(B2, DC, TH)
            proj_fp32(pl, 1024, vln2, TH, ev_cp(vlqs), evac_eng="v", pre=2,
                      pf_next=(wlin[:, 0:1024], 1024, DC))
            dump("lqs", B2)

            vlq = view(B3, DC, TH); vlk = view(B4, DC, TH)
            vlv = view(B5, 8, 16, 65)
            def ev_q8(e, ps, ofc, ts, te):
                return e.activation(out=vlq[:, ofc, ts:te], in_=ps, func=AF.Identity,
                                    scale=0.125, bias=sm("blq8")[:, ofc:ofc + 1])
            proj_fp32(wlin[:, 0:1024], 1024, vlqs, TH, ev_q8, evac_eng="s", pre=2,
                      pf_next=(wlin[:, 1024:2048], 1024, DC))
            def ev_kb(e, ps, ofc, ts, te):
                return e.activation(out=vlk[:, ofc, ts:te], in_=ps, func=AF.Identity,
                                    bias=sm("blk")[:, ofc:ofc + 1])
            proj_fp32(wlin[:, 1024:2048], 1024, vlqs, TH, ev_kb, evac_eng="s", pre=2,
                      pf_next=(wlin[:, 2048:3072], 1024, DC))
            with nc.Block("l1_lvo") as blk:
                @blk.vector
                def _(v):
                    v.memset(vlv[:, :, :, 64:65], 1.0)
                    inc(v.drain(), "gv", 1)
            tm_proj(vlqs, vlv, blv_row, wlin[:, 2048:3072], pre=2,
                    pf_next=(wlout, 1024, DC))
            dump("lq", B3)
            dump("lk", B4)
            dump("lv", B5)

            vlo = view(B2, DC, TH)
            b_sc = val("l_sc"); b_se = val("l_se"); b_av = val("l_av")
            b_nr = val("l_nr"); b_rb = val("l_rb"); b_no = val("l_no")
            b_dm = val("dm"); b_st = val("l_st")
            with nc.Block("l1_local") as blk:
                @blk.tensor
                def _(t):
                    t.wait_ge(S("pev"), val("pev"))
                    for lw in range(8):
                        lsl = slice(lw * 128, (lw + 1) * 128)
                        t.wait_ge(S("l_no"), b_no + lw * 4)
                        last = None
                        for h in range(16):
                            r0 = (h % 2) * 64
                            reg = PS[h // 4].ap()[:, (h % 4) * 128:(h % 4 + 1) * 128]
                            t.matmul(reg, vlk[r0:r0 + 64, h // 2, lsl],
                                     vlq[r0:r0 + 64, h // 2, lsl], start=True, stop=False)
                            last = t.matmul(reg, CID.ap(), CL1.ap(), start=False, stop=True)
                            if h % 4 == 3:
                                inc(last, "l_sc", 1)
                        t.wait_ge(S("l_se"), b_se + lw * 4 + 4)
                        for h in range(16):
                            reg = PS[4 + h // 4].ap()[:, (h % 4) * 128:(h % 4 + 1) * 128]
                            last = t.matmul(reg[0:65, :], vlv[:, lw, h, :],
                                            SB2.ap()[:, h * 128:(h + 1) * 128],
                                            start=True, stop=True)
                            if h % 4 == 3:
                                inc(last, "l_av", 1)
                        t.wait_ge(S("l_nr"), b_nr + lw + 1)
                        last = None
                        for g in range(4):
                            last = t.matmul(PS[g].ap()[0:64, 0:512], CONB.ap()[64:65, 0:64],
                                            SB2.ap()[64:65, g * 512:(g + 1) * 512],
                                            start=True, stop=True)
                        inc(last, "l_rb", 1)

                @blk.scalar
                def _(s):
                    for lw in range(8):
                        s.wait_ge(S("dm"), b_dm + 16 * lw)
                        s.wait_ge(S("l_sc"), b_sc + lw * 4 + 2)
                        s.activation(out=SB2.ap()[:, 0:1024], in_=psw(0, 2), func=AF.Exp)
                        s.wait_ge(S("l_sc"), b_sc + lw * 4 + 4)
                        s.activation(out=SB2.ap()[:, 1024:2048], in_=psw(2, 2), func=AF.Exp)
                        inc(s.drain(), "l_se", 4)
                        s.wait_ge(S("l_av"), b_av + lw * 4 + 4)
                        for g in range(4):
                            s.activation(out=SB2.ap()[64:65, g * 512:(g + 1) * 512],
                                         in_=PS[4 + g].ap()[64:65, 0:512], func=AF.Ln)
                        s.drain()
                        with nc.allow_low_precision(reason="softmax recip row bf16"):
                            s.activation(out=SB2.ap()[64:65, 0:2048],
                                         in_=SB2.ap()[64:65, 0:2048], func=AF.Exp,
                                         scale=-1.0)
                        inc(s.drain(), "l_nr", 1)

                @blk.vector
                def _(v):
                    wrbc = view(WB, 2048)
                    for lw in range(8):
                        v.wait_ge(S("l_rb"), b_rb + lw + 1)
                        v.wait_ge(S("dm"), b_dm + 16 * lw)
                        for g in range(4):
                            v.tensor_copy(out=wrbc[0:64, g * 512:(g + 1) * 512],
                                          in_=PS[g].ap()[0:64, 0:512])
                        v.drain()
                        ei = None
                        for h in range(16):
                            g = h // 4
                            src = PS[4 + g].ap()[0:64, (h % 4) * 128:(h % 4 + 1) * 128]
                            rb = wrbc[0:64, h * 128:(h + 1) * 128]
                            if h % 2 == 0:
                                dst = vlo[0:64, h // 2, lw * 128:(lw + 1) * 128]
                            else:
                                dst = SB2.ap()[0:64, (h // 2) * 128:(h // 2 + 1) * 128]
                            ei = v.tensor_mul(out=dst, in0=src, in1=rb)
                        inc(ei, "l_no", 4)
                        inc(v.drain(), "l_st", 1)

                @blk.sync
                def _(sy):
                    vlo4 = view(B2, DC, 8, 128)
                    stg = SB2.ap()[0:64, 0:1024].rearrange("p (c t) -> p c t", c=8)
                    for lw in range(8):
                        sy.wait_ge(S("l_st"), b_st + lw + 1)
                        inc(sy.dma_start(out=vlo4[64:128, :, lw, :], in_=stg), "dm", 16)

            dump("lo", B2)
            vfus = view(B3, DC, TH)
            ensure_row("rowB", blout_row)

            def ev_lo(e, ps, ofc, ts, te):
                return e.activation(out=vfus[:, ofc, ts:te], in_=ps, func=AF.Identity,
                                    scale=sm("w0", 1))
            proj_fp32(wlout, 1024, vlo, TH, ev_lo, evac_eng="s", bias_row=brow("rowB"),
                      pre=2, pf_next=(pg, 1024, DC))

            dump("fusl", B3)
            vgqs = view(B4, DC, TH)
            proj_fp32(pg, 1024, vln2, TH, ev_cp(vgqs), evac_eng="v", pre=2,
                      pf_next=(wgin[:, 0:1024], 1024, DC))

            vgq = view(B1, DC, TH); vgk = view(B2, DC, TH)
            vgv = view(B5, 8, 16, 65)
            def ev_gq(e, ps, ofc, ts, te):
                return e.activation(out=vgq[:, ofc, ts:te], in_=ps, func=AF.Identity,
                                    scale=0.125, bias=sm("bgq8")[:, ofc:ofc + 1])
            proj_fp32(wgin[:, 0:1024], 1024, vgqs, TH, ev_gq, evac_eng="s", pre=2,
                      pf_next=(wgin[:, 1024:2048], 1024, DC))
            def ev_gk(e, ps, ofc, ts, te):
                return e.activation(out=vgk[:, ofc, ts:te], in_=ps, func=AF.Identity,
                                    bias=sm("bgk")[:, ofc:ofc + 1])
            proj_fp32(wgin[:, 1024:2048], 1024, vgqs, TH, ev_gk, evac_eng="s", pre=2,
                      pf_next=(wgin[:, 2048:3072], 1024, DC))
            with nc.Block("l1_gvo") as blk:
                @blk.vector
                def _(v):
                    v.memset(vgv[:, :, :, 64:65], 1.0)
                    inc(v.drain(), "gv", 1)
            tm_proj(vgqs, vgv, bgv_row, wgin[:, 2048:3072], pre=2,
                    pf_next=(wgout, 1024, DC))

            vet = view(B4, 8, TH)
            b_gsc = val("ga_sc"); b_gse = val("ga_se"); b_gav = val("ga_av")
            b_gnr = val("ga_nr"); b_grb = val("ga_rb"); b_gno = val("ga_no")
            b_gdm = val("dm"); b_gst = val("ga_st")
            with nc.Block("l1_global") as blk:
                @blk.tensor
                def _(t):
                    t.wait_ge(S("pev"), val("pev"))
                    for h in range(16):
                        c = h // 2
                        r0 = (h % 2) * 64

                        def emit_av(jt):
                            last = None
                            for ih in range(2):
                                last = t.matmul(PS[2 + ih].ap()[0:65, 0:512],
                                                vgv[:, jt, h, :],
                                                vet[:, jt, ih * 512:(ih + 1) * 512],
                                                start=(jt == 0), stop=(jt == 7))
                            return last

                        t.wait_ge(S("ga_no"), b_gno + h)
                        for jt in range(8):
                            gidx = h * 8 + jt
                            bank = (jt % 2) * 6
                            if gidx >= 2:
                                t.wait_ge(S("ga_se"), b_gse + (gidx - 1) * 2)
                            last = None
                            for ih in range(2):
                                reg = PS[bank + ih].ap()[:, 0:512]
                                ops = [("sc", None)]
                                dlo = jt * 128
                                s0, s1 = ih * 512, (ih + 1) * 512
                                if dlo > s0:
                                    ops.append(("full", min(dlo, s1) - s0))
                                if dlo < s1 and dlo + 128 > s0:
                                    ops.append(("diag", dlo - s0))
                                nops = len(ops)
                                for oi, (kind, arg) in enumerate(ops):
                                    st = (oi == 0)
                                    sp = (oi == nops - 1)
                                    if kind == "sc":
                                        last = t.matmul(reg,
                                                        vgk[r0:r0 + 64, c, jt * 128:(jt + 1) * 128],
                                                        vgq[r0:r0 + 64, c, ih * 512:(ih + 1) * 512],
                                                        start=st, stop=sp)
                                    elif kind == "full":
                                        last = t.matmul(reg[:, 0:arg], CONC.ap()[0:1, :],
                                                        CONR.ap()[0:1, 0:arg],
                                                        start=st, stop=sp)
                                    else:
                                        last = t.matmul(reg[:, arg:arg + 128], CID.ap(),
                                                        CL1.ap(), start=st, stop=sp)
                            inc(last, "ga_sc", 1)
                        t.wait_ge(S("ga_se"), b_gse + (h * 8 + 8) * 2)
                        for jt in range(8):
                            lav = emit_av(jt)
                        inc(lav, "ga_av", 2)
                        t.wait_ge(S("ga_nr"), b_gnr + h + 1)
                        last = None
                        for ih in range(2):
                            last = t.matmul(PS[4 + ih].ap()[0:64, 0:512], CONB.ap()[64:65, 0:64],
                                            SB2.ap()[64:65, ih * 512:(ih + 1) * 512],
                                            start=True, stop=True)
                        inc(last, "ga_rb", 1)

                @blk.scalar
                def _(s):
                    for h in range(16):
                        for jt in range(8):
                            gidx = h * 8 + jt
                            bank = (jt % 2) * 6
                            s.wait_ge(S("ga_sc"), b_gsc + gidx + 1)
                            s.activation(out=vet[:, jt, 0:1024],
                                         in_=psw(bank, 2), func=AF.Exp)
                            inc(s.drain(), "ga_se", 2)
                        s.wait_ge(S("ga_av"), b_gav + h * 2 + 2)
                        for ih in range(2):
                            s.activation(out=SB2.ap()[64:65, ih * 512:(ih + 1) * 512],
                                         in_=PS[2 + ih].ap()[64:65, 0:512], func=AF.Ln)
                        s.drain()
                        with nc.allow_low_precision(reason="softmax recip row bf16"):
                            s.activation(out=SB2.ap()[64:65, 0:1024],
                                         in_=SB2.ap()[64:65, 0:1024], func=AF.Exp,
                                         scale=-1.0)
                        inc(s.drain(), "ga_nr", 1)

                @blk.vector
                def _(v):
                    wrbc = view(WB, 2048)
                    for h in range(16):
                        c = h // 2
                        v.wait_ge(S("ga_rb"), b_grb + h + 1)
                        if h % 2 == 1:
                            v.wait_ge(S("dm"), b_gdm + 16 * (h // 2))
                        for ih in range(2):
                            v.tensor_copy(out=wrbc[0:64, ih * 512:(ih + 1) * 512],
                                          in_=PS[4 + ih].ap()[0:64, 0:512])
                        v.drain()
                        ei = None
                        for ih in range(2):
                            if h % 2 == 0:
                                dst = vgq[0:64, c, ih * 512:(ih + 1) * 512]
                            else:
                                dst = SB2.ap()[0:64, ih * 512:(ih + 1) * 512]
                            ei = v.tensor_mul(out=dst,
                                              in0=PS[2 + ih].ap()[0:64, 0:512],
                                              in1=wrbc[0:64, ih * 512:(ih + 1) * 512])
                        inc(ei, "ga_no", 1)
                        if h % 2 == 1:
                            inc(v.drain(), "ga_st", 1)

                @blk.sync
                def _(sy):
                    vgq4 = view(B1, DC, TH)
                    for ho in range(8):
                        sy.wait_ge(S("ga_st"), b_gst + ho + 1)
                        inc(sy.dma_start(out=vgq4[64:128, ho, :],
                                         in_=SB2.ap()[0:64, 0:1024]), "dm", 16)

            ensure_row("rowB", bgout_row)
            dump("go", B1)
            def ev_go(e, ps, ofc, ts, te):
                dst = vfus[:, ofc, ts:te]
                return e.scalar_tensor_tensor(out=dst, in0=ps, scalar=sm("w1", 1),
                                              in1=dst, op0=ALU.mult, op1=ALU.add)
            proj_fp32(wgout, 1024, vgq, TH, ev_go, evac_eng="v", bias_row=brow("rowB"),
                      pre=2)

            dump("fusg", B3)
            vd = view(B1, DC, TH)
            ln_1024(None, vfus, vd, None, view(B4, DC, TH),
                    gamma=sm("g1"), beta=sm("b1"))
            with nc.Block("l1_res") as blk:
                @blk.vector
                def _(v):
                    v.wait_ge(S("lnv"), val("lnv"))
                    v.tensor_add(out=xsl, in0=vd, in1=xsl)
                    inc(v.drain(), "gv", 1)

    def l1_ffn():
        def u_ap(hc):
            buf = [B3, B4, B5, B1][hc // 8]
            return view(buf, 8, TH)[:, hc % 8, :]

        for half in range(2):
            t0 = half * TH
            xsl = XRES.ap()[:, :, t0:t0 + TH]
            vy = view(B1, DC, TH)
            ln_1024(xsl, None, vy, view(B3, DC, TH), view(B4, DC, TH))
            with nc.Block("l1f_res1") as blk:
                @blk.vector
                def _(v):
                    v.wait_ge(S("lnv"), val("lnv"))
                    v.tensor_add(out=xsl, in0=vy, in1=xsl)
                    inc(v.drain(), "gv", 1)
            vh = view(B2, DC, TH)

            def pf_fc1(sy):
                src = fc1q[:, 0:128].rearrange("(kc p) of -> p kc of", p=P)
                inc(sy.dma_start(out=view(WS, 2, DC, 128)[:, 0].bitcast(I32),
                                 in_=src), "dmp0", 16)
            ln_1024(None, vy, vh, None, view(B4, DC, TH),
                    gamma=sm("g2"), beta=sm("b2"), pf=pf_fc1)

            b_wv = val("fwv"); b_mm = val("fmm")
            b_sl = val("fsl"); b_sr = val("fsr"); b_gt = val("fgt")
            b_p = [val("dmp0") - 16, val("dmp1")]
            with nc.Block("l1f_fc1") as blk:
                @blk.sync
                def _(sy):
                    for hc in range(32):
                        if hc >= 2:
                            sy.wait_ge(S("fwv"), b_wv + hc - 1)
                        if hc >= 1:
                            src = fc1q[:, hc * 128:(hc + 1) * 128].rearrange(
                                "(kc p) of -> p kc of", p=P)
                            inc(sy.dma_start(out=view(WS, 2, DC, 128)[:, hc % 2].bitcast(I32),
                                             in_=src), f"dmp{hc % 2}", 16)
                        inc(sy.dma_start(out=RSTG.ap()[0:1, (hc % 2) * 1024:(hc % 2) * 1024 + 1024],
                                         in_=s1_rows[hc:hc + 1, :]), f"dmp{hc % 2}", 16)
                    sy.wait_ge(S("fsr"), b_sr + 31)
                    src = fc2q[0:128, 0:256]
                    inc(sy.dma_start(out=view(WS, 4, 256)[:, 0].bitcast(I32),
                                     in_=src), "dmq0", 16)
                    inc(sy.dma_start(out=RSTG.ap()[0:1, 2048:2304],
                                     in_=s2_rows[0:1, 0:256]), "dmq0", 16)

                @blk.vector
                def _(v):
                    for i in range(33):
                        if i < 32:
                            hc = i
                            v.wait_ge(S(f"dmp{hc % 2}"), b_p[hc % 2] + 32 * (hc // 2 + 1))
                            if hc >= 2:
                                v.wait_ge(S("fmm"), b_mm + 2 * (hc - 1))
                            v.tensor_scalar_add(out=view(WB, 2, DC, 128)[:, hc % 2],
                                                in0=view(WS, 2, DC, 128)[:, hc % 2].bitcast(I32),
                                                scalar1=-128.0)
                            v.tensor_copy(out=BROW.ap()[0:1, (hc % 2) * 1024:(hc % 2) * 1024 + 1024],
                                          in_=RSTG.ap()[0:1, (hc % 2) * 1024:(hc % 2) * 1024 + 1024])
                            inc(v.drain(), "fsr", 1)
                        if i >= 1:
                            hc = i - 1
                            v.wait_ge(S("fgt"), b_gt + hc + 1)
                            wbd = view(WB, 2, 1024)[:, hc % 2]
                            v.tensor_mul(out=wbd, in0=wbd, in1=psw(6, 2)[:, 0:1024])
                            inc(v.drain(), "fwv", 1)

                @blk.tensor
                def _(t):
                    for hc in range(32):
                        t.wait_ge(S("fsr"), b_sr + hc + 1)
                        if hc >= 1:
                            t.wait_ge(S("fwv"), b_wv + hc)
                        srow = BROW.ap()[0:1, (hc % 2) * 1024:(hc % 2) * 1024 + 1024]
                        for bb in range(2):
                            t.matmul(PS[6 + bb].ap()[:, 0:512],
                                     CONC.ap()[0:1, :], srow[:, bb * 512:(bb + 1) * 512],
                                     start=True, stop=True)
                        inc(t.drain(), "fgt", 1)
                        t.wait_ge(S("fwv"), b_wv + hc + 1)
                        if hc >= 2:
                            t.wait_ge(S("fsl"), b_sl + 2 * (hc - 1))
                        last = None
                        for tb in range(2):
                            for kc in range(DC):
                                last = t.matmul(PS[2 * (hc % 2) + tb].ap()[:, 0:512],
                                                view(WB, 2, DC, 128)[:, hc % 2, kc, :],
                                                vh[:, kc, tb * 512:(tb + 1) * 512],
                                                start=(kc == 0), stop=(kc == DC - 1))
                            inc(last, "fmm", 1)

                @blk.scalar
                def _(s):
                    for hc in range(32):
                        for tb in range(2):
                            s.wait_ge(S("fmm"), b_mm + 2 * hc + tb + 1)
                            s.activation(out=u_ap(hc)[:, tb * 512:(tb + 1) * 512],
                                         in_=PS[2 * (hc % 2) + tb].ap()[:, 0:512], func=AF.Silu,
                                         bias=sm("fb1", 32)[:, hc:hc + 1])
                            inc(s.drain(), "fsl", 1)

            for ofq in range(4):
                ocs0 = ofq * 256
                b_wv = val("fwv"); b_mm = val("fmm")
                b_sr = val("fsr"); b_gt = val("fgt"); b_sl = val("fsl")
                b_pe = val("fpe")
                b_q = [val("dmq0") - 32] + [val(f"dmq{j}") for j in range(1, 4)]
                with nc.Block("l1f_fc2") as blk:
                    @blk.sync
                    def _(sy, ofq=ofq):
                        for kc in range(1, 32):
                            if kc >= 4:
                                sy.wait_ge(S("fwv"), b_wv + kc - 3)
                            src = fc2q[kc * 128:(kc + 1) * 128, ocs0:ocs0 + 256]
                            inc(sy.dma_start(out=view(WS, 4, 256)[:, kc % 4].bitcast(I32),
                                             in_=src), f"dmq{kc % 4}", 16)
                            inc(sy.dma_start(out=RSTG.ap()[0:1, 2048 + (kc % 4) * 256:2048 + (kc % 4) * 256 + 256],
                                             in_=s2_rows[kc:kc + 1, ocs0:ocs0 + 256]),
                                f"dmq{kc % 4}", 16)
                        if ofq < 3:
                            sy.wait_ge(S("fsr"), b_sr + 29)
                            nx0 = (ofq + 1) * 256
                            src = fc2q[0:128, nx0:nx0 + 256]
                            inc(sy.dma_start(out=view(WS, 4, 256)[:, 0].bitcast(I32),
                                             in_=src), "dmq0", 16)
                            inc(sy.dma_start(out=RSTG.ap()[0:1, 2048:2304],
                                             in_=s2_rows[0:1, nx0:nx0 + 256]), "dmq0", 16)

                    @blk.vector
                    def _(v):
                        for i in range(33):
                            if i < 32:
                                kc = i
                                v.wait_ge(S(f"dmq{kc % 4}"), b_q[kc % 4] + 32 * (kc // 4 + 1))
                                if kc >= 4:
                                    v.wait_ge(S("fpe"), b_pe + kc - 3)
                                v.tensor_scalar_add(out=view(WB, 4, 256, off=4096)[:, kc % 4],
                                                    in0=view(WS, 4, 256)[:, kc % 4].bitcast(I32),
                                                    scalar1=-128.0)
                                v.tensor_copy(out=BROW.ap()[0:1, (kc % 4) * 256:(kc % 4) * 256 + 256],
                                              in_=RSTG.ap()[0:1, 2048 + (kc % 4) * 256:2048 + (kc % 4) * 256 + 256])
                                inc(v.drain(), "fsr", 1)
                            if i >= 1:
                                kc = i - 1
                                v.wait_ge(S("fgt"), b_gt + kc + 1)
                                wbd = view(WB, 4, 256, off=4096)[:, kc % 4]
                                v.tensor_mul(out=wbd, in0=wbd,
                                             in1=PS[6 + (kc % 4) // 2].ap()[:, (kc % 2) * 256:(kc % 2) * 256 + 256])
                                inc(v.drain(), "fwv", 1)

                    @blk.tensor
                    def _(t):
                        for kc in range(32):
                            t.wait_ge(S("fsr"), b_sr + kc + 1)
                            if kc >= 4:
                                t.wait_ge(S("fwv"), b_wv + kc - 3)
                            srow = BROW.ap()[0:1, (kc % 4) * 256:(kc % 4) * 256 + 256]
                            t.matmul(PS[6 + (kc % 4) // 2].ap()[:, (kc % 2) * 256:(kc % 2) * 256 + 256],
                                     CONC.ap()[0:1, :], srow, start=True, stop=True)
                            inc(t.drain(), "fgt", 1)
                            t.wait_ge(S("fwv"), b_wv + kc + 1)
                            last = None
                            for jj2 in range(2):
                                for tb in range(2):
                                    last = t.matmul(PS[2 * jj2 + tb].ap()[:, 0:512],
                                                    view(WB, 4, 256, off=4096)[:, kc % 4, jj2 * 128:(jj2 + 1) * 128],
                                                    u_ap(kc)[:, tb * 512:(tb + 1) * 512],
                                                    start=(kc == 0), stop=(kc == 31))
                            inc(last, "fpe", 1)
                        inc(t.drain(), "fmm", 1)

                    @blk.scalar
                    def _(s, ofq=ofq):
                        s.wait_ge(S("fmm"), b_mm + 1)
                        ei = None
                        for jj2 in range(2):
                            ofc = ofq * 2 + jj2
                            for tb in range(2):
                                ei = s.activation(out=SB2.ap()[:, (2 * jj2 + tb) * 512:(2 * jj2 + tb + 1) * 512],
                                                  in_=PS[2 * jj2 + tb].ap()[:, 0:512],
                                                  func=AF.Identity, scale=0.5,
                                                  bias=sm("hfb2", 8)[:, ofc:ofc + 1])
                        inc(s.drain(), "fsl", 1)

                    @blk.vector
                    def _(v, ofq=ofq):
                        v.wait_ge(S("fsl"), b_sl + 1)
                        ei = None
                        for jj2 in range(2):
                            ofc = ofq * 2 + jj2
                            for tb in range(2):
                                dst = XRES.ap()[:, ofc, t0 + tb * 512:t0 + (tb + 1) * 512]
                                ei = v.tensor_add(out=dst,
                                                  in0=SB2.ap()[:, (2 * jj2 + tb) * 512:(2 * jj2 + tb + 1) * 512],
                                                  in1=dst)
                        inc(ei, "fev", 1)

            if half == 0:
                with nc.Block("sthalf") as blk:
                    @blk.sync
                    def _(sy):
                        yv = y_out.rearrange("(c p) t -> p c t", p=P)
                        inc(sy.dma_start(out=yv[:, :, 0:TH],
                                         in_=XRES.ap()[:, :, 0:TH]), "dm", 16)

    seq = [("L0A", l0_attn), ("L0F", l0_ffn), ("L1A", l1_attn), ("L1F", l1_ffn)]
    try:
        for name, fn in seq:
            fn()
            if stop_after == name:
                break
    except StopBuild:
        return nc

    with nc.Block("store") as blk:
        @blk.sync
        def _(sy):
            yv = y_out.rearrange("(c p) t -> p c t", p=P)
            inc(sy.dma_start(out=yv[:, :, TH:2 * TH],
                             in_=XRES.ap()[:, :, TH:2 * TH]), "dm", 16)
            sy.wait_ge(S("dm"), val("dm"))
            sy.wait_ge(S("dmx"), 16)

    return nc


def _prep_weights(inp):
    C = np.ascontiguousarray
    d = {}
    d["wq"] = C(inp["L0_attn_Wq"].T)
    d["wk"] = C(inp["L0_attn_Wk"].T)
    d["wv"] = C(inp["L0_attn_Wv"].T)
    d["wo"] = C(inp["L0_attn_Wo"].T)
    d["wg1"] = C(inp["L0_attn_Wg1"].T)
    d["wg2"] = C(inp["L0_attn_Wg2"].T)
    d["gn_col"] = C(inp["L0_attn_g_norm_w"].reshape(8, 128).T)
    d["w0g"] = C(inp["L0_ffn_w_gate"].T)
    d["w0u"] = C(inp["L0_ffn_w_up"].T)
    d["w0d"] = C(inp["L0_ffn_w_down"].T)
    d["pl"] = C(inp["L1_attn_proj_local"])
    d["pg"] = C(inp["L1_attn_proj_global"])
    d["wlin"] = C(inp["L1_attn_Wl_in"].T)
    d["wgin"] = C(inp["L1_attn_Wg_in"].T)
    d["wlout"] = C(inp["L1_attn_Wl_out"].T)
    d["wgout"] = C(inp["L1_attn_Wg_out"].T)
    bl = inp["L1_attn_bl_in"]; bg = inp["L1_attn_bg_in"]
    d["blq_col"] = C(bl[0:1024].reshape(8, 128).T)
    d["blk_col"] = C(bl[1024:2048].reshape(8, 128).T)
    d["blv_row"] = C(bl[2048:3072].reshape(1, 1024))
    d["bgq_col"] = C(bg[0:1024].reshape(8, 128).T)
    d["bgk_col"] = C(bg[1024:2048].reshape(8, 128).T)
    d["bgv_row"] = C(bg[2048:3072].reshape(1, 1024))
    d["blout_row"] = C(inp["L1_attn_bl_out"].reshape(1, 1024))
    d["bgout_row"] = C(inp["L1_attn_bg_out"].reshape(1, 1024))
    d["fusw"] = C(inp["L1_attn_fusion_w"].reshape(1, 2))
    d["g1_col"] = C(inp["L1_attn_ln_gamma"].reshape(8, 128).T)
    d["b1_col"] = C(inp["L1_attn_ln_beta"].reshape(8, 128).T)
    d["g2_col"] = C(inp["L1_ffn_ln_gamma"].reshape(8, 128).T)
    d["b2_col"] = C(inp["L1_ffn_ln_beta"].reshape(8, 128).T)
    d["fc1q"] = C(inp["L1_ffn_fc1_q"])
    d["fc2q"] = C(inp["L1_ffn_fc2_q"])
    s1 = inp["L1_ffn_fc1_scales"]
    d["s1_rows"] = C(np.stack([s1[:, hc * 128:(hc + 1) * 128].ravel()
                               for hc in range(32)]))
    d["s2_rows"] = C(inp["L1_ffn_fc2_scales"])
    d["fb1_col"] = C(inp["L1_ffn_fc1_bias"].reshape(32, 128).T)
    d["fb2_row"] = C(inp["L1_ffn_fc2_bias"].reshape(1, 1024))
    d["fb2_col"] = C(inp["L1_ffn_fc2_bias"].reshape(8, 128).T)
    return d


def kernel(_stop_after=None, _cores=None, **inputs):
    x = np.asarray(inputs["L0_attn_hidden_states"], dtype=np.float32)
    wd = _prep_weights({k: np.asarray(v) for k, v in inputs.items()})
    nc = build_program(stop_after=_stop_after)
    ncores = NCORES if _cores is None else _cores
    in_maps = []
    for b in range(ncores):
        m = dict(wd)
        m["x"] = np.ascontiguousarray(x[b].T)
        in_maps.append(m)
    res = run_bass_kernel_spmd(nc, in_maps, core_ids=list(range(ncores)))
    out = np.stack([np.asarray(r["y"], dtype=np.float32).T for r in res.results])
    if ncores < NCORES:
        pad = np.zeros((NCORES - ncores, T, 1024), np.float32)
        out = np.concatenate([out, pad], axis=0)
    return out


# version 20 (final)
# speedup vs baseline: 1.0079x; 1.0014x over previous
import numpy as np
import ml_dtypes
import concourse.bass as bass
import concourse.mybir as mybir
from concourse.bass_utils import run_bass_kernel_spmd

F32 = mybir.dt.float32
BF16 = mybir.dt.bfloat16
I32 = mybir.dt.int32
AF = mybir.ActivationFunctionType
ALU = mybir.AluOpType
AX = mybir.AxisListType

P = 128
T = 2048
TH = 1024
DC = 8
NCORES = 8
LN_EPS = 1e-5
QS = 1.0 / 127.0
BF = ml_dtypes.bfloat16


def _ceil(a, b):
    return (a + b - 1) // b


def view(buf, *shape, off=0):
    n = int(np.prod(shape))
    ap = buf.ap()[:, off:off + n]
    if len(shape) == 1:
        return ap
    pat = "p (" + " ".join(f"d{i}" for i in range(len(shape))) + ") -> p " + \
          " ".join(f"d{i}" for i in range(len(shape)))
    return ap.rearrange(pat, **{f"d{i}": s for i, s in enumerate(shape)})


class StopBuild(Exception):
    pass


def build_program(stop_after=None, dbg=None):
    nc = bass.Bass("TRN2", target_bir_lowering=False)

    x_in = nc.dram_tensor("x", [1024, T], F32, kind="ExternalInput").ap()
    y_out = nc.dram_tensor("y", [1024, T], F32, kind="ExternalOutput").ap()

    def din(name, shape, dt=F32):
        return nc.dram_tensor(name, shape, dt, kind="ExternalInput").ap()

    wq = din("wq", [1024, 1024]); wk = din("wk", [1024, 1024])
    wv = din("wv", [1024, 1024]); wo = din("wo", [1024, 1024])
    wg1 = din("wg1", [1024, 128]); wg2 = din("wg2", [128, 1024])
    gn_col = din("gn_col", [128, 8])
    w0g = din("w0g", [1024, 3072], I32)
    w0u = din("w0u", [1024, 3072], I32)
    w0d = din("w0d", [3072, 1024], I32)
    pl = din("pl", [1024, 1024]); pg = din("pg", [1024, 1024])
    wlin = din("wlin", [1024, 3072]); wgin = din("wgin", [1024, 3072])
    wlout = din("wlout", [1024, 1024]); wgout = din("wgout", [1024, 1024])
    blq_col = din("blq_col", [128, 8]); blk_col = din("blk_col", [128, 8])
    bgq_col = din("bgq_col", [128, 8]); bgk_col = din("bgk_col", [128, 8])
    blv_row = din("blv_row", [1, 1024]); bgv_row = din("bgv_row", [1, 1024])
    blout_row = din("blout_row", [1, 1024]); bgout_row = din("bgout_row", [1, 1024])
    fusw = din("fusw", [1, 2])
    g1_col = din("g1_col", [128, 8]); b1_col = din("b1_col", [128, 8])
    g2_col = din("g2_col", [128, 8]); b2_col = din("b2_col", [128, 8])
    fc1q = din("fc1q", [1024, 4096], I32)
    fc2q = din("fc2q", [4096, 1024], I32)
    s1_rows = din("s1_rows", [32, 1024]); s2_rows = din("s2_rows", [32, 1024])
    fb1_col = din("fb1_col", [128, 32]); fb2_row = din("fb2_row", [1, 1024])
    fb2_col = din("fb2_col", [128, 8])

    jj, ii = np.meshgrid(np.arange(P), np.arange(P), indexing="ij")
    ident_h = nc.inline_tensor(np.eye(P, dtype=BF), name="c_ident").ap()
    gmask_h = nc.inline_tensor(np.tile((ii >= jj).astype(BF), (1, 4)), name="c_gmask").ap()
    l1m_h = nc.inline_tensor((jj > ii).astype(BF), name="c_l1m").ap()
    onesr_h = nc.inline_tensor(np.ones((1, 512), dtype=BF), name="c_onesr").ap()
    onesc_h = nc.inline_tensor(np.ones((1, 128), dtype=BF), name="c_onesc").ap()
    sumc_h = nc.inline_tensor(np.ones((128, 1), dtype=BF), name="c_sumc").ap()
    _cb = np.zeros((128, 128), dtype=BF); _cb[64, :] = 1.0
    conb_h = nc.inline_tensor(_cb, name="c_conb").ap()
    _fconsts = [LN_EPS, -0.5, 1.0 / 1024.0, QS, 0.125, 1.0, 0.0]
    fcon_h = nc.inline_tensor(np.tile(np.asarray(_fconsts, np.float32), (P, 1)),
                              name="c_fcon").ap()

    A = nc.alloc_sbuf_tensor
    XRES = A("XRES", [P, DC, T], F32)
    BN = 8320
    B1 = A("B1", [P, BN], BF16)
    B2 = A("B2", [P, BN], BF16)
    B3 = A("B3", [P, BN], BF16)
    B4 = A("B4", [P, BN], BF16)
    B5 = A("B5", [P, BN], BF16)
    WS = A("WS", [P, 4096], F32)
    WB = A("WB", [P, 8192], BF16)
    SGLA = A("SGLA", [P, DC, 128], F32)
    SM = A("SM", [P, 640], F32)
    SB2 = A("SB2", [P, 2048], BF16)
    BROW = A("BROW", [1, 2048], BF16)
    RSTG = A("RSTG", [1, 3072], F32)
    CID = A("CID", [P, 128], BF16)
    CGM = A("CGM", [P, 512], BF16)
    CL1 = A("CL1", [P, 128], BF16)
    CONR = A("CONR", [1, 512], BF16)
    CONC = A("CONC", [1, 128], BF16)
    CSUM = A("CSUM", [P, 1], BF16)
    CONB = A("CONB", [P, 128], BF16)
    CFC = A("CFC", [P, len(_fconsts)], F32)
    for _j, _v in enumerate(_fconsts):
        nc.const_aps.register(F32, float(_v), CFC.ap()[:, _j:_j + 1])

    PSALL = nc.alloc_psum_tensor("psall", [P, 4096], F32)

    class _PSView:
        def __init__(self, i):
            self.i = i

        def ap(self):
            return PSALL.ap()[:, self.i * 512:(self.i + 1) * 512]

    PS = [_PSView(i) for i in range(8)]

    def psb(i):
        return PS[i].ap().bitcast(BF16)

    def psw(i, n):
        return PSALL.ap()[:, i * 512:(i + n) * 512]

    _blkn = [0]
    _origBlock = nc.Block

    def _Block(tag):
        _blkn[0] += 1
        return _origBlock(f"{tag}{_blkn[0]}")
    nc.Block = _Block

    sems = {}
    cnt = {}

    def S(name):
        if name not in sems:
            sems[name] = nc.alloc_semaphore(name)
            cnt[name] = 0
        return sems[name]

    def inc(instr, name, k=1):
        instr.then_inc(S(name), k)
        cnt[name] += k
        return cnt[name]

    def val(name):
        S(name)
        return cnt[name]

    smoff = {}
    _smn = [0]

    def sm(name, width=8):
        if name not in smoff:
            smoff[name] = _smn[0]
            _smn[0] += width
            assert _smn[0] <= 640, "SM overflow"
        return SM.ap()[:, smoff[name]:smoff[name] + width]

    broff = {}
    _brn = [0]

    def brow(name, width=1024):
        if name not in broff:
            broff[name] = _brn[0]
            _brn[0] += width
            assert _brn[0] <= 2048, "BROW overflow"
        return BROW.ap()[:, broff[name]:broff[name] + width]

    with nc.Block("init") as blk:
        @blk.sync
        def _(sy):
            xr = x_in.rearrange("(c p) t -> p c t", p=P)
            inc(sy.dma_start(out=XRES.ap()[:, :, 0:TH], in_=xr[:, :, 0:TH]), "dm", 16)
            inc(sy.dma_start(out=XRES.ap()[:, :, TH:2 * TH], in_=xr[:, :, TH:2 * TH]),
                "dmx", 16)
            for dst, src in [(CID, ident_h), (CGM, gmask_h), (CL1, l1m_h),
                             (CONR, onesr_h), (CONC, onesc_h), (CSUM, sumc_h),
                             (CFC, fcon_h), (CONB, conb_h)]:
                inc(sy.dma_start(out=dst.ap(), in_=src), "dm", 16)
            for name, h in [("gn", gn_col), ("blq", blq_col), ("blk", blk_col),
                            ("bgq", bgq_col), ("bgk", bgk_col), ("g1", g1_col),
                            ("b1", b1_col), ("g2", g2_col), ("b2", b2_col)]:
                inc(sy.dma_start(out=sm(name, 8), in_=h), "dm", 16)
            inc(sy.dma_start(out=sm("fb1", 32), in_=fb1_col), "dm", 16)
            inc(sy.dma_start(out=sm("fb2c", 8), in_=fb2_col), "dm", 16)

        @blk.vector
        def _(v):
            v.wait_ge(S("dm"), val("dm"))
            v.memset(SGLA.ap(), 0.0)
            v.tensor_scalar_mul(out=sm("blq8"), in0=sm("blq"), scalar1=0.125)
            v.tensor_scalar_mul(out=sm("bgq8"), in0=sm("bgq"), scalar1=0.125)
            v.tensor_scalar_mul(out=sm("hfb2", 8), in0=sm("fb2c", 8), scalar1=0.5)

    def ensure_row(slot, hbm_row):
        with nc.Block("rowcvt") as blk:
            @blk.sync
            def _(sy):
                inc(sy.dma_start(out=RSTG.ap()[0:1, 0:1024], in_=hbm_row), "dm", 16)

            @blk.vector
            def _(v):
                v.wait_ge(S("dm"), val("dm"))
                v.tensor_copy(out=brow(slot), in_=RSTG.ap()[0:1, 0:1024])

    def dump(tag, buf, f32=False):
        if dbg != tag:
            return
        with nc.Block("dbg") as blk:
            @blk.sync
            def _(sy):
                yv = y_out.rearrange("(a b) t -> a (b t)", a=128)
                src = buf.ap() if f32 else buf.ap().bitcast(F32)
                n = src.shape[1] if len(src.shape) == 2 else int(np.prod(src.shape[1:]))
                srcf = src if len(src.shape) == 2 else src.rearrange("p a b -> p (a b)")
                inc(sy.dma_start(out=yv[:, 0:n], in_=srcf), "dm", 16)
                sy.wait_ge(S("dm"), val("dm"))
        raise StopBuild

    def ln_1024(src_f32, src_bf, out_view, scr1, scr2, gamma=None, beta=None,
                pf=None, xw=None):
        srcx = src_f32 if src_f32 is not None else src_bf
        with nc.Block("ln") as blk:
            if pf is not None:
                @blk.sync
                def _(sy):
                    if callable(pf):
                        pf(sy)
                    else:
                        pf_emit(sy, *pf)

            @blk.vector
            def _(v):
                v.wait_ge(S("dm"), val("dm"))
                if xw is not None:
                    for nm, tg in xw:
                        v.wait_ge(S(nm), tg)
                if src_f32 is not None:
                    v.tensor_copy(out=scr1, in_=src_f32)
                inc(v.drain(), "lnv", 1)

            @blk.scalar
            def _(s):
                if xw is not None:
                    for nm, tg in xw:
                        s.wait_ge(S(nm), tg)
                s.activation(out=scr2, in_=srcx, func=AF.Square)
                inc(s.drain(), "lns", 1)

            @blk.tensor
            def _(t):
                t.wait_ge(S("lnv"), val("lnv"))
                t.wait_ge(S("lns"), val("lns"))
                xb = scr1 if src_f32 is not None else src_bf
                for b in range(2):
                    c0, c1 = b * 512, (b + 1) * 512
                    for c in range(DC):
                        t.matmul(PS[0 + b].ap()[0:1, 0:512], CSUM.ap(),
                                 xb[:, c, c0:c1], start=(c == 0), stop=(c == DC - 1))
                    for c in range(DC):
                        t.matmul(PS[2 + b].ap()[0:1, 0:512], CSUM.ap(),
                                 scr2[:, c, c0:c1], start=(c == 0), stop=(c == DC - 1))
                inc(t.drain(), "lnt", 1)

            @blk.vector
            def _(v):
                v.wait_ge(S("lnt"), val("lnt"))
                mu = RSTG.ap()[0:1, 1024:2048]
                var = RSTG.ap()[0:1, 2048:3072]
                for b in range(2):
                    bs = slice(b * 512, (b + 1) * 512)
                    v.tensor_scalar_mul(out=mu[:, bs], in0=PS[0 + b].ap()[0:1, 0:512],
                                        scalar1=1.0 / 1024.0)
                v.drain()
                for b in range(2):
                    bs = slice(b * 512, (b + 1) * 512)
                    v.tensor_mul(out=var[:, bs], in0=mu[:, bs], in1=mu[:, bs])
                    v.drain()
                    v.scalar_tensor_tensor(out=var[:, bs], in0=PS[2 + b].ap()[0:1, 0:512],
                                           scalar=1.0 / 1024.0, in1=var[:, bs],
                                           op0=ALU.mult, op1=ALU.subtract)
                inc(v.drain(), "lnv", 1)

            @blk.scalar
            def _(s):
                s.wait_ge(S("lnv"), val("lnv"))
                rstd = RSTG.ap()[0:1, 2048:3072]
                s.activation(out=rstd, in_=rstd, func=AF.Ln, bias=LN_EPS)
                s.drain()
                s.activation(out=rstd, in_=rstd, func=AF.Exp, scale=-0.5)
                inc(s.drain(), "lns", 1)

            @blk.vector
            def _(v):
                v.wait_ge(S("lns"), val("lns"))
                mu = RSTG.ap()[0:1, 1024:2048]
                rstd = RSTG.ap()[0:1, 2048:3072]
                arow = RSTG.ap().bitcast(BF16)[0:1, 0:1024]
                brw = RSTG.ap().bitcast(BF16)[0:1, 1024:2048]
                v.tensor_copy(out=arow, in_=rstd)
                v.scalar_tensor_tensor(out=brw, in0=mu, scalar=-1.0, in1=rstd,
                                       op0=ALU.mult, op1=ALU.mult)
                inc(v.drain(), "lnv", 1)

            @blk.tensor
            def _(t):
                t.wait_ge(S("lnv"), val("lnv"))
                arow = RSTG.ap().bitcast(BF16)[0:1, 0:1024]
                brw = RSTG.ap().bitcast(BF16)[0:1, 1024:2048]
                for b in range(2):
                    bs = slice(b * 512, (b + 1) * 512)
                    t.matmul(PS[4 + b].ap()[:, 0:512], CONC.ap()[0:1, :],
                             arow[:, bs], start=True, stop=True)
                    t.matmul(PS[6 + b].ap()[:, 0:512], CONC.ap()[0:1, :],
                             brw[:, bs], start=True, stop=True)
                inc(t.drain(), "lnt", 1)

            @blk.vector
            def _(v):
                v.wait_ge(S("lnt"), val("lnt"))
                for b in range(2):
                    c0, c1 = b * 512, (b + 1) * 512
                    abc = PS[4 + b].ap()[:, 0:512].unsqueeze(1).broadcast_to([P, DC, 512])
                    bbc = PS[6 + b].ap()[:, 0:512].unsqueeze(1).broadcast_to([P, DC, 512])
                    v.tensor_mul(out=scr2[:, :, c0:c1], in0=srcx[:, :, c0:c1], in1=abc)
                    v.drain()
                    v.tensor_add(out=out_view[:, :, c0:c1], in0=scr2[:, :, c0:c1], in1=bbc)
                if gamma is not None:
                    v.drain()
                    for c in range(DC):
                        v.tensor_scalar(out=out_view[:, c, :], in0=out_view[:, c, :],
                                        scalar1=gamma[:, c:c + 1], scalar2=beta[:, c:c + 1],
                                        op0=ALU.mult, op1=ALU.add)
                inc(v.drain(), "lnv", 1)

    def load_w_fp32(w_hbm, of0, nofe, wb_off=0, scale_cols=None, kc_n=DC):
        per = 2048 // kc_n
        o = 0
        while o < nofe:
            cur = min(per, nofe - o)
            with nc.Block("ldw") as blk:
                @blk.sync
                def _(sy, o=o, cur=cur):
                    src = w_hbm[:, of0 + o:of0 + o + cur].rearrange(
                        "(kc p) of -> p kc of", p=P)
                    inc(sy.dma_start(out=view(WS, kc_n, cur), in_=src), "dm", 16)

                @blk.vector
                def _(v, o=o, cur=cur):
                    v.wait_ge(S("dm"), val("dm"))
                    dst = view(WB, kc_n, nofe, off=wb_off)
                    if scale_cols is None:
                        v.tensor_copy(out=dst[:, :, o:o + cur], in_=view(WS, kc_n, cur))
                    else:
                        for kc in range(kc_n):
                            v.tensor_scalar_mul(out=dst[:, kc, o:o + cur],
                                                in0=view(WS, kc_n, cur)[:, kc, :],
                                                scalar1=scale_cols[:, kc:kc + 1])
                    inc(v.drain(), "wv", 1)
            o += cur

    def n_units(nofe, kc_n):
        return _ceil(nofe, 2048 // kc_n)

    def pf_emit(sy, w_hbm, nofe, kc_n, war=None):
        unit_ofe = 2048 // kc_n
        nu = min(2, n_units(nofe, kc_n))
        for u in range(nu):
            if war is not None and war[u] is not None:
                sy.wait_ge(S("pwv"), war[u])
            o0, o1 = u * unit_ofe, min(nofe, (u + 1) * unit_ofe)
            src = w_hbm[:, o0:o1].rearrange("(kc p) of -> p kc of", p=P)
            inc(sy.dma_start(out=view(WS, 2, kc_n, unit_ofe)[:, u, :, 0:o1 - o0],
                             in_=src), f"dmp{u}", 16)
        return nu

    def fm_proj(src_bf, nt, nof, evac, evac_eng="v", bias_row=None, kc_n=DC,
                wbv=None, of_base=0):
        wbv_ = wbv if wbv is not None else view(WB, kc_n, nof * 128)
        tiles = [(ofc, tb) for ofc in range(nof) for tb in range(_ceil(nt, 512))]
        mm_b = val("pmm"); ev_b = val("pev")
        with nc.Block("proj") as blk:
            @blk.tensor
            def _(t):
                t.wait_ge(S("wv"), val("wv"))
                t.wait_ge(S("dm"), val("dm"))
                for i, (ofc, tb) in enumerate(tiles):
                    bank = i % 2
                    ts, te = tb * 512, min(nt, (tb + 1) * 512)
                    if i >= 2:
                        t.wait_ge(S("pev"), ev_b + i - 1)
                    last = None
                    for kc in range(kc_n):
                        last = t.matmul(PS[bank].ap()[:, 0:te - ts],
                                        wbv_[:, kc, ofc * 128:(ofc + 1) * 128],
                                        src_bf[:, kc, ts:te], start=(kc == 0),
                                        stop=(kc == kc_n - 1 and bias_row is None))
                    if bias_row is not None:
                        last = t.matmul(PS[bank].ap()[:, 0:te - ts],
                                        bias_row[0:1, (of_base + ofc) * 128:(of_base + ofc + 1) * 128],
                                        CONR.ap()[0:1, 0:te - ts], start=False, stop=True)
                    inc(last, "pmm", 1)

            def emit_evac(eng):
                for i, (ofc, tb) in enumerate(tiles):
                    bank = i % 2
                    ts, te = tb * 512, min(nt, (tb + 1) * 512)
                    eng.wait_ge(S("pmm"), mm_b + i + 1)
                    ei = evac(eng, PS[bank].ap()[:, 0:te - ts], ofc, ts, te)
                    inc(ei, "pev", 1)

            if evac_eng == "v":
                @blk.vector
                def _(v):
                    emit_evac(v)
            else:
                @blk.scalar
                def _(s):
                    emit_evac(s)

    def proj_fp32(w_hbm, nofe, src_bf, nt, evac, evac_eng="v", bias_row=None,
                  kc_n=DC, of_base=0, scale_cols=None, pre=0, pf_next=None,
                  row_load=None):
        unit_ofe = 2048 // kc_n
        units = []
        o = 0
        while o < nofe:
            units.append((o, min(nofe, o + unit_ofe)))
            o = units[-1][1]
        ntb = _ceil(nt, 512)
        tcounts = []
        tot = 0
        for (o0, o1) in units:
            tot += ((o1 - o0) // 128) * ntb
            tcounts.append(tot)
        b_p = [val("dmp0") - (16 if pre >= 1 else 0),
               val("dmp1") - (16 if pre >= 2 else 0)]
        b_wv = val("pwv"); b_mm = val("pmm"); b_ev = val("pev")
        b_dmt = val("dm")
        with nc.Block("projp") as blk:
            @blk.sync
            def _(sy):
                if row_load is not None:
                    inc(sy.dma_start(out=RSTG.ap()[0:1, 0:1024], in_=row_load[1]),
                        "dmr", 16)
                for u, (o0, o1) in enumerate(units):
                    if u < pre:
                        continue
                    if u >= 2:
                        sy.wait_ge(S("pwv"), b_wv + u - 1)
                    src = w_hbm[:, o0:o1].rearrange("(kc p) of -> p kc of", p=P)
                    inc(sy.dma_start(out=view(WS, 2, kc_n, unit_ofe)[:, u % 2, :, 0:o1 - o0],
                                     in_=src), f"dmp{u % 2}", 16)
                if pf_next is not None:
                    U = len(units)
                    war = [b_wv + ((U - 1) // 2) * 2 + 1, None]
                    if U >= 2:
                        war[1] = b_wv + ((U - 2) // 2) * 2 + 2
                    pf_emit(sy, *pf_next, war=war)

            def unit_tiles(u):
                o0, o1 = units[u]
                base = tcounts[u - 1] if u > 0 else 0
                out = []
                i = base
                for ofr in range((o1 - o0) // 128):
                    for tb in range(ntb):
                        out.append((i, o0 // 128 + ofr, tb))
                        i += 1
                return out

            def emit_ev_unit(eng, u):
                for i, ofc, tb in unit_tiles(u):
                    bank = i % 8
                    ts, te = tb * 512, min(nt, (tb + 1) * 512)
                    eng.wait_ge(S("pmm"), b_mm + i + 1)
                    ei = evac(eng, PS[bank].ap()[:, 0:te - ts], ofc, ts, te)
                    inc(ei, "pev", 1)

            @blk.vector
            def _(v):
                if row_load is not None:
                    v.wait_ge(S("dmr"), val("dmr"))
                    v.tensor_copy(out=brow(row_load[0]), in_=RSTG.ap()[0:1, 0:1024])
                for step in range(len(units) + 1):
                    if step < len(units):
                        u = step
                        o0, o1 = units[u]
                        v.wait_ge(S(f"dmp{u % 2}"), b_p[u % 2] + 16 * (u // 2 + 1))
                        if u >= 2:
                            v.wait_ge(S("pmm"), b_mm + tcounts[u - 2])
                        dstw = view(WB, 2, kc_n, unit_ofe)[:, u % 2]
                        srcw = view(WS, 2, kc_n, unit_ofe)[:, u % 2]
                        if scale_cols is None:
                            v.tensor_copy(out=dstw[:, :, 0:o1 - o0], in_=srcw[:, :, 0:o1 - o0])
                        else:
                            for kc in range(kc_n):
                                v.tensor_scalar_mul(out=dstw[:, kc, 0:o1 - o0],
                                                    in0=srcw[:, kc, 0:o1 - o0],
                                                    scalar1=scale_cols[:, kc:kc + 1])
                        inc(v.drain(), "pwv", 1)
                    if evac_eng == "v" and step >= 1:
                        emit_ev_unit(v, step - 1)

            @blk.tensor
            def _(t):
                t.wait_ge(S("dm"), b_dmt)
                i = 0
                for u, (o0, o1) in enumerate(units):
                    t.wait_ge(S("pwv"), b_wv + u + 1)
                    for ofr in range((o1 - o0) // 128):
                        ofc = o0 // 128 + ofr
                        for tb in range(ntb):
                            bank = i % 8
                            ts, te = tb * 512, min(nt, (tb + 1) * 512)
                            if i >= 8:
                                t.wait_ge(S("pev"), b_ev + i - 7)
                            last = None
                            for kc in range(kc_n):
                                last = t.matmul(PS[bank].ap()[:, 0:te - ts],
                                                view(WB, 2, kc_n, unit_ofe)[:, u % 2, kc, ofr * 128:(ofr + 1) * 128],
                                                src_bf[:, kc, ts:te], start=(kc == 0),
                                                stop=(kc == kc_n - 1 and bias_row is None))
                            if bias_row is not None:
                                last = t.matmul(PS[bank].ap()[:, 0:te - ts],
                                                bias_row[0:1, (of_base + ofc) * 128:(of_base + ofc + 1) * 128],
                                                CONR.ap()[0:1, 0:te - ts], start=False, stop=True)
                            inc(last, "pmm", 1)
                            i += 1

            if evac_eng == "s":
                @blk.scalar
                def _(se):
                    for u in range(len(units)):
                        emit_ev_unit(se, u)

    def tm_proj(src_bf, vdst, bias_name, w_hbm, pre=0, pf_next=None):
        b_mm = val("pmm"); b_ev = val("pev"); b_wv = val("pwv")
        b_p = [val("dmp0") - (16 if pre >= 1 else 0),
               val("dmp1") - (16 if pre >= 2 else 0)]
        with nc.Block("tmproj") as blk:
            @blk.sync
            def _(sy):
                inc(sy.dma_start(out=RSTG.ap()[0:1, 0:1024], in_=bias_name), "dmr", 16)
                for u in range(4):
                    if u < pre:
                        continue
                    if u >= 2:
                        sy.wait_ge(S("pwv"), b_wv + u - 1)
                    src = w_hbm[:, u * 256:(u + 1) * 256].rearrange(
                        "(kc p) of -> p kc of", p=P)
                    inc(sy.dma_start(out=view(WS, 2, DC, 256)[:, u % 2],
                                     in_=src), f"dmp{u % 2}", 16)
                if pf_next is not None:
                    pf_emit(sy, *pf_next, war=[b_wv + 3, b_wv + 4])

            @blk.vector
            def _(v):
                v.wait_ge(S("dmr"), val("dmr"))
                v.tensor_copy(out=brow("rowA"), in_=RSTG.ap()[0:1, 0:1024])
                for u in range(4):
                    v.wait_ge(S(f"dmp{u % 2}"), b_p[u % 2] + 16 * (u // 2 + 1))
                    v.tensor_copy(out=view(WB, DC, 1024)[:, :, u * 256:(u + 1) * 256],
                                  in_=view(WS, 2, DC, 256)[:, u % 2])
                    inc(v.drain(), "pwv", 1)
                i = 0
                for ofb in range(2):
                    for tt in range(8):
                        bank = i % 4
                        v.wait_ge(S("pmm"), b_mm + i + 1)
                        ei = None
                        for hh in range(8):
                            h = ofb * 8 + hh
                            ei = v.tensor_copy(out=vdst[:, tt, h, 0:64],
                                               in_=PS[bank].ap()[:, hh * 64:(hh + 1) * 64])
                        inc(ei, "pev", 1)
                        i += 1

            @blk.tensor
            def _(t):
                t.wait_ge(S("gv"), val("gv"))
                t.wait_ge(S("dm"), val("dm"))
                i = 0
                for ofb in range(2):
                    t.wait_ge(S("pwv"), b_wv + 2 * (ofb + 1))
                    for tt in range(8):
                        bank = i % 4
                        if i >= 4:
                            t.wait_ge(S("pev"), b_ev + i - 3)
                        for kc in range(DC):
                            t.matmul(PS[bank].ap()[:, 0:512],
                                     src_bf[:, kc, tt * 128:(tt + 1) * 128],
                                     view(WB, DC, 1024)[:, kc, ofb * 512:(ofb + 1) * 512],
                                     start=(kc == 0), stop=False)
                        last = t.matmul(PS[bank].ap()[:, 0:512], CONC.ap()[0:1, :],
                                        brow("rowA")[0:1, ofb * 512:(ofb + 1) * 512],
                                        start=False, stop=True)
                        inc(last, "pmm", 1)
                        i += 1

    def l0_attn():
        vq = view(B2, DC, TH)
        vE = view(B3, DC, TH)
        vvt = view(B4, DC, TH)
        vgs = view(B5, DC, TH)
        vP = view(B1, DC, TH)
        for half in range(2):
            t0 = half * TH
            xsl = XRES.ap()[:, :, t0:t0 + TH]
            ln_1024(xsl, None, vP, view(B2, DC, TH), view(B3, DC, TH),
                    pf=(wq, 1024, DC), xw=[("dmx", 16)] if half == 1 else None)
            dump("ln0", B1)

            def ev_q(e, ps, ofc, ts, te):
                return e.activation(out=vq[:, ofc, ts:te], in_=ps, func=AF.Silu)
            proj_fp32(wq, 1024, vP, TH, ev_q, evac_eng="s", pre=2,
                      pf_next=(wg1, 128, DC))
            dump("q", B2)

            def ev_g1(e, ps, ofc, ts, te):
                return e.tensor_copy(out=SB2.ap()[:, ts:te], in_=ps)
            proj_fp32(wg1, 128, vP, TH, ev_g1, evac_eng="v", pre=1,
                      pf_next=(wg2, 1024, 1))

            def ev_g2(e, ps, ofc, ts, te):
                return e.activation(out=vgs[:, ofc, ts:te], in_=ps, func=AF.Silu)
            g1v = SB2.ap()[:, 0:TH].rearrange("p (a t) -> p a t", a=1)
            proj_fp32(wg2, 1024, g1v, TH, ev_g2, evac_eng="s", kc_n=1, pre=1,
                      pf_next=(wk, 1024, DC))

            def ev_k(e, ps, ofc, ts, te):
                return e.activation(out=vE[:, ofc, ts:te], in_=ps, func=AF.Exp)
            proj_fp32(wk, 1024, vP, TH, ev_k, evac_eng="s", pre=2,
                      pf_next=(wv, 1024, DC))
            dump("E", B3)

            b_mm = val("pmm"); b_ev = val("pev"); b_wv = val("pwv")
            b_p = [val("dmp0") - 16, val("dmp1") - 16]
            with nc.Block("gla_v") as blk:
                @blk.sync
                def _(sy):
                    for u in range(2, 4):
                        sy.wait_ge(S("pwv"), b_wv + u - 1)
                        src = wv[:, u * 256:(u + 1) * 256].rearrange(
                            "(kc p) of -> p kc of", p=P)
                        inc(sy.dma_start(out=view(WS, 2, DC, 256)[:, u % 2],
                                         in_=src), f"dmp{u % 2}", 16)
                    pf_emit(sy, wo, 1024, DC, war=[b_wv + 3, b_wv + 4])

                @blk.vector
                def _(v):
                    for u in range(4):
                        v.wait_ge(S(f"dmp{u % 2}"), b_p[u % 2] + 16 * (u // 2 + 1))
                        v.tensor_copy(out=view(WB, DC, 1024)[:, :, u * 256:(u + 1) * 256],
                                      in_=view(WS, 2, DC, 256)[:, u % 2])
                        inc(v.drain(), "pwv", 1)
                    i = 0
                    for ofb in range(2):
                        for tt in range(8):
                            bank = i % 2
                            v.wait_ge(S("pmm"), b_mm + i + 1)
                            ei = v.tensor_copy(out=vvt[:, tt, ofb * 512:(ofb + 1) * 512],
                                               in_=PS[bank].ap()[:, 0:512])
                            inc(ei, "pev", 1)
                            i += 1

                @blk.tensor
                def _(t):
                    i = 0
                    for ofb in range(2):
                        t.wait_ge(S("pwv"), b_wv + 2 * (ofb + 1))
                        for tt in range(8):
                            bank = i % 2
                            if i >= 2:
                                t.wait_ge(S("pev"), b_ev + i - 1)
                            last = None
                            for kc in range(DC):
                                last = t.matmul(PS[bank].ap()[:, 0:512],
                                                vP[:, kc, tt * 128:(tt + 1) * 128],
                                                view(WB, DC, 1024)[:, kc, ofb * 512:(ofb + 1) * 512],
                                                start=(kc == 0), stop=(kc == DC - 1))
                            inc(last, "pmm", 1)
                            i += 1

            dump("v", B4)
            dump("gsw", B5)

            with nc.Block("gla_fac") as blk:
                @blk.vector
                def _(v, half=half):
                    v.wait_ge(S("pev"), val("pev"))
                    for h in range(DC):
                        init = 0.0 if half == 0 else sm("Pcar")[:, h:h + 1]
                        v.tensor_tensor_scan(out=vP[:, h, :], data0=vE[:, h, :],
                                             data1=vE[:, h, :], initial=init,
                                             op0=ALU.add, op1=ALU.bypass)
                    v.drain()
                    pbp4 = sm("Pbp", 64).rearrange("p (h n) -> p h n", h=8).unsqueeze(3)
                    pls4 = sm("Pls", 64).rearrange("p (h n) -> p h n", h=8).unsqueeze(3)
                    vP4c = view(B1, DC, 8, 128)
                    v.tensor_copy(out=pls4, in_=vP4c[:, :, :, 127:128])
                    v.tensor_copy(out=pbp4[:, :, 1:8, :], in_=vP4c[:, :, 0:7, 127:128])
                    if half == 0:
                        v.tensor_copy(out=pbp4[:, :, 0:1, :], in_=vP4c[:, :, 0:1, 0:1])
                    else:
                        v.tensor_copy(out=pbp4[:, :, 0:1, :],
                                      in_=sm("Pcar").unsqueeze(2).unsqueeze(3))
                    v.drain()
                    v.tensor_copy(out=sm("Pcar").unsqueeze(2).unsqueeze(3),
                                  in_=vP4c[:, :, 7:8, 127:128])
                    v.drain()
                    v.reciprocal(out=sm("rPbp", 64), in_=sm("Pbp", 64))
                    v.reciprocal(out=sm("rPls", 64), in_=sm("Pls", 64))
                    v.drain()
                    v.tensor_mul(out=sm("ebl", 64), in0=sm("Pbp", 64), in1=sm("rPls", 64))
                    inc(v.drain(), "gv", 1)

                @blk.scalar
                def _(s):
                    s.wait_ge(S("gv"), val("gv"))
                    s.activation(out=vP, in_=vP, func=AF.Ln)
                    s.activation(out=sm("lnPbp", 64), in_=sm("Pbp", 64), func=AF.Ln)
                    inc(s.drain(), "gs", 1)

                @blk.vector
                def _(v):
                    v.wait_ge(S("gs"), val("gs"))
                    lnpbp = sm("lnPbp", 64).rearrange("p (h n) -> p h n", h=8)
                    rep = lnpbp.unsqueeze(3).broadcast_to([P, 8, 8, 128])
                    vP4 = view(B1, DC, 8, 128)
                    v.tensor_sub(out=vP4, in0=rep, in1=vP4)
                    inc(v.drain(), "gv", 1)

                @blk.scalar
                def _(s):
                    s.wait_ge(S("gv"), val("gv"))
                    s.activation(out=vP, in_=vP, func=AF.Exp)
                    inc(s.drain(), "gs", 1)

                @blk.vector
                def _(v, half=half):
                    v.wait_ge(S("gs"), val("gs"))
                    v.tensor_mul(out=vq, in0=vq, in1=vP)
                    rpb = sm("rPbp", 64).rearrange("p (h n) -> p h n", h=8)
                    rep = rpb.unsqueeze(3).broadcast_to([P, 8, 8, 128])
                    vE4 = view(B3, DC, 8, 128)
                    v.tensor_mul(out=vE4, in0=vE4, in1=rep)
                    if half == 0:
                        v.memset(SB2.ap()[:, 1024:2048], 0.0)
                    inc(v.drain(), "gv", 1)

            dump("fac", B1)
            dump("q2", B2)
            dump("k2", B3)

            etm = SB2.ap()[:, 0:1024].rearrange("p (h d) -> p h d", h=8)
            vq4o = view(B2, DC, 8, 128)
            sbf = SB2.ap()[:, 1024:2048].rearrange("p (h d) -> p h d", h=8)
            amask = view(WB, 8, 128)
            ebl = sm("ebl", 64)
            b_mm = val("g_mm"); b_tr = val("g_tr"); b_ds = val("g_ds")
            b_om = val("g_om"); b_ae = val("g_ae"); b_te = val("g_te")
            b_su = val("g_su"); b_oe = val("g_oe")
            with nc.Block("gla_chunks") as blk:
                @blk.tensor
                def _(t):
                    t.wait_ge(S("gv"), val("gv"))
                    for n in range(8):
                        nsl = slice(n * 128, (n + 1) * 128)
                        t.wait_ge(S("g_ae"), b_ae + 2 * n)
                        last = None
                        for h in range(8):
                            last = t.matmul(PS[h // 4].ap()[:, (h % 4) * 128:(h % 4 + 1) * 128],
                                            vE[:, h, nsl], vq[:, h, nsl],
                                            start=True, stop=True)
                        inc(last, "g_mm", 1)
                        t.wait_ge(S("g_te"), b_te + n)
                        for h in range(8):
                            last = t.transpose(psb(2)[:, h * 128:(h + 1) * 128],
                                               vE[:, h, nsl], CID.ap())
                        inc(last, "g_tr", 1)
                        t.wait_ge(S("g_su"), b_su + n)
                        t.wait_ge(S("g_te"), b_te + n + 1)
                        for h in range(8):
                            last = t.matmul(PS[3 + h // 4].ap()[:, (h % 4) * 128:(h % 4 + 1) * 128],
                                            etm[:, h, :], vvt[:, n, h * 128:(h + 1) * 128],
                                            start=True, stop=True)
                        inc(last, "g_ds", 1)
                        t.wait_ge(S("g_ae"), b_ae + 2 * (n + 1))
                        t.wait_ge(S("g_oe"), b_oe + n)
                        for h in range(8):
                            reg = PS[5 + h // 4].ap()[:, (h % 4) * 128:(h % 4 + 1) * 128]
                            t.matmul(reg, sbf[:, h, :], vq[:, h, nsl], start=True, stop=False)
                            last = t.matmul(reg, vvt[:, n, h * 128:(h + 1) * 128],
                                            amask[:, h, :], start=False, stop=True)
                        inc(last, "g_om", 1)

                @blk.vector
                def _(v):
                    for n in range(8):
                        v.wait_ge(S("g_mm"), b_mm + n + 1)
                        v.wait_ge(S("g_om"), b_om + n)
                        for g in range(2):
                            v.tensor_mul(
                                out=amask[:, 4 * g:4 * (g + 1), :].rearrange("p a b -> p (a b)"),
                                in0=PS[g].ap()[:, 0:512], in1=CGM.ap())
                        inc(v.drain(), "g_ae", 2)
                        v.wait_ge(S("g_ds"), b_ds + n + 1)
                        v.wait_ge(S("g_om"), b_om + n + 1)
                        for g in range(2):
                            v.tensor_add(out=SGLA.ap().rearrange("p h d -> p (h d)")[:, g * 512:(g + 1) * 512],
                                         in0=PS[3 + g].ap()[:, 0:512],
                                         in1=SGLA.ap().rearrange("p h d -> p (h d)")[:, g * 512:(g + 1) * 512])
                        v.drain()
                        for h in range(8):
                            gi = h * 8 + n
                            v.tensor_scalar_mul(out=SGLA.ap()[:, h, :], in0=SGLA.ap()[:, h, :],
                                                scalar1=ebl[:, gi:gi + 1])
                        v.drain()
                        v.tensor_copy(out=SB2.ap()[:, 1024:2048],
                                      in_=SGLA.ap().rearrange("p h d -> p (h d)"))
                        inc(v.drain(), "g_su", 1)

                @blk.scalar
                def _(s):
                    for n in range(8):
                        nsl = slice(n * 128, (n + 1) * 128)
                        s.wait_ge(S("g_tr"), b_tr + n + 1)
                        s.wait_ge(S("g_ds"), b_ds + n)
                        s.copy(out=SB2.ap()[:, 0:1024], in_=psb(2)[:, 0:1024])
                        inc(s.drain(), "g_te", 1)
                        s.wait_ge(S("g_om"), b_om + n + 1)
                        ei = s.activation(out=vq[:, :, nsl].transpose([0, 2, 1]) if False
                                          else vq4o[:, :, n, :],
                                          in_=psw(5, 2), func=AF.Copy)
                        inc(ei, "g_oe", 1)

            dump("o", B2)
            dump("sm", SM, f32=True)
            dump("S", SGLA, f32=True)
            dump("am", WB)
            dump("sb2", SB2)

            vosq = view(B3, DC, TH)
            with nc.Block("gla_rms") as blk:
                @blk.scalar
                def _(s):
                    s.wait_ge(S("g_oe"), val("g_oe"))
                    s.activation(out=vosq, in_=vq, func=AF.Square)
                    inc(s.drain(), "gs", 1)

                @blk.tensor
                def _(t):
                    t.wait_ge(S("gs"), val("gs"))
                    for b in range(2):
                        for c in range(DC):
                            t.matmul(PS[0 + b].ap()[0:1, 0:512], CSUM.ap(),
                                     vosq[:, c, b * 512:(b + 1) * 512],
                                     start=(c == 0), stop=(c == DC - 1))
                    inc(t.drain(), "gt", 1)

                @blk.scalar
                def _(s):
                    s.wait_ge(S("gt"), val("gt"))
                    rr = RSTG.ap()[0:1, 2048:3072]
                    for b in range(2):
                        s.activation(out=rr[:, b * 512:(b + 1) * 512],
                                     in_=PS[b].ap()[0:1, 0:512], func=AF.Ln,
                                     scale=1.0 / 1024.0, bias=LN_EPS)
                    s.drain()
                    s.activation(out=rr, in_=rr, func=AF.Exp, scale=-0.5)
                    inc(s.drain(), "gs", 1)

                @blk.vector
                def _(v):
                    v.wait_ge(S("gs"), val("gs"))
                    ab = RSTG.ap().bitcast(BF16)[0:1, 0:1024]
                    v.tensor_copy(out=ab, in_=RSTG.ap()[0:1, 2048:3072])
                    inc(v.drain(), "gv", 1)

                @blk.tensor
                def _(t):
                    t.wait_ge(S("gv"), val("gv"))
                    ab = RSTG.ap().bitcast(BF16)[0:1, 0:1024]
                    for b in range(2):
                        t.matmul(PS[2 + b].ap()[:, 0:512], CONC.ap()[0:1, :],
                                 ab[:, b * 512:(b + 1) * 512], start=True, stop=True)
                    inc(t.drain(), "gt", 1)

                @blk.vector
                def _(v):
                    v.wait_ge(S("gt"), val("gt"))
                    for b in range(2):
                        c0, c1 = b * 512, (b + 1) * 512
                        rbc = PS[2 + b].ap()[:, 0:512].unsqueeze(1).broadcast_to([P, DC, 512])
                        v.tensor_mul(out=vosq[:, :, c0:c1], in0=vq[:, :, c0:c1], in1=rbc)
                    v.drain()
                    v.tensor_mul(out=vosq, in0=vosq, in1=vgs)
                    inc(v.drain(), "gv", 1)

            def ev_wo(e, ps, ofc, ts, te):
                return e.tensor_add(out=XRES.ap()[:, ofc, t0 + ts:t0 + te],
                                    in0=ps, in1=XRES.ap()[:, ofc, t0 + ts:t0 + te])
            proj_fp32(wo, 1024, vosq, TH, ev_wo, evac_eng="v", scale_cols=sm("gn"),
                      pre=2)

    def l0_ffn():
        vln = view(B1, DC, TH)

        def h_ap(hc):
            buf = [B2, B3, B4][hc // 8]
            return view(buf, 8, TH)[:, hc % 8, :]

        for half in range(2):
            t0 = half * TH
            xsl = XRES.ap()[:, :, t0:t0 + TH]
            def pf_gu(sy):
                for wi, wt in enumerate([w0g, w0u]):
                    src = wt[:, 0:128].rearrange("(kc p) of -> p kc of", p=P)
                    inc(sy.dma_start(
                        out=view(WS, 2, 2, DC, 128)[:, 0, wi].bitcast(I32),
                        in_=src), "dmp0", 16)
            ln_1024(xsl, None, vln, view(B2, DC, TH), view(B3, DC, TH), pf=pf_gu)

            b_wv = val("fwv"); b_mm = val("fmm")
            b_sl = val("fsl"); b_ev = val("fev")
            b_p = [val("dmp0") - 32, val("dmp1")]
            with nc.Block("l0f_gu") as blk:
                @blk.sync
                def _(sy):
                    for hc in range(1, 24):
                        if hc >= 2:
                            sy.wait_ge(S("fmm"), b_mm + 4 * (hc - 1))
                        for wi, wt in enumerate([w0g, w0u]):
                            src = wt[:, hc * 128:(hc + 1) * 128].rearrange(
                                "(kc p) of -> p kc of", p=P)
                            inc(sy.dma_start(
                                out=view(WS, 2, 2, DC, 128)[:, hc % 2, wi].bitcast(I32),
                                in_=src), f"dmp{hc % 2}", 16)
                    for kc2 in range(2):
                        sy.wait_ge(S("fwv"), b_wv + 23 + kc2)
                        src = w0d[kc2 * 128:(kc2 + 1) * 128, 0:512]
                        inc(sy.dma_start(out=view(WS, 2, 512)[:, kc2, :].bitcast(I32),
                                         in_=src), f"dmp{kc2}", 16)

                @blk.vector
                def _(v):
                    for i in range(25):
                        if i < 24:
                            hc = i
                            v.wait_ge(S(f"dmp{hc % 2}"), b_p[hc % 2] + 32 * (hc // 2 + 1))
                            if hc >= 2:
                                v.wait_ge(S("fmm"), b_mm + 4 * (hc - 1))
                            for wi in range(2):
                                v.tensor_copy(out=view(WB, 2, 2, DC, 128)[:, hc % 2, wi],
                                              in_=view(WS, 2, 2, DC, 128)[:, hc % 2, wi].bitcast(I32))
                            inc(v.drain(), "fwv", 1)
                        if i >= 1:
                            hc = i - 1
                            for tb in range(2):
                                v.wait_ge(S("fsl"), b_sl + 2 * hc + tb + 1)
                                v.wait_ge(S("fmm"), b_mm + 4 * hc + 2 + tb + 1)
                                ei = v.tensor_mul(out=h_ap(hc)[:, tb * 512:(tb + 1) * 512],
                                                  in0=PS[4 * (hc % 2) + 2 + tb].ap()[:, 0:512],
                                                  in1=SB2.ap()[:, (2 * (hc % 2) + tb) * 512:(2 * (hc % 2) + tb + 1) * 512])
                                inc(ei, "fev", 1)

                @blk.tensor
                def _(t):
                    for hc in range(24):
                        t.wait_ge(S("fwv"), b_wv + hc + 1)
                        if hc >= 2:
                            t.wait_ge(S("fsl"), b_sl + 2 * (hc - 1))
                            t.wait_ge(S("fev"), b_ev + 2 * (hc - 1))
                        last = None
                        for wi in range(2):
                            for tb in range(2):
                                bank = 4 * (hc % 2) + 2 * wi + tb
                                for kc in range(DC):
                                    last = t.matmul(PS[bank].ap()[:, 0:512],
                                                    view(WB, 2, 2, DC, 128)[:, hc % 2, wi, kc, :],
                                                    vln[:, kc, tb * 512:(tb + 1) * 512],
                                                    start=(kc == 0), stop=(kc == DC - 1))
                                inc(last, "fmm", 1)

                @blk.scalar
                def _(s):
                    for hc in range(24):
                        for tb in range(2):
                            s.wait_ge(S("fmm"), b_mm + 4 * hc + tb + 1)
                            s.activation(out=SB2.ap()[:, (2 * (hc % 2) + tb) * 512:(2 * (hc % 2) + tb + 1) * 512],
                                         in_=PS[4 * (hc % 2) + tb].ap()[:, 0:512], func=AF.Silu,
                                         scale=QS)
                            inc(s.drain(), "fsl", 1)


            for ofh in range(2):
                def wd_ap(kc):
                    if kc < 16:
                        return view(B1, 16, 512)[:, kc, :]
                    return view(WB, 8, 512)[:, kc - 16, :]
                b_wv = val("fwv"); b_mm = val("fmm")
                b_p = [val("dmp0") - 16, val("dmp1") - 16]
                with nc.Block("l0f_dn") as blk:
                    @blk.sync
                    def _(sy, ofh=ofh):
                        for kc in range(2, 24):
                            sy.wait_ge(S("fwv"), b_wv + kc - 1)
                            src = w0d[kc * 128:(kc + 1) * 128, ofh * 512:(ofh + 1) * 512]
                            inc(sy.dma_start(out=view(WS, 2, 512)[:, kc % 2, :].bitcast(I32),
                                             in_=src), f"dmp{kc % 2}", 16)
                        if ofh == 0:
                            for kc2 in range(2):
                                sy.wait_ge(S("fwv"), b_wv + 23 + kc2)
                                src = w0d[kc2 * 128:(kc2 + 1) * 128, 512:1024]
                                inc(sy.dma_start(out=view(WS, 2, 512)[:, kc2, :].bitcast(I32),
                                                 in_=src), f"dmp{kc2}", 16)

                    @blk.vector
                    def _(v):
                        for kc in range(24):
                            v.wait_ge(S(f"dmp{kc % 2}"), b_p[kc % 2] + 16 * (kc // 2 + 1))
                            v.tensor_copy(out=wd_ap(kc),
                                          in_=view(WS, 2, 512)[:, kc % 2, :].bitcast(I32))
                            inc(v.drain(), "fwv", 1)
                        for tb in range(2):
                            v.wait_ge(S("fmm"), b_mm + tb + 1)
                            ei = None
                            for j in range(4):
                                ofc = ofh * 4 + j
                                dst = XRES.ap()[:, ofc, t0 + tb * 512:t0 + (tb + 1) * 512]
                                ei = v.scalar_tensor_tensor(out=dst,
                                                            in0=PS[4 * tb + j].ap()[:, 0:512],
                                                            scalar=QS * QS, in1=dst,
                                                            op0=ALU.mult, op1=ALU.add)
                            inc(ei, "fev", 1)

                    @blk.tensor
                    def _(t):
                        for tb in range(2):
                            last = None
                            for kc in range(24):
                                if tb == 0:
                                    t.wait_ge(S("fwv"), b_wv + kc + 1)
                                for j in range(4):
                                    last = t.matmul(PS[4 * tb + j].ap()[:, 0:512],
                                                    wd_ap(kc)[:, j * 128:(j + 1) * 128],
                                                    h_ap(kc)[:, tb * 512:(tb + 1) * 512],
                                                    start=(kc == 0), stop=(kc == 23))
                            inc(last, "fmm", 1)

    def l1_attn():
        with nc.Block("l1_fw") as blk:
            @blk.sync
            def _(sy):
                inc(sy.dma_start(out=RSTG.ap()[0:1, 0:2], in_=fusw), "dm", 16)

            @blk.scalar
            def _(s):
                s.wait_ge(S("dm"), val("dm"))
                s.activation(out=RSTG.ap()[0:1, 1024:1026], in_=RSTG.ap()[0:1, 0:2], func=AF.Exp)
                inc(s.drain(), "gs", 1)

            @blk.vector
            def _(v):
                v.wait_ge(S("gs"), val("gs"))
                v.reduce_sum(out=RSTG.ap()[0:1, 1026:1027], in_=RSTG.ap()[0:1, 1024:1026], axis=AX.X)
                v.drain()
                v.reciprocal(out=RSTG.ap()[0:1, 1027:1028], in_=RSTG.ap()[0:1, 1026:1027])
                v.drain()
                v.tensor_scalar_mul(out=RSTG.ap().bitcast(BF16)[0:1, 0:2],
                                    in0=RSTG.ap()[0:1, 1024:1026],
                                    scalar1=RSTG.ap()[0:1, 1027:1028])
                inc(v.drain(), "gv", 1)

            @blk.tensor
            def _(t):
                t.wait_ge(S("gv"), val("gv"))
                wb2 = RSTG.ap().bitcast(BF16)[0:1, 0:2]
                t.matmul(PS[0].ap()[:, 0:1], CONC.ap()[0:1, :], wb2[:, 0:1],
                         start=True, stop=True)
                t.matmul(PS[0].ap()[:, 1:2], CONC.ap()[0:1, :], wb2[:, 1:2],
                         start=True, stop=True)
                inc(t.drain(), "gt", 1)

            @blk.vector
            def _(v):
                v.wait_ge(S("gt"), val("gt"))
                v.tensor_copy(out=sm("w0", 1), in_=PS[0].ap()[:, 0:1])
                v.tensor_copy(out=sm("w1", 1), in_=PS[0].ap()[:, 1:2])
                inc(v.drain(), "gv", 1)

        for w in range(2):
            t0 = w * TH
            xsl = XRES.ap()[:, :, t0:t0 + TH]
            vln2 = view(B1, DC, TH)
            ln_1024(xsl, None, vln2, view(B3, DC, TH), view(B4, DC, TH),
                    pf=(pl, 1024, DC))
            dump("l1ln", B1)

            def ev_cp(dst):
                def f(e, ps, ofc, ts, te):
                    return e.tensor_copy(out=dst[:, ofc, ts:te], in_=ps)
                return f

            vlqs = view(B2, DC, TH)
            proj_fp32(pl, 1024, vln2, TH, ev_cp(vlqs), evac_eng="v", pre=2,
                      pf_next=(wlin[:, 0:1024], 1024, DC))
            dump("lqs", B2)

            vlq = view(B3, DC, TH); vlk = view(B4, DC, TH)
            vlv = view(B5, 8, 16, 65)
            def ev_q8(e, ps, ofc, ts, te):
                return e.activation(out=vlq[:, ofc, ts:te], in_=ps, func=AF.Identity,
                                    scale=0.125, bias=sm("blq8")[:, ofc:ofc + 1])
            proj_fp32(wlin[:, 0:1024], 1024, vlqs, TH, ev_q8, evac_eng="s", pre=2,
                      pf_next=(wlin[:, 1024:2048], 1024, DC))
            def ev_kb(e, ps, ofc, ts, te):
                return e.activation(out=vlk[:, ofc, ts:te], in_=ps, func=AF.Identity,
                                    bias=sm("blk")[:, ofc:ofc + 1])
            proj_fp32(wlin[:, 1024:2048], 1024, vlqs, TH, ev_kb, evac_eng="s", pre=2,
                      pf_next=(wlin[:, 2048:3072], 1024, DC))
            with nc.Block("l1_lvo") as blk:
                @blk.vector
                def _(v):
                    v.memset(vlv[:, :, :, 64:65], 1.0)
                    inc(v.drain(), "gv", 1)
            tm_proj(vlqs, vlv, blv_row, wlin[:, 2048:3072], pre=2,
                    pf_next=(wlout, 1024, DC))
            dump("lq", B3)
            dump("lk", B4)
            dump("lv", B5)

            vlo = view(B2, DC, TH)
            b_sc = val("l_sc"); b_se = val("l_se"); b_av = val("l_av")
            b_nr = val("l_nr"); b_rb = val("l_rb"); b_no = val("l_no")
            b_dm = val("dm"); b_st = val("l_st")
            with nc.Block("l1_local") as blk:
                @blk.tensor
                def _(t):
                    t.wait_ge(S("pev"), val("pev"))
                    for lw in range(8):
                        lsl = slice(lw * 128, (lw + 1) * 128)
                        t.wait_ge(S("l_no"), b_no + lw * 4)
                        last = None
                        for h in range(16):
                            r0 = (h % 2) * 64
                            reg = PS[h // 4].ap()[:, (h % 4) * 128:(h % 4 + 1) * 128]
                            t.matmul(reg, vlk[r0:r0 + 64, h // 2, lsl],
                                     vlq[r0:r0 + 64, h // 2, lsl], start=True, stop=False)
                            last = t.matmul(reg, CID.ap(), CL1.ap(), start=False, stop=True)
                            if h % 4 == 3:
                                inc(last, "l_sc", 1)
                        t.wait_ge(S("l_se"), b_se + lw * 4 + 4)
                        for h in range(16):
                            reg = PS[4 + h // 4].ap()[:, (h % 4) * 128:(h % 4 + 1) * 128]
                            last = t.matmul(reg[0:65, :], vlv[:, lw, h, :],
                                            SB2.ap()[:, h * 128:(h + 1) * 128],
                                            start=True, stop=True)
                            if h % 4 == 3:
                                inc(last, "l_av", 1)
                        t.wait_ge(S("l_nr"), b_nr + lw + 1)
                        last = None
                        for g in range(4):
                            last = t.matmul(PS[g].ap()[0:64, 0:512], CONB.ap()[64:65, 0:64],
                                            SB2.ap()[64:65, g * 512:(g + 1) * 512],
                                            start=True, stop=True)
                        inc(last, "l_rb", 1)

                @blk.scalar
                def _(s):
                    for lw in range(8):
                        s.wait_ge(S("dm"), b_dm + 16 * lw)
                        s.wait_ge(S("l_sc"), b_sc + lw * 4 + 2)
                        s.activation(out=SB2.ap()[:, 0:1024], in_=psw(0, 2), func=AF.Exp)
                        s.wait_ge(S("l_sc"), b_sc + lw * 4 + 4)
                        s.activation(out=SB2.ap()[:, 1024:2048], in_=psw(2, 2), func=AF.Exp)
                        inc(s.drain(), "l_se", 4)
                        s.wait_ge(S("l_av"), b_av + lw * 4 + 4)
                        for g in range(4):
                            s.activation(out=SB2.ap()[64:65, g * 512:(g + 1) * 512],
                                         in_=PS[4 + g].ap()[64:65, 0:512], func=AF.Ln)
                        s.drain()
                        with nc.allow_low_precision(reason="softmax recip row bf16"):
                            s.activation(out=SB2.ap()[64:65, 0:2048],
                                         in_=SB2.ap()[64:65, 0:2048], func=AF.Exp,
                                         scale=-1.0)
                        inc(s.drain(), "l_nr", 1)

                @blk.vector
                def _(v):
                    wrbc = view(WB, 2048)
                    for lw in range(8):
                        v.wait_ge(S("l_rb"), b_rb + lw + 1)
                        v.wait_ge(S("dm"), b_dm + 16 * lw)
                        for g in range(4):
                            v.tensor_copy(out=wrbc[0:64, g * 512:(g + 1) * 512],
                                          in_=PS[g].ap()[0:64, 0:512])
                        v.drain()
                        ei = None
                        for h in range(16):
                            g = h // 4
                            src = PS[4 + g].ap()[0:64, (h % 4) * 128:(h % 4 + 1) * 128]
                            rb = wrbc[0:64, h * 128:(h + 1) * 128]
                            if h % 2 == 0:
                                dst = vlo[0:64, h // 2, lw * 128:(lw + 1) * 128]
                            else:
                                dst = SB2.ap()[0:64, (h // 2) * 128:(h // 2 + 1) * 128]
                            ei = v.tensor_mul(out=dst, in0=src, in1=rb)
                        inc(ei, "l_no", 4)
                        inc(v.drain(), "l_st", 1)

                @blk.sync
                def _(sy):
                    vlo4 = view(B2, DC, 8, 128)
                    stg = SB2.ap()[0:64, 0:1024].rearrange("p (c t) -> p c t", c=8)
                    for lw in range(8):
                        sy.wait_ge(S("l_st"), b_st + lw + 1)
                        inc(sy.dma_start(out=vlo4[64:128, :, lw, :], in_=stg), "dm", 16)

            dump("lo", B2)
            vfus = view(B3, DC, TH)

            def ev_lo(e, ps, ofc, ts, te):
                return e.activation(out=vfus[:, ofc, ts:te], in_=ps, func=AF.Identity,
                                    scale=sm("w0", 1))
            proj_fp32(wlout, 1024, vlo, TH, ev_lo, evac_eng="s", bias_row=brow("rowB"),
                      pre=2, pf_next=(pg, 1024, DC), row_load=("rowB", blout_row))

            dump("fusl", B3)
            vgqs = view(B4, DC, TH)
            proj_fp32(pg, 1024, vln2, TH, ev_cp(vgqs), evac_eng="v", pre=2,
                      pf_next=(wgin[:, 0:1024], 1024, DC))

            vgq = view(B1, DC, TH); vgk = view(B2, DC, TH)
            vgv = view(B5, 8, 16, 65)
            def ev_gq(e, ps, ofc, ts, te):
                return e.activation(out=vgq[:, ofc, ts:te], in_=ps, func=AF.Identity,
                                    scale=0.125, bias=sm("bgq8")[:, ofc:ofc + 1])
            proj_fp32(wgin[:, 0:1024], 1024, vgqs, TH, ev_gq, evac_eng="s", pre=2,
                      pf_next=(wgin[:, 1024:2048], 1024, DC))
            def ev_gk(e, ps, ofc, ts, te):
                return e.activation(out=vgk[:, ofc, ts:te], in_=ps, func=AF.Identity,
                                    bias=sm("bgk")[:, ofc:ofc + 1])
            proj_fp32(wgin[:, 1024:2048], 1024, vgqs, TH, ev_gk, evac_eng="s", pre=2,
                      pf_next=(wgin[:, 2048:3072], 1024, DC))
            with nc.Block("l1_gvo") as blk:
                @blk.vector
                def _(v):
                    v.memset(vgv[:, :, :, 64:65], 1.0)
                    inc(v.drain(), "gv", 1)
            tm_proj(vgqs, vgv, bgv_row, wgin[:, 2048:3072], pre=2,
                    pf_next=(wgout, 1024, DC))

            vet = view(B4, 8, TH)
            b_gsc = val("ga_sc"); b_gse = val("ga_se"); b_gav = val("ga_av")
            b_gnr = val("ga_nr"); b_grb = val("ga_rb"); b_gno = val("ga_no")
            b_gdm = val("dm"); b_gst = val("ga_st")
            with nc.Block("l1_global") as blk:
                @blk.tensor
                def _(t):
                    t.wait_ge(S("pev"), val("pev"))
                    for h in range(16):
                        c = h // 2
                        r0 = (h % 2) * 64

                        def emit_av(jt):
                            last = None
                            for ih in range(2):
                                last = t.matmul(PS[2 + ih].ap()[0:65, 0:512],
                                                vgv[:, jt, h, :],
                                                vet[:, jt, ih * 512:(ih + 1) * 512],
                                                start=(jt == 0), stop=(jt == 7))
                            return last

                        t.wait_ge(S("ga_no"), b_gno + h)
                        for jt in range(8):
                            gidx = h * 8 + jt
                            bank = (jt % 2) * 6
                            if gidx >= 2:
                                t.wait_ge(S("ga_se"), b_gse + (gidx - 1) * 2)
                            last = None
                            for ih in range(2):
                                reg = PS[bank + ih].ap()[:, 0:512]
                                ops = [("sc", None)]
                                dlo = jt * 128
                                s0, s1 = ih * 512, (ih + 1) * 512
                                if dlo > s0:
                                    ops.append(("full", min(dlo, s1) - s0))
                                if dlo < s1 and dlo + 128 > s0:
                                    ops.append(("diag", dlo - s0))
                                nops = len(ops)
                                for oi, (kind, arg) in enumerate(ops):
                                    st = (oi == 0)
                                    sp = (oi == nops - 1)
                                    if kind == "sc":
                                        last = t.matmul(reg,
                                                        vgk[r0:r0 + 64, c, jt * 128:(jt + 1) * 128],
                                                        vgq[r0:r0 + 64, c, ih * 512:(ih + 1) * 512],
                                                        start=st, stop=sp)
                                    elif kind == "full":
                                        last = t.matmul(reg[:, 0:arg], CONC.ap()[0:1, :],
                                                        CONR.ap()[0:1, 0:arg],
                                                        start=st, stop=sp)
                                    else:
                                        last = t.matmul(reg[:, arg:arg + 128], CID.ap(),
                                                        CL1.ap(), start=st, stop=sp)
                            inc(last, "ga_sc", 1)
                        t.wait_ge(S("ga_se"), b_gse + (h * 8 + 8) * 2)
                        for jt in range(8):
                            lav = emit_av(jt)
                        inc(lav, "ga_av", 2)
                        t.wait_ge(S("ga_nr"), b_gnr + h + 1)
                        last = None
                        for ih in range(2):
                            last = t.matmul(PS[4 + ih].ap()[0:64, 0:512], CONB.ap()[64:65, 0:64],
                                            SB2.ap()[64:65, ih * 512:(ih + 1) * 512],
                                            start=True, stop=True)
                        inc(last, "ga_rb", 1)

                @blk.scalar
                def _(s):
                    for h in range(16):
                        for jt in range(8):
                            gidx = h * 8 + jt
                            bank = (jt % 2) * 6
                            s.wait_ge(S("ga_sc"), b_gsc + gidx + 1)
                            s.activation(out=vet[:, jt, 0:1024],
                                         in_=psw(bank, 2), func=AF.Exp)
                            inc(s.drain(), "ga_se", 2)
                        s.wait_ge(S("ga_av"), b_gav + h * 2 + 2)
                        for ih in range(2):
                            s.activation(out=SB2.ap()[64:65, ih * 512:(ih + 1) * 512],
                                         in_=PS[2 + ih].ap()[64:65, 0:512], func=AF.Ln)
                        s.drain()
                        with nc.allow_low_precision(reason="softmax recip row bf16"):
                            s.activation(out=SB2.ap()[64:65, 0:1024],
                                         in_=SB2.ap()[64:65, 0:1024], func=AF.Exp,
                                         scale=-1.0)
                        inc(s.drain(), "ga_nr", 1)

                @blk.vector
                def _(v):
                    wrbc = view(WB, 2048)
                    for h in range(16):
                        c = h // 2
                        v.wait_ge(S("ga_rb"), b_grb + h + 1)
                        if h % 2 == 1:
                            v.wait_ge(S("dm"), b_gdm + 16 * (h // 2))
                        for ih in range(2):
                            v.tensor_copy(out=wrbc[0:64, ih * 512:(ih + 1) * 512],
                                          in_=PS[4 + ih].ap()[0:64, 0:512])
                        v.drain()
                        ei = None
                        for ih in range(2):
                            if h % 2 == 0:
                                dst = vgq[0:64, c, ih * 512:(ih + 1) * 512]
                            else:
                                dst = SB2.ap()[0:64, ih * 512:(ih + 1) * 512]
                            ei = v.tensor_mul(out=dst,
                                              in0=PS[2 + ih].ap()[0:64, 0:512],
                                              in1=wrbc[0:64, ih * 512:(ih + 1) * 512])
                        inc(ei, "ga_no", 1)
                        if h % 2 == 1:
                            inc(v.drain(), "ga_st", 1)

                @blk.sync
                def _(sy):
                    vgq4 = view(B1, DC, TH)
                    for ho in range(8):
                        sy.wait_ge(S("ga_st"), b_gst + ho + 1)
                        inc(sy.dma_start(out=vgq4[64:128, ho, :],
                                         in_=SB2.ap()[0:64, 0:1024]), "dm", 16)

            dump("go", B1)
            def ev_go(e, ps, ofc, ts, te):
                dst = vfus[:, ofc, ts:te]
                return e.scalar_tensor_tensor(out=dst, in0=ps, scalar=sm("w1", 1),
                                              in1=dst, op0=ALU.mult, op1=ALU.add)
            proj_fp32(wgout, 1024, vgq, TH, ev_go, evac_eng="v", bias_row=brow("rowB"),
                      pre=2, row_load=("rowB", bgout_row))

            dump("fusg", B3)
            vd = view(B1, DC, TH)
            ln_1024(None, vfus, vd, None, view(B4, DC, TH),
                    gamma=sm("g1"), beta=sm("b1"))
            with nc.Block("l1_res") as blk:
                @blk.vector
                def _(v):
                    v.wait_ge(S("lnv"), val("lnv"))
                    v.tensor_add(out=xsl, in0=vd, in1=xsl)
                    inc(v.drain(), "gv", 1)

    def l1_ffn():
        def u_ap(hc):
            buf = [B3, B4, B5, B1][hc // 8]
            return view(buf, 8, TH)[:, hc % 8, :]

        for half in range(2):
            t0 = half * TH
            xsl = XRES.ap()[:, :, t0:t0 + TH]
            vy = view(B1, DC, TH)
            ln_1024(xsl, None, vy, view(B3, DC, TH), view(B4, DC, TH))
            with nc.Block("l1f_res1") as blk:
                @blk.vector
                def _(v):
                    v.wait_ge(S("lnv"), val("lnv"))
                    v.tensor_add(out=xsl, in0=vy, in1=xsl)
                    inc(v.drain(), "gv", 1)
            vh = view(B2, DC, TH)

            def pf_fc1(sy):
                src = fc1q[:, 0:128].rearrange("(kc p) of -> p kc of", p=P)
                inc(sy.dma_start(out=view(WS, 2, DC, 128)[:, 0].bitcast(I32),
                                 in_=src), "dmp0", 16)
            ln_1024(None, vy, vh, None, view(B4, DC, TH),
                    gamma=sm("g2"), beta=sm("b2"), pf=pf_fc1)

            b_wv = val("fwv"); b_mm = val("fmm")
            b_sl = val("fsl"); b_sr = val("fsr"); b_gt = val("fgt")
            b_p = [val("dmp0") - 16, val("dmp1")]
            with nc.Block("l1f_fc1") as blk:
                @blk.sync
                def _(sy):
                    for hc in range(32):
                        if hc >= 2:
                            sy.wait_ge(S("fwv"), b_wv + hc - 1)
                        if hc >= 1:
                            src = fc1q[:, hc * 128:(hc + 1) * 128].rearrange(
                                "(kc p) of -> p kc of", p=P)
                            inc(sy.dma_start(out=view(WS, 2, DC, 128)[:, hc % 2].bitcast(I32),
                                             in_=src), f"dmp{hc % 2}", 16)
                        inc(sy.dma_start(out=RSTG.ap()[0:1, (hc % 2) * 1024:(hc % 2) * 1024 + 1024],
                                         in_=s1_rows[hc:hc + 1, :]), f"dmp{hc % 2}", 16)
                    sy.wait_ge(S("fsr"), b_sr + 31)
                    src = fc2q[0:128, 0:256]
                    inc(sy.dma_start(out=view(WS, 4, 256)[:, 0].bitcast(I32),
                                     in_=src), "dmq0", 16)
                    inc(sy.dma_start(out=RSTG.ap()[0:1, 2048:2304],
                                     in_=s2_rows[0:1, 0:256]), "dmq0", 16)

                @blk.vector
                def _(v):
                    for i in range(33):
                        if i < 32:
                            hc = i
                            v.wait_ge(S(f"dmp{hc % 2}"), b_p[hc % 2] + 32 * (hc // 2 + 1))
                            if hc >= 2:
                                v.wait_ge(S("fmm"), b_mm + 2 * (hc - 1))
                            v.tensor_scalar_add(out=view(WB, 2, DC, 128)[:, hc % 2],
                                                in0=view(WS, 2, DC, 128)[:, hc % 2].bitcast(I32),
                                                scalar1=-128.0)
                            v.tensor_copy(out=BROW.ap()[0:1, (hc % 2) * 1024:(hc % 2) * 1024 + 1024],
                                          in_=RSTG.ap()[0:1, (hc % 2) * 1024:(hc % 2) * 1024 + 1024])
                            inc(v.drain(), "fsr", 1)
                        if i >= 1:
                            hc = i - 1
                            v.wait_ge(S("fgt"), b_gt + hc + 1)
                            wbd = view(WB, 2, 1024)[:, hc % 2]
                            v.tensor_mul(out=wbd, in0=wbd, in1=psw(6, 2)[:, 0:1024])
                            inc(v.drain(), "fwv", 1)

                @blk.tensor
                def _(t):
                    for hc in range(32):
                        t.wait_ge(S("fsr"), b_sr + hc + 1)
                        if hc >= 1:
                            t.wait_ge(S("fwv"), b_wv + hc)
                        srow = BROW.ap()[0:1, (hc % 2) * 1024:(hc % 2) * 1024 + 1024]
                        for bb in range(2):
                            t.matmul(PS[6 + bb].ap()[:, 0:512],
                                     CONC.ap()[0:1, :], srow[:, bb * 512:(bb + 1) * 512],
                                     start=True, stop=True)
                        inc(t.drain(), "fgt", 1)
                        t.wait_ge(S("fwv"), b_wv + hc + 1)
                        if hc >= 2:
                            t.wait_ge(S("fsl"), b_sl + 2 * (hc - 1))
                        last = None
                        for tb in range(2):
                            for kc in range(DC):
                                last = t.matmul(PS[2 * (hc % 2) + tb].ap()[:, 0:512],
                                                view(WB, 2, DC, 128)[:, hc % 2, kc, :],
                                                vh[:, kc, tb * 512:(tb + 1) * 512],
                                                start=(kc == 0), stop=(kc == DC - 1))
                            inc(last, "fmm", 1)

                @blk.scalar
                def _(s):
                    for hc in range(32):
                        for tb in range(2):
                            s.wait_ge(S("fmm"), b_mm + 2 * hc + tb + 1)
                            s.activation(out=u_ap(hc)[:, tb * 512:(tb + 1) * 512],
                                         in_=PS[2 * (hc % 2) + tb].ap()[:, 0:512], func=AF.Silu,
                                         bias=sm("fb1", 32)[:, hc:hc + 1])
                            inc(s.drain(), "fsl", 1)

            for ofq in range(4):
                ocs0 = ofq * 256
                b_wv = val("fwv"); b_mm = val("fmm")
                b_sr = val("fsr"); b_gt = val("fgt"); b_sl = val("fsl")
                b_pe = val("fpe")
                b_q = [val("dmq0") - 32] + [val(f"dmq{j}") for j in range(1, 4)]
                with nc.Block("l1f_fc2") as blk:
                    @blk.sync
                    def _(sy, ofq=ofq):
                        for kc in range(1, 32):
                            if kc >= 4:
                                sy.wait_ge(S("fwv"), b_wv + kc - 3)
                            src = fc2q[kc * 128:(kc + 1) * 128, ocs0:ocs0 + 256]
                            inc(sy.dma_start(out=view(WS, 4, 256)[:, kc % 4].bitcast(I32),
                                             in_=src), f"dmq{kc % 4}", 16)
                            inc(sy.dma_start(out=RSTG.ap()[0:1, 2048 + (kc % 4) * 256:2048 + (kc % 4) * 256 + 256],
                                             in_=s2_rows[kc:kc + 1, ocs0:ocs0 + 256]),
                                f"dmq{kc % 4}", 16)
                        if ofq < 3:
                            sy.wait_ge(S("fsr"), b_sr + 29)
                            nx0 = (ofq + 1) * 256
                            src = fc2q[0:128, nx0:nx0 + 256]
                            inc(sy.dma_start(out=view(WS, 4, 256)[:, 0].bitcast(I32),
                                             in_=src), "dmq0", 16)
                            inc(sy.dma_start(out=RSTG.ap()[0:1, 2048:2304],
                                             in_=s2_rows[0:1, nx0:nx0 + 256]), "dmq0", 16)

                    @blk.vector
                    def _(v):
                        for i in range(33):
                            if i < 32:
                                kc = i
                                v.wait_ge(S(f"dmq{kc % 4}"), b_q[kc % 4] + 32 * (kc // 4 + 1))
                                if kc >= 4:
                                    v.wait_ge(S("fpe"), b_pe + kc - 3)
                                v.tensor_scalar_add(out=view(WB, 4, 256, off=4096)[:, kc % 4],
                                                    in0=view(WS, 4, 256)[:, kc % 4].bitcast(I32),
                                                    scalar1=-128.0)
                                v.tensor_copy(out=BROW.ap()[0:1, (kc % 4) * 256:(kc % 4) * 256 + 256],
                                              in_=RSTG.ap()[0:1, 2048 + (kc % 4) * 256:2048 + (kc % 4) * 256 + 256])
                                inc(v.drain(), "fsr", 1)
                            if i >= 1:
                                kc = i - 1
                                v.wait_ge(S("fgt"), b_gt + kc + 1)
                                wbd = view(WB, 4, 256, off=4096)[:, kc % 4]
                                v.tensor_mul(out=wbd, in0=wbd,
                                             in1=PS[6 + (kc % 4) // 2].ap()[:, (kc % 2) * 256:(kc % 2) * 256 + 256])
                                inc(v.drain(), "fwv", 1)

                    @blk.tensor
                    def _(t):
                        for kc in range(32):
                            t.wait_ge(S("fsr"), b_sr + kc + 1)
                            if kc >= 4:
                                t.wait_ge(S("fwv"), b_wv + kc - 3)
                            srow = BROW.ap()[0:1, (kc % 4) * 256:(kc % 4) * 256 + 256]
                            t.matmul(PS[6 + (kc % 4) // 2].ap()[:, (kc % 2) * 256:(kc % 2) * 256 + 256],
                                     CONC.ap()[0:1, :], srow, start=True, stop=True)
                            inc(t.drain(), "fgt", 1)
                            t.wait_ge(S("fwv"), b_wv + kc + 1)
                            last = None
                            for jj2 in range(2):
                                for tb in range(2):
                                    last = t.matmul(PS[2 * jj2 + tb].ap()[:, 0:512],
                                                    view(WB, 4, 256, off=4096)[:, kc % 4, jj2 * 128:(jj2 + 1) * 128],
                                                    u_ap(kc)[:, tb * 512:(tb + 1) * 512],
                                                    start=(kc == 0), stop=(kc == 31))
                            inc(last, "fpe", 1)
                        inc(t.drain(), "fmm", 1)

                    @blk.scalar
                    def _(s, ofq=ofq):
                        s.wait_ge(S("fmm"), b_mm + 1)
                        ei = None
                        for jj2 in range(2):
                            ofc = ofq * 2 + jj2
                            for tb in range(2):
                                ei = s.activation(out=SB2.ap()[:, (2 * jj2 + tb) * 512:(2 * jj2 + tb + 1) * 512],
                                                  in_=PS[2 * jj2 + tb].ap()[:, 0:512],
                                                  func=AF.Identity, scale=0.5,
                                                  bias=sm("hfb2", 8)[:, ofc:ofc + 1])
                        inc(s.drain(), "fsl", 1)

                    @blk.vector
                    def _(v, ofq=ofq):
                        v.wait_ge(S("fsl"), b_sl + 1)
                        ei = None
                        for jj2 in range(2):
                            ofc = ofq * 2 + jj2
                            for tb in range(2):
                                dst = XRES.ap()[:, ofc, t0 + tb * 512:t0 + (tb + 1) * 512]
                                ei = v.tensor_add(out=dst,
                                                  in0=SB2.ap()[:, (2 * jj2 + tb) * 512:(2 * jj2 + tb + 1) * 512],
                                                  in1=dst)
                        inc(ei, "fev", 1)

            if half == 0:
                with nc.Block("sthalf") as blk:
                    @blk.sync
                    def _(sy):
                        yv = y_out.rearrange("(c p) t -> p c t", p=P)
                        inc(sy.dma_start(out=yv[:, :, 0:TH],
                                         in_=XRES.ap()[:, :, 0:TH]), "dm", 16)

    seq = [("L0A", l0_attn), ("L0F", l0_ffn), ("L1A", l1_attn), ("L1F", l1_ffn)]
    try:
        for name, fn in seq:
            fn()
            if stop_after == name:
                break
    except StopBuild:
        return nc

    with nc.Block("store") as blk:
        @blk.sync
        def _(sy):
            yv = y_out.rearrange("(c p) t -> p c t", p=P)
            inc(sy.dma_start(out=yv[:, :, TH:2 * TH],
                             in_=XRES.ap()[:, :, TH:2 * TH]), "dm", 16)
            sy.wait_ge(S("dm"), val("dm"))
            sy.wait_ge(S("dmx"), 16)

    return nc


def _prep_weights(inp):
    C = np.ascontiguousarray
    d = {}
    d["wq"] = C(inp["L0_attn_Wq"].T)
    d["wk"] = C(inp["L0_attn_Wk"].T)
    d["wv"] = C(inp["L0_attn_Wv"].T)
    d["wo"] = C(inp["L0_attn_Wo"].T)
    d["wg1"] = C(inp["L0_attn_Wg1"].T)
    d["wg2"] = C(inp["L0_attn_Wg2"].T)
    d["gn_col"] = C(inp["L0_attn_g_norm_w"].reshape(8, 128).T)
    d["w0g"] = C(inp["L0_ffn_w_gate"].T)
    d["w0u"] = C(inp["L0_ffn_w_up"].T)
    d["w0d"] = C(inp["L0_ffn_w_down"].T)
    d["pl"] = C(inp["L1_attn_proj_local"])
    d["pg"] = C(inp["L1_attn_proj_global"])
    d["wlin"] = C(inp["L1_attn_Wl_in"].T)
    d["wgin"] = C(inp["L1_attn_Wg_in"].T)
    d["wlout"] = C(inp["L1_attn_Wl_out"].T)
    d["wgout"] = C(inp["L1_attn_Wg_out"].T)
    bl = inp["L1_attn_bl_in"]; bg = inp["L1_attn_bg_in"]
    d["blq_col"] = C(bl[0:1024].reshape(8, 128).T)
    d["blk_col"] = C(bl[1024:2048].reshape(8, 128).T)
    d["blv_row"] = C(bl[2048:3072].reshape(1, 1024))
    d["bgq_col"] = C(bg[0:1024].reshape(8, 128).T)
    d["bgk_col"] = C(bg[1024:2048].reshape(8, 128).T)
    d["bgv_row"] = C(bg[2048:3072].reshape(1, 1024))
    d["blout_row"] = C(inp["L1_attn_bl_out"].reshape(1, 1024))
    d["bgout_row"] = C(inp["L1_attn_bg_out"].reshape(1, 1024))
    d["fusw"] = C(inp["L1_attn_fusion_w"].reshape(1, 2))
    d["g1_col"] = C(inp["L1_attn_ln_gamma"].reshape(8, 128).T)
    d["b1_col"] = C(inp["L1_attn_ln_beta"].reshape(8, 128).T)
    d["g2_col"] = C(inp["L1_ffn_ln_gamma"].reshape(8, 128).T)
    d["b2_col"] = C(inp["L1_ffn_ln_beta"].reshape(8, 128).T)
    d["fc1q"] = C(inp["L1_ffn_fc1_q"])
    d["fc2q"] = C(inp["L1_ffn_fc2_q"])
    s1 = inp["L1_ffn_fc1_scales"]
    d["s1_rows"] = C(np.stack([s1[:, hc * 128:(hc + 1) * 128].ravel()
                               for hc in range(32)]))
    d["s2_rows"] = C(inp["L1_ffn_fc2_scales"])
    d["fb1_col"] = C(inp["L1_ffn_fc1_bias"].reshape(32, 128).T)
    d["fb2_row"] = C(inp["L1_ffn_fc2_bias"].reshape(1, 1024))
    d["fb2_col"] = C(inp["L1_ffn_fc2_bias"].reshape(8, 128).T)
    return d


def kernel(_stop_after=None, _cores=None, **inputs):
    x = np.asarray(inputs["L0_attn_hidden_states"], dtype=np.float32)
    wd = _prep_weights({k: np.asarray(v) for k, v in inputs.items()})
    nc = build_program(stop_after=_stop_after)
    ncores = NCORES if _cores is None else _cores
    in_maps = []
    for b in range(ncores):
        m = dict(wd)
        m["x"] = np.ascontiguousarray(x[b].T)
        in_maps.append(m)
    res = run_bass_kernel_spmd(nc, in_maps, core_ids=list(range(ncores)))
    out = np.stack([np.asarray(r["y"], dtype=np.float32).T for r in res.results])
    if ncores < NCORES:
        pad = np.zeros((NCORES - ncores, T, 1024), np.float32)
        out = np.concatenate([out, pad], axis=0)
    return out


# version 21
# speedup vs baseline: 1.0130x; 1.0050x over previous
import numpy as np
import ml_dtypes
import concourse.bass as bass
import concourse.mybir as mybir
from concourse.bass_utils import run_bass_kernel_spmd

F32 = mybir.dt.float32
BF16 = mybir.dt.bfloat16
I32 = mybir.dt.int32
AF = mybir.ActivationFunctionType
ALU = mybir.AluOpType
AX = mybir.AxisListType

P = 128
T = 2048
TH = 1024
DC = 8
NCORES = 8
LN_EPS = 1e-5
QS = 1.0 / 127.0
BF = ml_dtypes.bfloat16


def _ceil(a, b):
    return (a + b - 1) // b


def view(buf, *shape, off=0):
    n = int(np.prod(shape))
    ap = buf.ap()[:, off:off + n]
    if len(shape) == 1:
        return ap
    pat = "p (" + " ".join(f"d{i}" for i in range(len(shape))) + ") -> p " + \
          " ".join(f"d{i}" for i in range(len(shape)))
    return ap.rearrange(pat, **{f"d{i}": s for i, s in enumerate(shape)})


class StopBuild(Exception):
    pass


def build_program(stop_after=None, dbg=None):
    nc = bass.Bass("TRN2", target_bir_lowering=False)

    x_in = nc.dram_tensor("x", [1024, T], F32, kind="ExternalInput").ap()
    y_out = nc.dram_tensor("y", [1024, T], F32, kind="ExternalOutput").ap()

    def din(name, shape, dt=F32):
        return nc.dram_tensor(name, shape, dt, kind="ExternalInput").ap()

    wq = din("wq", [1024, 1024]); wk = din("wk", [1024, 1024])
    wv = din("wv", [1024, 1024]); wo = din("wo", [1024, 1024])
    wg1 = din("wg1", [1024, 128]); wg2 = din("wg2", [128, 1024])
    gn_col = din("gn_col", [128, 8])
    w0g = din("w0g", [1024, 3072], I32)
    w0u = din("w0u", [1024, 3072], I32)
    w0d = din("w0d", [3072, 1024], I32)
    pl = din("pl", [1024, 1024]); pg = din("pg", [1024, 1024])
    wlin = din("wlin", [1024, 3072]); wgin = din("wgin", [1024, 3072])
    wlout = din("wlout", [1024, 1024]); wgout = din("wgout", [1024, 1024])
    blq_col = din("blq_col", [128, 8]); blk_col = din("blk_col", [128, 8])
    bgq_col = din("bgq_col", [128, 8]); bgk_col = din("bgk_col", [128, 8])
    blv_row = din("blv_row", [1, 1024]); bgv_row = din("bgv_row", [1, 1024])
    blout_row = din("blout_row", [1, 1024]); bgout_row = din("bgout_row", [1, 1024])
    fusw = din("fusw", [1, 2])
    g1_col = din("g1_col", [128, 8]); b1_col = din("b1_col", [128, 8])
    g2_col = din("g2_col", [128, 8]); b2_col = din("b2_col", [128, 8])
    fc1q = din("fc1q", [1024, 4096], I32)
    fc2q = din("fc2q", [4096, 1024], I32)
    s1_rows = din("s1_rows", [32, 1024]); s2_rows = din("s2_rows", [32, 1024])
    fb1_col = din("fb1_col", [128, 32]); fb2_row = din("fb2_row", [1, 1024])
    fb2_col = din("fb2_col", [128, 8])

    jj, ii = np.meshgrid(np.arange(P), np.arange(P), indexing="ij")
    ident_h = nc.inline_tensor(np.eye(P, dtype=BF), name="c_ident").ap()
    gmask_h = nc.inline_tensor(np.tile((ii >= jj).astype(BF), (1, 4)), name="c_gmask").ap()
    l1m_h = nc.inline_tensor((jj > ii).astype(BF), name="c_l1m").ap()
    onesr_h = nc.inline_tensor(np.ones((1, 512), dtype=BF), name="c_onesr").ap()
    onesc_h = nc.inline_tensor(np.ones((1, 128), dtype=BF), name="c_onesc").ap()
    sumc_h = nc.inline_tensor(np.ones((128, 1), dtype=BF), name="c_sumc").ap()
    _cb = np.zeros((128, 128), dtype=BF); _cb[64, :] = 1.0
    conb_h = nc.inline_tensor(_cb, name="c_conb").ap()
    _fconsts = [LN_EPS, -0.5, 1.0 / 1024.0, QS, 0.125, 1.0, 0.0]
    fcon_h = nc.inline_tensor(np.tile(np.asarray(_fconsts, np.float32), (P, 1)),
                              name="c_fcon").ap()

    A = nc.alloc_sbuf_tensor
    XRES = A("XRES", [P, DC, T], F32)
    BN = 8320
    B1 = A("B1", [P, BN], BF16)
    B2 = A("B2", [P, BN], BF16)
    B3 = A("B3", [P, BN], BF16)
    B4 = A("B4", [P, BN], BF16)
    B5 = A("B5", [P, BN], BF16)
    WS = A("WS", [P, 4096], F32)
    WB = A("WB", [P, 8192], BF16)
    SGLA = A("SGLA", [P, DC, 128], F32)
    SM = A("SM", [P, 640], F32)
    SB2 = A("SB2", [P, 2048], BF16)
    BROW = A("BROW", [1, 2048], BF16)
    RSTG = A("RSTG", [1, 3072], F32)
    CID = A("CID", [P, 128], BF16)
    CGM = A("CGM", [P, 512], BF16)
    CL1 = A("CL1", [P, 128], BF16)
    CONR = A("CONR", [1, 512], BF16)
    CONC = A("CONC", [1, 128], BF16)
    CSUM = A("CSUM", [P, 1], BF16)
    CONB = A("CONB", [P, 128], BF16)
    CFC = A("CFC", [P, len(_fconsts)], F32)
    for _j, _v in enumerate(_fconsts):
        nc.const_aps.register(F32, float(_v), CFC.ap()[:, _j:_j + 1])

    PSALL = nc.alloc_psum_tensor("psall", [P, 4096], F32)

    class _PSView:
        def __init__(self, i):
            self.i = i

        def ap(self):
            return PSALL.ap()[:, self.i * 512:(self.i + 1) * 512]

    PS = [_PSView(i) for i in range(8)]

    def psb(i):
        return PS[i].ap().bitcast(BF16)

    def psw(i, n):
        return PSALL.ap()[:, i * 512:(i + n) * 512]

    _blkn = [0]
    _origBlock = nc.Block

    def _Block(tag):
        _blkn[0] += 1
        return _origBlock(f"{tag}{_blkn[0]}")
    nc.Block = _Block

    sems = {}
    cnt = {}

    def S(name):
        if name not in sems:
            sems[name] = nc.alloc_semaphore(name)
            cnt[name] = 0
        return sems[name]

    def inc(instr, name, k=1):
        instr.then_inc(S(name), k)
        cnt[name] += k
        return cnt[name]

    def val(name):
        S(name)
        return cnt[name]

    smoff = {}
    _smn = [0]

    def sm(name, width=8):
        if name not in smoff:
            smoff[name] = _smn[0]
            _smn[0] += width
            assert _smn[0] <= 640, "SM overflow"
        return SM.ap()[:, smoff[name]:smoff[name] + width]

    broff = {}
    _brn = [0]

    def brow(name, width=1024):
        if name not in broff:
            broff[name] = _brn[0]
            _brn[0] += width
            assert _brn[0] <= 2048, "BROW overflow"
        return BROW.ap()[:, broff[name]:broff[name] + width]

    with nc.Block("init") as blk:
        @blk.sync
        def _(sy):
            xr = x_in.rearrange("(c p) t -> p c t", p=P)
            inc(sy.dma_start(out=XRES.ap()[:, :, 0:TH], in_=xr[:, :, 0:TH]), "dm", 16)
            inc(sy.dma_start(out=XRES.ap()[:, :, TH:2 * TH], in_=xr[:, :, TH:2 * TH]),
                "dmx", 16)
            for dst, src in [(CID, ident_h), (CGM, gmask_h), (CL1, l1m_h),
                             (CONR, onesr_h), (CONC, onesc_h), (CSUM, sumc_h),
                             (CFC, fcon_h), (CONB, conb_h)]:
                inc(sy.dma_start(out=dst.ap(), in_=src), "dm", 16)
            for name, h in [("gn", gn_col), ("blq", blq_col), ("blk", blk_col),
                            ("bgq", bgq_col), ("bgk", bgk_col), ("g1", g1_col),
                            ("b1", b1_col), ("g2", g2_col), ("b2", b2_col)]:
                inc(sy.dma_start(out=sm(name, 8), in_=h), "dm", 16)
            inc(sy.dma_start(out=sm("fb1", 32), in_=fb1_col), "dm", 16)
            inc(sy.dma_start(out=sm("fb2c", 8), in_=fb2_col), "dm", 16)

        @blk.vector
        def _(v):
            v.wait_ge(S("dm"), val("dm"))
            v.memset(SGLA.ap(), 0.0)
            v.tensor_scalar_mul(out=sm("blq8"), in0=sm("blq"), scalar1=0.125)
            v.tensor_scalar_mul(out=sm("bgq8"), in0=sm("bgq"), scalar1=0.125)
            v.tensor_scalar_mul(out=sm("hfb2", 8), in0=sm("fb2c", 8), scalar1=0.5)

    def ensure_row(slot, hbm_row):
        with nc.Block("rowcvt") as blk:
            @blk.sync
            def _(sy):
                inc(sy.dma_start(out=RSTG.ap()[0:1, 0:1024], in_=hbm_row), "dm", 16)

            @blk.vector
            def _(v):
                v.wait_ge(S("dm"), val("dm"))
                v.tensor_copy(out=brow(slot), in_=RSTG.ap()[0:1, 0:1024])

    def dump(tag, buf, f32=False):
        if dbg != tag:
            return
        with nc.Block("dbg") as blk:
            @blk.sync
            def _(sy):
                yv = y_out.rearrange("(a b) t -> a (b t)", a=128)
                src = buf.ap() if f32 else buf.ap().bitcast(F32)
                n = src.shape[1] if len(src.shape) == 2 else int(np.prod(src.shape[1:]))
                srcf = src if len(src.shape) == 2 else src.rearrange("p a b -> p (a b)")
                inc(sy.dma_start(out=yv[:, 0:n], in_=srcf), "dm", 16)
                sy.wait_ge(S("dm"), val("dm"))
        raise StopBuild

    def ln_1024(src_f32, src_bf, out_view, scr1, scr2, gamma=None, beta=None,
                pf=None, xw=None):
        srcx = src_f32 if src_f32 is not None else src_bf
        with nc.Block("ln") as blk:
            if pf is not None:
                @blk.sync
                def _(sy):
                    if callable(pf):
                        pf(sy)
                    else:
                        pf_emit(sy, *pf)

            @blk.vector
            def _(v):
                v.wait_ge(S("dm"), val("dm"))
                if xw is not None:
                    for nm, tg in xw:
                        v.wait_ge(S(nm), tg)
                if src_f32 is not None:
                    v.tensor_copy(out=scr1, in_=src_f32)
                inc(v.drain(), "lnv", 1)

            @blk.scalar
            def _(s):
                if xw is not None:
                    for nm, tg in xw:
                        s.wait_ge(S(nm), tg)
                s.activation(out=scr2, in_=srcx, func=AF.Square)
                inc(s.drain(), "lns", 1)

            @blk.tensor
            def _(t):
                t.wait_ge(S("lnv"), val("lnv"))
                t.wait_ge(S("lns"), val("lns"))
                xb = scr1 if src_f32 is not None else src_bf
                for b in range(2):
                    c0, c1 = b * 512, (b + 1) * 512
                    for c in range(DC):
                        t.matmul(PS[0 + b].ap()[0:1, 0:512], CSUM.ap(),
                                 xb[:, c, c0:c1], start=(c == 0), stop=(c == DC - 1))
                    for c in range(DC):
                        t.matmul(PS[2 + b].ap()[0:1, 0:512], CSUM.ap(),
                                 scr2[:, c, c0:c1], start=(c == 0), stop=(c == DC - 1))
                inc(t.drain(), "lnt", 1)

            @blk.vector
            def _(v):
                v.wait_ge(S("lnt"), val("lnt"))
                mu = RSTG.ap()[0:1, 1024:2048]
                var = RSTG.ap()[0:1, 2048:3072]
                for b in range(2):
                    bs = slice(b * 512, (b + 1) * 512)
                    v.tensor_scalar_mul(out=mu[:, bs], in0=PS[0 + b].ap()[0:1, 0:512],
                                        scalar1=1.0 / 1024.0)
                v.drain()
                for b in range(2):
                    bs = slice(b * 512, (b + 1) * 512)
                    v.tensor_mul(out=var[:, bs], in0=mu[:, bs], in1=mu[:, bs])
                    v.drain()
                    v.scalar_tensor_tensor(out=var[:, bs], in0=PS[2 + b].ap()[0:1, 0:512],
                                           scalar=1.0 / 1024.0, in1=var[:, bs],
                                           op0=ALU.mult, op1=ALU.subtract)
                inc(v.drain(), "lnv", 1)

            @blk.scalar
            def _(s):
                s.wait_ge(S("lnv"), val("lnv"))
                rstd = RSTG.ap()[0:1, 2048:3072]
                s.activation(out=rstd, in_=rstd, func=AF.Ln, bias=LN_EPS)
                s.drain()
                s.activation(out=rstd, in_=rstd, func=AF.Exp, scale=-0.5)
                inc(s.drain(), "lns", 1)

            @blk.vector
            def _(v):
                v.wait_ge(S("lns"), val("lns"))
                mu = RSTG.ap()[0:1, 1024:2048]
                rstd = RSTG.ap()[0:1, 2048:3072]
                arow = RSTG.ap().bitcast(BF16)[0:1, 0:1024]
                brw = RSTG.ap().bitcast(BF16)[0:1, 1024:2048]
                v.tensor_copy(out=arow, in_=rstd)
                v.scalar_tensor_tensor(out=brw, in0=mu, scalar=-1.0, in1=rstd,
                                       op0=ALU.mult, op1=ALU.mult)
                inc(v.drain(), "lnv", 1)

            @blk.tensor
            def _(t):
                t.wait_ge(S("lnv"), val("lnv"))
                arow = RSTG.ap().bitcast(BF16)[0:1, 0:1024]
                brw = RSTG.ap().bitcast(BF16)[0:1, 1024:2048]
                for b in range(2):
                    bs = slice(b * 512, (b + 1) * 512)
                    t.matmul(PS[4 + b].ap()[:, 0:512], CONC.ap()[0:1, :],
                             arow[:, bs], start=True, stop=True)
                    t.matmul(PS[6 + b].ap()[:, 0:512], CONC.ap()[0:1, :],
                             brw[:, bs], start=True, stop=True)
                inc(t.drain(), "lnt", 1)

            @blk.vector
            def _(v):
                v.wait_ge(S("lnt"), val("lnt"))
                for b in range(2):
                    c0, c1 = b * 512, (b + 1) * 512
                    abc = PS[4 + b].ap()[:, 0:512].unsqueeze(1).broadcast_to([P, DC, 512])
                    bbc = PS[6 + b].ap()[:, 0:512].unsqueeze(1).broadcast_to([P, DC, 512])
                    v.tensor_mul(out=scr2[:, :, c0:c1], in0=srcx[:, :, c0:c1], in1=abc)
                    v.drain()
                    v.tensor_add(out=out_view[:, :, c0:c1], in0=scr2[:, :, c0:c1], in1=bbc)
                if gamma is not None:
                    v.drain()
                    for c in range(DC):
                        v.tensor_scalar(out=out_view[:, c, :], in0=out_view[:, c, :],
                                        scalar1=gamma[:, c:c + 1], scalar2=beta[:, c:c + 1],
                                        op0=ALU.mult, op1=ALU.add)
                inc(v.drain(), "lnv", 1)

    def load_w_fp32(w_hbm, of0, nofe, wb_off=0, scale_cols=None, kc_n=DC):
        per = 2048 // kc_n
        o = 0
        while o < nofe:
            cur = min(per, nofe - o)
            with nc.Block("ldw") as blk:
                @blk.sync
                def _(sy, o=o, cur=cur):
                    src = w_hbm[:, of0 + o:of0 + o + cur].rearrange(
                        "(kc p) of -> p kc of", p=P)
                    inc(sy.dma_start(out=view(WS, kc_n, cur), in_=src), "dm", 16)

                @blk.vector
                def _(v, o=o, cur=cur):
                    v.wait_ge(S("dm"), val("dm"))
                    dst = view(WB, kc_n, nofe, off=wb_off)
                    if scale_cols is None:
                        v.tensor_copy(out=dst[:, :, o:o + cur], in_=view(WS, kc_n, cur))
                    else:
                        for kc in range(kc_n):
                            v.tensor_scalar_mul(out=dst[:, kc, o:o + cur],
                                                in0=view(WS, kc_n, cur)[:, kc, :],
                                                scalar1=scale_cols[:, kc:kc + 1])
                    inc(v.drain(), "wv", 1)
            o += cur

    def n_units(nofe, kc_n):
        return _ceil(nofe, 2048 // kc_n)

    def pf_emit(sy, w_hbm, nofe, kc_n, war=None):
        unit_ofe = 2048 // kc_n
        nu = min(2, n_units(nofe, kc_n))
        for u in range(nu):
            if war is not None and war[u] is not None:
                sy.wait_ge(S("pwv"), war[u])
            o0, o1 = u * unit_ofe, min(nofe, (u + 1) * unit_ofe)
            src = w_hbm[:, o0:o1].rearrange("(kc p) of -> p kc of", p=P)
            inc(sy.dma_start(out=view(WS, 2, kc_n, unit_ofe)[:, u, :, 0:o1 - o0],
                             in_=src), f"dmp{u}", 16)
        return nu

    def fm_proj(src_bf, nt, nof, evac, evac_eng="v", bias_row=None, kc_n=DC,
                wbv=None, of_base=0):
        wbv_ = wbv if wbv is not None else view(WB, kc_n, nof * 128)
        tiles = [(ofc, tb) for ofc in range(nof) for tb in range(_ceil(nt, 512))]
        mm_b = val("pmm"); ev_b = val("pev")
        with nc.Block("proj") as blk:
            @blk.tensor
            def _(t):
                t.wait_ge(S("wv"), val("wv"))
                t.wait_ge(S("dm"), val("dm"))
                for i, (ofc, tb) in enumerate(tiles):
                    bank = i % 2
                    ts, te = tb * 512, min(nt, (tb + 1) * 512)
                    if i >= 2:
                        t.wait_ge(S("pev"), ev_b + i - 1)
                    last = None
                    for kc in range(kc_n):
                        last = t.matmul(PS[bank].ap()[:, 0:te - ts],
                                        wbv_[:, kc, ofc * 128:(ofc + 1) * 128],
                                        src_bf[:, kc, ts:te], start=(kc == 0),
                                        stop=(kc == kc_n - 1 and bias_row is None))
                    if bias_row is not None:
                        last = t.matmul(PS[bank].ap()[:, 0:te - ts],
                                        bias_row[0:1, (of_base + ofc) * 128:(of_base + ofc + 1) * 128],
                                        CONR.ap()[0:1, 0:te - ts], start=False, stop=True)
                    inc(last, "pmm", 1)

            def emit_evac(eng):
                for i, (ofc, tb) in enumerate(tiles):
                    bank = i % 2
                    ts, te = tb * 512, min(nt, (tb + 1) * 512)
                    eng.wait_ge(S("pmm"), mm_b + i + 1)
                    ei = evac(eng, PS[bank].ap()[:, 0:te - ts], ofc, ts, te)
                    inc(ei, "pev", 1)

            if evac_eng == "v":
                @blk.vector
                def _(v):
                    emit_evac(v)
            else:
                @blk.scalar
                def _(s):
                    emit_evac(s)

    def proj_fp32(w_hbm, nofe, src_bf, nt, evac, evac_eng="v", bias_row=None,
                  kc_n=DC, of_base=0, scale_cols=None, pre=0, pf_next=None,
                  row_load=None):
        unit_ofe = 2048 // kc_n
        units = []
        o = 0
        while o < nofe:
            units.append((o, min(nofe, o + unit_ofe)))
            o = units[-1][1]
        ntb = _ceil(nt, 512)
        tcounts = []
        tot = 0
        for (o0, o1) in units:
            tot += ((o1 - o0) // 128) * ntb
            tcounts.append(tot)
        b_p = [val("dmp0") - (16 if pre >= 1 else 0),
               val("dmp1") - (16 if pre >= 2 else 0)]
        b_wv = val("pwv"); b_mm = val("pmm"); b_ev = val("pev")
        b_dmt = val("dm")
        with nc.Block("projp") as blk:
            @blk.sync
            def _(sy):
                if row_load is not None:
                    inc(sy.dma_start(out=RSTG.ap()[0:1, 0:1024], in_=row_load[1]),
                        "dmr", 16)
                for u, (o0, o1) in enumerate(units):
                    if u < pre:
                        continue
                    if u >= 2:
                        sy.wait_ge(S("pwv"), b_wv + u - 1)
                    src = w_hbm[:, o0:o1].rearrange("(kc p) of -> p kc of", p=P)
                    inc(sy.dma_start(out=view(WS, 2, kc_n, unit_ofe)[:, u % 2, :, 0:o1 - o0],
                                     in_=src), f"dmp{u % 2}", 16)
                if pf_next is not None:
                    U = len(units)
                    war = [b_wv + ((U - 1) // 2) * 2 + 1, None]
                    if U >= 2:
                        war[1] = b_wv + ((U - 2) // 2) * 2 + 2
                    pf_emit(sy, *pf_next, war=war)

            def unit_tiles(u):
                o0, o1 = units[u]
                base = tcounts[u - 1] if u > 0 else 0
                out = []
                i = base
                for ofr in range((o1 - o0) // 128):
                    for tb in range(ntb):
                        out.append((i, o0 // 128 + ofr, tb))
                        i += 1
                return out

            def emit_ev_unit(eng, u):
                for i, ofc, tb in unit_tiles(u):
                    bank = i % 8
                    ts, te = tb * 512, min(nt, (tb + 1) * 512)
                    eng.wait_ge(S("pmm"), b_mm + i + 1)
                    ei = evac(eng, PS[bank].ap()[:, 0:te - ts], ofc, ts, te)
                    inc(ei, "pev", 1)

            @blk.vector
            def _(v):
                if row_load is not None:
                    v.wait_ge(S("dmr"), val("dmr"))
                    v.tensor_copy(out=brow(row_load[0]), in_=RSTG.ap()[0:1, 0:1024])
                for step in range(len(units) + 1):
                    if step < len(units):
                        u = step
                        o0, o1 = units[u]
                        v.wait_ge(S(f"dmp{u % 2}"), b_p[u % 2] + 16 * (u // 2 + 1))
                        if u >= 2:
                            v.wait_ge(S("pmm"), b_mm + tcounts[u - 2])
                        dstw = view(WB, 2, kc_n, unit_ofe)[:, u % 2]
                        srcw = view(WS, 2, kc_n, unit_ofe)[:, u % 2]
                        if scale_cols is None:
                            v.tensor_copy(out=dstw[:, :, 0:o1 - o0], in_=srcw[:, :, 0:o1 - o0])
                        else:
                            for kc in range(kc_n):
                                v.tensor_scalar_mul(out=dstw[:, kc, 0:o1 - o0],
                                                    in0=srcw[:, kc, 0:o1 - o0],
                                                    scalar1=scale_cols[:, kc:kc + 1])
                        inc(v.drain(), "pwv", 1)
                    if evac_eng == "v" and step >= 1:
                        emit_ev_unit(v, step - 1)

            @blk.tensor
            def _(t):
                t.wait_ge(S("dm"), b_dmt)
                i = 0
                for u, (o0, o1) in enumerate(units):
                    t.wait_ge(S("pwv"), b_wv + u + 1)
                    for ofr in range((o1 - o0) // 128):
                        ofc = o0 // 128 + ofr
                        for tb in range(ntb):
                            bank = i % 8
                            ts, te = tb * 512, min(nt, (tb + 1) * 512)
                            if i >= 8:
                                t.wait_ge(S("pev"), b_ev + i - 7)
                            last = None
                            for kc in range(kc_n):
                                last = t.matmul(PS[bank].ap()[:, 0:te - ts],
                                                view(WB, 2, kc_n, unit_ofe)[:, u % 2, kc, ofr * 128:(ofr + 1) * 128],
                                                src_bf[:, kc, ts:te], start=(kc == 0),
                                                stop=(kc == kc_n - 1 and bias_row is None))
                            if bias_row is not None:
                                last = t.matmul(PS[bank].ap()[:, 0:te - ts],
                                                bias_row[0:1, (of_base + ofc) * 128:(of_base + ofc + 1) * 128],
                                                CONR.ap()[0:1, 0:te - ts], start=False, stop=True)
                            inc(last, "pmm", 1)
                            i += 1

            if evac_eng == "s":
                @blk.scalar
                def _(se):
                    for u in range(len(units)):
                        emit_ev_unit(se, u)

    def tm_proj(src_bf, vdst, bias_name, w_hbm, pre=0, pf_next=None):
        b_mm = val("pmm"); b_ev = val("pev"); b_wv = val("pwv")
        b_p = [val("dmp0") - (16 if pre >= 1 else 0),
               val("dmp1") - (16 if pre >= 2 else 0)]
        with nc.Block("tmproj") as blk:
            @blk.sync
            def _(sy):
                inc(sy.dma_start(out=RSTG.ap()[0:1, 0:1024], in_=bias_name), "dmr", 16)
                for u in range(4):
                    if u < pre:
                        continue
                    if u >= 2:
                        sy.wait_ge(S("pwv"), b_wv + u - 1)
                    src = w_hbm[:, u * 256:(u + 1) * 256].rearrange(
                        "(kc p) of -> p kc of", p=P)
                    inc(sy.dma_start(out=view(WS, 2, DC, 256)[:, u % 2],
                                     in_=src), f"dmp{u % 2}", 16)
                if pf_next is not None:
                    pf_emit(sy, *pf_next, war=[b_wv + 3, b_wv + 4])

            @blk.vector
            def _(v):
                v.wait_ge(S("dmr"), val("dmr"))
                v.tensor_copy(out=brow("rowA"), in_=RSTG.ap()[0:1, 0:1024])
                for u in range(4):
                    v.wait_ge(S(f"dmp{u % 2}"), b_p[u % 2] + 16 * (u // 2 + 1))
                    v.tensor_copy(out=view(WB, DC, 1024)[:, :, u * 256:(u + 1) * 256],
                                  in_=view(WS, 2, DC, 256)[:, u % 2])
                    inc(v.drain(), "pwv", 1)
                i = 0
                for ofb in range(2):
                    for tt in range(8):
                        bank = i % 4
                        v.wait_ge(S("pmm"), b_mm + i + 1)
                        ei = None
                        for hh in range(8):
                            h = ofb * 8 + hh
                            ei = v.tensor_copy(out=vdst[:, tt, h, 0:64],
                                               in_=PS[bank].ap()[:, hh * 64:(hh + 1) * 64])
                        inc(ei, "pev", 1)
                        i += 1

            @blk.tensor
            def _(t):
                t.wait_ge(S("gv"), val("gv"))
                t.wait_ge(S("dm"), val("dm"))
                i = 0
                for ofb in range(2):
                    t.wait_ge(S("pwv"), b_wv + 2 * (ofb + 1))
                    for tt in range(8):
                        bank = i % 4
                        if i >= 4:
                            t.wait_ge(S("pev"), b_ev + i - 3)
                        for kc in range(DC):
                            t.matmul(PS[bank].ap()[:, 0:512],
                                     src_bf[:, kc, tt * 128:(tt + 1) * 128],
                                     view(WB, DC, 1024)[:, kc, ofb * 512:(ofb + 1) * 512],
                                     start=(kc == 0), stop=False)
                        last = t.matmul(PS[bank].ap()[:, 0:512], CONC.ap()[0:1, :],
                                        brow("rowA")[0:1, ofb * 512:(ofb + 1) * 512],
                                        start=False, stop=True)
                        inc(last, "pmm", 1)
                        i += 1

    def l0_attn():
        vq = view(B2, DC, TH)
        vE = view(B3, DC, TH)
        vvt = view(B4, DC, TH)
        vgs = view(B5, DC, TH)
        vP = view(B1, DC, TH)
        for half in range(2):
            t0 = half * TH
            xsl = XRES.ap()[:, :, t0:t0 + TH]
            ln_1024(xsl, None, vP, view(B2, DC, TH), view(B3, DC, TH),
                    pf=(wq, 1024, DC), xw=[("dmx", 16)] if half == 1 else None)
            dump("ln0", B1)

            def ev_q(e, ps, ofc, ts, te):
                return e.activation(out=vq[:, ofc, ts:te], in_=ps, func=AF.Silu)
            proj_fp32(wq, 1024, vP, TH, ev_q, evac_eng="s", pre=2,
                      pf_next=(wg1, 128, DC))
            dump("q", B2)

            def ev_g1(e, ps, ofc, ts, te):
                return e.tensor_copy(out=SB2.ap()[:, ts:te], in_=ps)
            proj_fp32(wg1, 128, vP, TH, ev_g1, evac_eng="v", pre=1,
                      pf_next=(wg2, 1024, 1))

            def ev_g2(e, ps, ofc, ts, te):
                return e.activation(out=vgs[:, ofc, ts:te], in_=ps, func=AF.Silu)
            g1v = SB2.ap()[:, 0:TH].rearrange("p (a t) -> p a t", a=1)
            proj_fp32(wg2, 1024, g1v, TH, ev_g2, evac_eng="s", kc_n=1, pre=1,
                      pf_next=(wk, 1024, DC))

            def ev_k(e, ps, ofc, ts, te):
                return e.activation(out=vE[:, ofc, ts:te], in_=ps, func=AF.Exp)
            proj_fp32(wk, 1024, vP, TH, ev_k, evac_eng="s", pre=2,
                      pf_next=(wv, 1024, DC))
            dump("E", B3)

            b_mm = val("pmm"); b_ev = val("pev"); b_wv = val("pwv")
            b_p = [val("dmp0") - 16, val("dmp1") - 16]
            with nc.Block("gla_v") as blk:
                @blk.sync
                def _(sy):
                    for u in range(2, 4):
                        sy.wait_ge(S("pwv"), b_wv + u - 1)
                        src = wv[:, u * 256:(u + 1) * 256].rearrange(
                            "(kc p) of -> p kc of", p=P)
                        inc(sy.dma_start(out=view(WS, 2, DC, 256)[:, u % 2],
                                         in_=src), f"dmp{u % 2}", 16)
                    pf_emit(sy, wo, 1024, DC, war=[b_wv + 3, b_wv + 4])

                @blk.vector
                def _(v):
                    for u in range(4):
                        v.wait_ge(S(f"dmp{u % 2}"), b_p[u % 2] + 16 * (u // 2 + 1))
                        v.tensor_copy(out=view(WB, DC, 1024)[:, :, u * 256:(u + 1) * 256],
                                      in_=view(WS, 2, DC, 256)[:, u % 2])
                        inc(v.drain(), "pwv", 1)
                    i = 0
                    for ofb in range(2):
                        for tt in range(8):
                            bank = i % 2
                            v.wait_ge(S("pmm"), b_mm + i + 1)
                            ei = v.tensor_copy(out=vvt[:, tt, ofb * 512:(ofb + 1) * 512],
                                               in_=PS[bank].ap()[:, 0:512])
                            inc(ei, "pev", 1)
                            i += 1

                @blk.tensor
                def _(t):
                    i = 0
                    for ofb in range(2):
                        t.wait_ge(S("pwv"), b_wv + 2 * (ofb + 1))
                        for tt in range(8):
                            bank = i % 2
                            if i >= 2:
                                t.wait_ge(S("pev"), b_ev + i - 1)
                            last = None
                            for kc in range(DC):
                                last = t.matmul(PS[bank].ap()[:, 0:512],
                                                vP[:, kc, tt * 128:(tt + 1) * 128],
                                                view(WB, DC, 1024)[:, kc, ofb * 512:(ofb + 1) * 512],
                                                start=(kc == 0), stop=(kc == DC - 1))
                            inc(last, "pmm", 1)
                            i += 1

            dump("v", B4)
            dump("gsw", B5)

            with nc.Block("gla_fac") as blk:
                @blk.vector
                def _(v, half=half):
                    v.wait_ge(S("pev"), val("pev"))
                    for h in range(DC):
                        init = 0.0 if half == 0 else sm("Pcar")[:, h:h + 1]
                        v.tensor_tensor_scan(out=vP[:, h, :], data0=vE[:, h, :],
                                             data1=vE[:, h, :], initial=init,
                                             op0=ALU.add, op1=ALU.bypass)
                    v.drain()
                    pbp4 = sm("Pbp", 64).rearrange("p (h n) -> p h n", h=8).unsqueeze(3)
                    pls4 = sm("Pls", 64).rearrange("p (h n) -> p h n", h=8).unsqueeze(3)
                    vP4c = view(B1, DC, 8, 128)
                    v.tensor_copy(out=pls4, in_=vP4c[:, :, :, 127:128])
                    v.tensor_copy(out=pbp4[:, :, 1:8, :], in_=vP4c[:, :, 0:7, 127:128])
                    if half == 0:
                        v.tensor_copy(out=pbp4[:, :, 0:1, :], in_=vP4c[:, :, 0:1, 0:1])
                    else:
                        v.tensor_copy(out=pbp4[:, :, 0:1, :],
                                      in_=sm("Pcar").unsqueeze(2).unsqueeze(3))
                    v.drain()
                    v.tensor_copy(out=sm("Pcar").unsqueeze(2).unsqueeze(3),
                                  in_=vP4c[:, :, 7:8, 127:128])
                    v.drain()
                    v.reciprocal(out=sm("rPbp", 64), in_=sm("Pbp", 64))
                    v.reciprocal(out=sm("rPls", 64), in_=sm("Pls", 64))
                    v.drain()
                    v.tensor_mul(out=sm("ebl", 64), in0=sm("Pbp", 64), in1=sm("rPls", 64))
                    inc(v.drain(), "gv", 1)

                @blk.scalar
                def _(s):
                    s.wait_ge(S("gv"), val("gv"))
                    s.activation(out=vP, in_=vP, func=AF.Ln)
                    s.activation(out=sm("lnPbp", 64), in_=sm("Pbp", 64), func=AF.Ln)
                    inc(s.drain(), "gs", 1)

                @blk.vector
                def _(v):
                    v.wait_ge(S("gs"), val("gs"))
                    lnpbp = sm("lnPbp", 64).rearrange("p (h n) -> p h n", h=8)
                    rep = lnpbp.unsqueeze(3).broadcast_to([P, 8, 8, 128])
                    vP4 = view(B1, DC, 8, 128)
                    v.tensor_sub(out=vP4, in0=rep, in1=vP4)
                    inc(v.drain(), "gv", 1)

                @blk.scalar
                def _(s):
                    s.wait_ge(S("gv"), val("gv"))
                    s.activation(out=vP, in_=vP, func=AF.Exp)
                    inc(s.drain(), "gs", 1)

                @blk.vector
                def _(v, half=half):
                    v.wait_ge(S("gs"), val("gs"))
                    v.tensor_mul(out=vq, in0=vq, in1=vP)
                    rpb = sm("rPbp", 64).rearrange("p (h n) -> p h n", h=8)
                    rep = rpb.unsqueeze(3).broadcast_to([P, 8, 8, 128])
                    vE4 = view(B3, DC, 8, 128)
                    v.tensor_mul(out=vE4, in0=vE4, in1=rep)
                    if half == 0:
                        v.memset(SB2.ap()[:, 1024:2048], 0.0)
                    inc(v.drain(), "gv", 1)

            dump("fac", B1)
            dump("q2", B2)
            dump("k2", B3)

            etm = SB2.ap()[:, 0:1024].rearrange("p (h d) -> p h d", h=8)
            vq4o = view(B2, DC, 8, 128)
            sbf = SB2.ap()[:, 1024:2048].rearrange("p (h d) -> p h d", h=8)
            amask = view(WB, 8, 128)
            ebl = sm("ebl", 64)
            b_mm = val("g_mm"); b_tr = val("g_tr"); b_ds = val("g_ds")
            b_om = val("g_om"); b_ae = val("g_ae"); b_te = val("g_te")
            b_su = val("g_su"); b_oe = val("g_oe")
            with nc.Block("gla_chunks") as blk:
                @blk.tensor
                def _(t):
                    t.wait_ge(S("gv"), val("gv"))
                    for n in range(8):
                        nsl = slice(n * 128, (n + 1) * 128)
                        t.wait_ge(S("g_ae"), b_ae + 2 * n)
                        last = None
                        for h in range(8):
                            last = t.matmul(PS[h // 4].ap()[:, (h % 4) * 128:(h % 4 + 1) * 128],
                                            vE[:, h, nsl], vq[:, h, nsl],
                                            start=True, stop=True)
                        inc(last, "g_mm", 1)
                        t.wait_ge(S("g_te"), b_te + n)
                        for h in range(8):
                            last = t.transpose(psb(2)[:, h * 128:(h + 1) * 128],
                                               vE[:, h, nsl], CID.ap())
                        inc(last, "g_tr", 1)
                        t.wait_ge(S("g_su"), b_su + n)
                        t.wait_ge(S("g_te"), b_te + n + 1)
                        for h in range(8):
                            last = t.matmul(PS[3 + h // 4].ap()[:, (h % 4) * 128:(h % 4 + 1) * 128],
                                            etm[:, h, :], vvt[:, n, h * 128:(h + 1) * 128],
                                            start=True, stop=True)
                        inc(last, "g_ds", 1)
                        t.wait_ge(S("g_ae"), b_ae + 2 * (n + 1))
                        t.wait_ge(S("g_oe"), b_oe + n)
                        for h in range(8):
                            reg = PS[5 + h // 4].ap()[:, (h % 4) * 128:(h % 4 + 1) * 128]
                            t.matmul(reg, sbf[:, h, :], vq[:, h, nsl], start=True, stop=False)
                            last = t.matmul(reg, vvt[:, n, h * 128:(h + 1) * 128],
                                            amask[:, h, :], start=False, stop=True)
                        inc(last, "g_om", 1)

                @blk.vector
                def _(v):
                    for n in range(8):
                        v.wait_ge(S("g_mm"), b_mm + n + 1)
                        v.wait_ge(S("g_om"), b_om + n)
                        for g in range(2):
                            v.tensor_mul(
                                out=amask[:, 4 * g:4 * (g + 1), :].rearrange("p a b -> p (a b)"),
                                in0=PS[g].ap()[:, 0:512], in1=CGM.ap())
                        inc(v.drain(), "g_ae", 2)
                        v.wait_ge(S("g_ds"), b_ds + n + 1)
                        v.wait_ge(S("g_om"), b_om + n + 1)
                        for g in range(2):
                            v.tensor_add(out=SGLA.ap().rearrange("p h d -> p (h d)")[:, g * 512:(g + 1) * 512],
                                         in0=PS[3 + g].ap()[:, 0:512],
                                         in1=SGLA.ap().rearrange("p h d -> p (h d)")[:, g * 512:(g + 1) * 512])
                        v.drain()
                        for h in range(8):
                            gi = h * 8 + n
                            v.tensor_scalar_mul(out=SGLA.ap()[:, h, :], in0=SGLA.ap()[:, h, :],
                                                scalar1=ebl[:, gi:gi + 1])
                        v.drain()
                        v.tensor_copy(out=SB2.ap()[:, 1024:2048],
                                      in_=SGLA.ap().rearrange("p h d -> p (h d)"))
                        inc(v.drain(), "g_su", 1)

                @blk.scalar
                def _(s):
                    for n in range(8):
                        nsl = slice(n * 128, (n + 1) * 128)
                        s.wait_ge(S("g_tr"), b_tr + n + 1)
                        s.wait_ge(S("g_ds"), b_ds + n)
                        s.copy(out=SB2.ap()[:, 0:1024], in_=psb(2)[:, 0:1024])
                        inc(s.drain(), "g_te", 1)
                        s.wait_ge(S("g_om"), b_om + n + 1)
                        ei = s.activation(out=vq[:, :, nsl].transpose([0, 2, 1]) if False
                                          else vq4o[:, :, n, :],
                                          in_=psw(5, 2), func=AF.Copy)
                        inc(ei, "g_oe", 1)

            dump("o", B2)
            dump("sm", SM, f32=True)
            dump("S", SGLA, f32=True)
            dump("am", WB)
            dump("sb2", SB2)

            vosq = view(B3, DC, TH)
            with nc.Block("gla_rms") as blk:
                @blk.scalar
                def _(s):
                    s.wait_ge(S("g_oe"), val("g_oe"))
                    s.activation(out=vosq, in_=vq, func=AF.Square)
                    inc(s.drain(), "gs", 1)

                @blk.tensor
                def _(t):
                    t.wait_ge(S("gs"), val("gs"))
                    for b in range(2):
                        for c in range(DC):
                            t.matmul(PS[0 + b].ap()[0:1, 0:512], CSUM.ap(),
                                     vosq[:, c, b * 512:(b + 1) * 512],
                                     start=(c == 0), stop=(c == DC - 1))
                    inc(t.drain(), "gt", 1)

                @blk.scalar
                def _(s):
                    s.wait_ge(S("gt"), val("gt"))
                    rr = RSTG.ap()[0:1, 2048:3072]
                    for b in range(2):
                        s.activation(out=rr[:, b * 512:(b + 1) * 512],
                                     in_=PS[b].ap()[0:1, 0:512], func=AF.Ln,
                                     scale=1.0 / 1024.0, bias=LN_EPS)
                    s.drain()
                    s.activation(out=rr, in_=rr, func=AF.Exp, scale=-0.5)
                    inc(s.drain(), "gs", 1)

                @blk.vector
                def _(v):
                    v.wait_ge(S("gs"), val("gs"))
                    ab = RSTG.ap().bitcast(BF16)[0:1, 0:1024]
                    v.tensor_copy(out=ab, in_=RSTG.ap()[0:1, 2048:3072])
                    inc(v.drain(), "gv", 1)

                @blk.tensor
                def _(t):
                    t.wait_ge(S("gv"), val("gv"))
                    ab = RSTG.ap().bitcast(BF16)[0:1, 0:1024]
                    for b in range(2):
                        t.matmul(PS[2 + b].ap()[:, 0:512], CONC.ap()[0:1, :],
                                 ab[:, b * 512:(b + 1) * 512], start=True, stop=True)
                    inc(t.drain(), "gt", 1)

                @blk.vector
                def _(v):
                    v.wait_ge(S("gt"), val("gt"))
                    for b in range(2):
                        c0, c1 = b * 512, (b + 1) * 512
                        rbc = PS[2 + b].ap()[:, 0:512].unsqueeze(1).broadcast_to([P, DC, 512])
                        v.tensor_mul(out=vosq[:, :, c0:c1], in0=vq[:, :, c0:c1], in1=rbc)
                    v.drain()
                    v.tensor_mul(out=vosq, in0=vosq, in1=vgs)
                    inc(v.drain(), "gv", 1)

            def ev_wo(e, ps, ofc, ts, te):
                return e.tensor_add(out=XRES.ap()[:, ofc, t0 + ts:t0 + te],
                                    in0=ps, in1=XRES.ap()[:, ofc, t0 + ts:t0 + te])
            proj_fp32(wo, 1024, vosq, TH, ev_wo, evac_eng="v", scale_cols=sm("gn"),
                      pre=2)

    def l0_ffn():
        vln = view(B1, DC, TH)

        def h_ap(hc):
            buf = [B2, B3, B4][hc // 8]
            return view(buf, 8, TH)[:, hc % 8, :]

        for half in range(2):
            t0 = half * TH
            xsl = XRES.ap()[:, :, t0:t0 + TH]
            def pf_gu(sy):
                for hc in range(2):
                    for wi, wt in enumerate([w0g, w0u]):
                        src = wt[:, hc * 128:(hc + 1) * 128].rearrange(
                            "(kc p) of -> p kc of", p=P)
                        inc(sy.dma_start(
                            out=view(WS, 2, 2, DC, 128)[:, hc, wi].bitcast(I32),
                            in_=src), f"dmp{hc}", 16)
            ln_1024(xsl, None, vln, view(B2, DC, TH), view(B3, DC, TH), pf=pf_gu)

            b_wv = val("fwv"); b_mm = val("fmm")
            b_sl = val("fsl"); b_ev = val("fev")
            b_p = [val("dmp0") - 32, val("dmp1") - 32]
            with nc.Block("l0f_gu") as blk:
                @blk.sync
                def _(sy):
                    for hc in range(2, 24):
                        if hc >= 2:
                            sy.wait_ge(S("fmm"), b_mm + 4 * (hc - 1))
                        for wi, wt in enumerate([w0g, w0u]):
                            src = wt[:, hc * 128:(hc + 1) * 128].rearrange(
                                "(kc p) of -> p kc of", p=P)
                            inc(sy.dma_start(
                                out=view(WS, 2, 2, DC, 128)[:, hc % 2, wi].bitcast(I32),
                                in_=src), f"dmp{hc % 2}", 16)
                    for kc2 in range(2):
                        sy.wait_ge(S("fwv"), b_wv + 23 + kc2)
                        src = w0d[kc2 * 128:(kc2 + 1) * 128, 0:512]
                        inc(sy.dma_start(out=view(WS, 2, 512)[:, kc2, :].bitcast(I32),
                                         in_=src), f"dmp{kc2}", 16)

                @blk.vector
                def _(v):
                    for i in range(25):
                        if i < 24:
                            hc = i
                            v.wait_ge(S(f"dmp{hc % 2}"), b_p[hc % 2] + 32 * (hc // 2 + 1))
                            if hc >= 2:
                                v.wait_ge(S("fmm"), b_mm + 4 * (hc - 1))
                            for wi in range(2):
                                v.tensor_copy(out=view(WB, 2, 2, DC, 128)[:, hc % 2, wi],
                                              in_=view(WS, 2, 2, DC, 128)[:, hc % 2, wi].bitcast(I32))
                            inc(v.drain(), "fwv", 1)
                        if i >= 1:
                            hc = i - 1
                            for tb in range(2):
                                v.wait_ge(S("fsl"), b_sl + 2 * hc + tb + 1)
                                v.wait_ge(S("fmm"), b_mm + 4 * hc + 2 + tb + 1)
                                ei = v.tensor_mul(out=h_ap(hc)[:, tb * 512:(tb + 1) * 512],
                                                  in0=PS[4 * (hc % 2) + 2 + tb].ap()[:, 0:512],
                                                  in1=SB2.ap()[:, (2 * (hc % 2) + tb) * 512:(2 * (hc % 2) + tb + 1) * 512])
                                inc(ei, "fev", 1)

                @blk.tensor
                def _(t):
                    for hc in range(24):
                        t.wait_ge(S("fwv"), b_wv + hc + 1)
                        if hc >= 2:
                            t.wait_ge(S("fsl"), b_sl + 2 * (hc - 1))
                            t.wait_ge(S("fev"), b_ev + 2 * (hc - 1))
                        last = None
                        for wi in range(2):
                            for tb in range(2):
                                bank = 4 * (hc % 2) + 2 * wi + tb
                                for kc in range(DC):
                                    last = t.matmul(PS[bank].ap()[:, 0:512],
                                                    view(WB, 2, 2, DC, 128)[:, hc % 2, wi, kc, :],
                                                    vln[:, kc, tb * 512:(tb + 1) * 512],
                                                    start=(kc == 0), stop=(kc == DC - 1))
                                inc(last, "fmm", 1)

                @blk.scalar
                def _(s):
                    for hc in range(24):
                        for tb in range(2):
                            s.wait_ge(S("fmm"), b_mm + 4 * hc + tb + 1)
                            s.activation(out=SB2.ap()[:, (2 * (hc % 2) + tb) * 512:(2 * (hc % 2) + tb + 1) * 512],
                                         in_=PS[4 * (hc % 2) + tb].ap()[:, 0:512], func=AF.Silu,
                                         scale=QS)
                            inc(s.drain(), "fsl", 1)


            for ofh in range(2):
                def wd_ap(kc):
                    if kc < 16:
                        return view(B1, 16, 512)[:, kc, :]
                    return view(WB, 8, 512)[:, kc - 16, :]
                b_wv = val("fwv"); b_mm = val("fmm")
                b_p = [val("dmp0") - 16, val("dmp1") - 16]
                with nc.Block("l0f_dn") as blk:
                    @blk.sync
                    def _(sy, ofh=ofh):
                        for kc in range(2, 24):
                            sy.wait_ge(S("fwv"), b_wv + kc - 1)
                            src = w0d[kc * 128:(kc + 1) * 128, ofh * 512:(ofh + 1) * 512]
                            inc(sy.dma_start(out=view(WS, 2, 512)[:, kc % 2, :].bitcast(I32),
                                             in_=src), f"dmp{kc % 2}", 16)
                        if ofh == 0:
                            for kc2 in range(2):
                                sy.wait_ge(S("fwv"), b_wv + 23 + kc2)
                                src = w0d[kc2 * 128:(kc2 + 1) * 128, 512:1024]
                                inc(sy.dma_start(out=view(WS, 2, 512)[:, kc2, :].bitcast(I32),
                                                 in_=src), f"dmp{kc2}", 16)

                    @blk.vector
                    def _(v):
                        for kc in range(24):
                            v.wait_ge(S(f"dmp{kc % 2}"), b_p[kc % 2] + 16 * (kc // 2 + 1))
                            v.tensor_copy(out=wd_ap(kc),
                                          in_=view(WS, 2, 512)[:, kc % 2, :].bitcast(I32))
                            inc(v.drain(), "fwv", 1)
                        for tb in range(2):
                            v.wait_ge(S("fmm"), b_mm + tb + 1)
                            ei = None
                            for j in range(4):
                                ofc = ofh * 4 + j
                                dst = XRES.ap()[:, ofc, t0 + tb * 512:t0 + (tb + 1) * 512]
                                ei = v.scalar_tensor_tensor(out=dst,
                                                            in0=PS[4 * tb + j].ap()[:, 0:512],
                                                            scalar=QS * QS, in1=dst,
                                                            op0=ALU.mult, op1=ALU.add)
                            inc(ei, "fev", 1)

                    @blk.tensor
                    def _(t):
                        for tb in range(2):
                            last = None
                            for kc in range(24):
                                if tb == 0:
                                    t.wait_ge(S("fwv"), b_wv + kc + 1)
                                for j in range(4):
                                    last = t.matmul(PS[4 * tb + j].ap()[:, 0:512],
                                                    wd_ap(kc)[:, j * 128:(j + 1) * 128],
                                                    h_ap(kc)[:, tb * 512:(tb + 1) * 512],
                                                    start=(kc == 0), stop=(kc == 23))
                            inc(last, "fmm", 1)

    def l1_attn():
        with nc.Block("l1_fw") as blk:
            @blk.sync
            def _(sy):
                inc(sy.dma_start(out=RSTG.ap()[0:1, 0:2], in_=fusw), "dm", 16)

            @blk.scalar
            def _(s):
                s.wait_ge(S("dm"), val("dm"))
                s.activation(out=RSTG.ap()[0:1, 1024:1026], in_=RSTG.ap()[0:1, 0:2], func=AF.Exp)
                inc(s.drain(), "gs", 1)

            @blk.vector
            def _(v):
                v.wait_ge(S("gs"), val("gs"))
                v.reduce_sum(out=RSTG.ap()[0:1, 1026:1027], in_=RSTG.ap()[0:1, 1024:1026], axis=AX.X)
                v.drain()
                v.reciprocal(out=RSTG.ap()[0:1, 1027:1028], in_=RSTG.ap()[0:1, 1026:1027])
                v.drain()
                v.tensor_scalar_mul(out=RSTG.ap().bitcast(BF16)[0:1, 0:2],
                                    in0=RSTG.ap()[0:1, 1024:1026],
                                    scalar1=RSTG.ap()[0:1, 1027:1028])
                inc(v.drain(), "gv", 1)

            @blk.tensor
            def _(t):
                t.wait_ge(S("gv"), val("gv"))
                wb2 = RSTG.ap().bitcast(BF16)[0:1, 0:2]
                t.matmul(PS[0].ap()[:, 0:1], CONC.ap()[0:1, :], wb2[:, 0:1],
                         start=True, stop=True)
                t.matmul(PS[0].ap()[:, 1:2], CONC.ap()[0:1, :], wb2[:, 1:2],
                         start=True, stop=True)
                inc(t.drain(), "gt", 1)

            @blk.vector
            def _(v):
                v.wait_ge(S("gt"), val("gt"))
                v.tensor_copy(out=sm("w0", 1), in_=PS[0].ap()[:, 0:1])
                v.tensor_copy(out=sm("w1", 1), in_=PS[0].ap()[:, 1:2])
                inc(v.drain(), "gv", 1)

        for w in range(2):
            t0 = w * TH
            xsl = XRES.ap()[:, :, t0:t0 + TH]
            vln2 = view(B1, DC, TH)
            ln_1024(xsl, None, vln2, view(B3, DC, TH), view(B4, DC, TH),
                    pf=(pl, 1024, DC))
            dump("l1ln", B1)

            def ev_cp(dst):
                def f(e, ps, ofc, ts, te):
                    return e.tensor_copy(out=dst[:, ofc, ts:te], in_=ps)
                return f

            vlqs = view(B2, DC, TH)
            proj_fp32(pl, 1024, vln2, TH, ev_cp(vlqs), evac_eng="v", pre=2,
                      pf_next=(wlin[:, 0:1024], 1024, DC))
            dump("lqs", B2)

            vlq = view(B3, DC, TH); vlk = view(B4, DC, TH)
            vlv = view(B5, 8, 16, 65)
            def ev_q8(e, ps, ofc, ts, te):
                return e.activation(out=vlq[:, ofc, ts:te], in_=ps, func=AF.Identity,
                                    scale=0.125, bias=sm("blq8")[:, ofc:ofc + 1])
            proj_fp32(wlin[:, 0:1024], 1024, vlqs, TH, ev_q8, evac_eng="s", pre=2,
                      pf_next=(wlin[:, 1024:2048], 1024, DC))
            def ev_kb(e, ps, ofc, ts, te):
                return e.activation(out=vlk[:, ofc, ts:te], in_=ps, func=AF.Identity,
                                    bias=sm("blk")[:, ofc:ofc + 1])
            proj_fp32(wlin[:, 1024:2048], 1024, vlqs, TH, ev_kb, evac_eng="s", pre=2,
                      pf_next=(wlin[:, 2048:3072], 1024, DC))
            with nc.Block("l1_lvo") as blk:
                @blk.vector
                def _(v):
                    v.memset(vlv[:, :, :, 64:65], 1.0)
                    inc(v.drain(), "gv", 1)
            tm_proj(vlqs, vlv, blv_row, wlin[:, 2048:3072], pre=2,
                    pf_next=(wlout, 1024, DC))
            dump("lq", B3)
            dump("lk", B4)
            dump("lv", B5)

            vlo = view(B2, DC, TH)
            b_sc = val("l_sc"); b_se = val("l_se"); b_av = val("l_av")
            b_nr = val("l_nr"); b_rb = val("l_rb"); b_no = val("l_no")
            b_dm = val("dm"); b_st = val("l_st")
            with nc.Block("l1_local") as blk:
                @blk.tensor
                def _(t):
                    t.wait_ge(S("pev"), val("pev"))
                    for lw in range(8):
                        lsl = slice(lw * 128, (lw + 1) * 128)
                        t.wait_ge(S("l_no"), b_no + lw * 4)
                        last = None
                        for h in range(16):
                            r0 = (h % 2) * 64
                            reg = PS[h // 4].ap()[:, (h % 4) * 128:(h % 4 + 1) * 128]
                            t.matmul(reg, vlk[r0:r0 + 64, h // 2, lsl],
                                     vlq[r0:r0 + 64, h // 2, lsl], start=True, stop=False)
                            last = t.matmul(reg, CID.ap(), CL1.ap(), start=False, stop=True)
                            if h % 4 == 3:
                                inc(last, "l_sc", 1)
                        t.wait_ge(S("l_se"), b_se + lw * 4 + 4)
                        for h in range(16):
                            reg = PS[4 + h // 4].ap()[:, (h % 4) * 128:(h % 4 + 1) * 128]
                            last = t.matmul(reg[0:65, :], vlv[:, lw, h, :],
                                            SB2.ap()[:, h * 128:(h + 1) * 128],
                                            start=True, stop=True)
                            if h % 4 == 3:
                                inc(last, "l_av", 1)
                        t.wait_ge(S("l_nr"), b_nr + lw + 1)
                        last = None
                        for g in range(4):
                            last = t.matmul(PS[g].ap()[0:64, 0:512], CONB.ap()[64:65, 0:64],
                                            SB2.ap()[64:65, g * 512:(g + 1) * 512],
                                            start=True, stop=True)
                        inc(last, "l_rb", 1)

                @blk.scalar
                def _(s):
                    for lw in range(8):
                        s.wait_ge(S("dm"), b_dm + 16 * lw)
                        s.wait_ge(S("l_sc"), b_sc + lw * 4 + 2)
                        s.activation(out=SB2.ap()[:, 0:1024], in_=psw(0, 2), func=AF.Exp)
                        s.wait_ge(S("l_sc"), b_sc + lw * 4 + 4)
                        s.activation(out=SB2.ap()[:, 1024:2048], in_=psw(2, 2), func=AF.Exp)
                        inc(s.drain(), "l_se", 4)
                        s.wait_ge(S("l_av"), b_av + lw * 4 + 4)
                        for g in range(4):
                            s.activation(out=SB2.ap()[64:65, g * 512:(g + 1) * 512],
                                         in_=PS[4 + g].ap()[64:65, 0:512], func=AF.Ln)
                        s.drain()
                        with nc.allow_low_precision(reason="softmax recip row bf16"):
                            s.activation(out=SB2.ap()[64:65, 0:2048],
                                         in_=SB2.ap()[64:65, 0:2048], func=AF.Exp,
                                         scale=-1.0)
                        inc(s.drain(), "l_nr", 1)

                @blk.vector
                def _(v):
                    wrbc = view(WB, 2048)
                    for lw in range(8):
                        v.wait_ge(S("l_rb"), b_rb + lw + 1)
                        v.wait_ge(S("dm"), b_dm + 16 * lw)
                        for g in range(4):
                            v.tensor_copy(out=wrbc[0:64, g * 512:(g + 1) * 512],
                                          in_=PS[g].ap()[0:64, 0:512])
                        v.drain()
                        ei = None
                        for h in range(16):
                            g = h // 4
                            src = PS[4 + g].ap()[0:64, (h % 4) * 128:(h % 4 + 1) * 128]
                            rb = wrbc[0:64, h * 128:(h + 1) * 128]
                            if h % 2 == 0:
                                dst = vlo[0:64, h // 2, lw * 128:(lw + 1) * 128]
                            else:
                                dst = SB2.ap()[0:64, (h // 2) * 128:(h // 2 + 1) * 128]
                            ei = v.tensor_mul(out=dst, in0=src, in1=rb)
                        inc(ei, "l_no", 4)
                        inc(v.drain(), "l_st", 1)

                @blk.sync
                def _(sy):
                    vlo4 = view(B2, DC, 8, 128)
                    stg = SB2.ap()[0:64, 0:1024].rearrange("p (c t) -> p c t", c=8)
                    for lw in range(8):
                        sy.wait_ge(S("l_st"), b_st + lw + 1)
                        inc(sy.dma_start(out=vlo4[64:128, :, lw, :], in_=stg), "dm", 16)

            dump("lo", B2)
            vfus = view(B3, DC, TH)

            def ev_lo(e, ps, ofc, ts, te):
                return e.activation(out=vfus[:, ofc, ts:te], in_=ps, func=AF.Identity,
                                    scale=sm("w0", 1))
            proj_fp32(wlout, 1024, vlo, TH, ev_lo, evac_eng="s", bias_row=brow("rowB"),
                      pre=2, pf_next=(pg, 1024, DC), row_load=("rowB", blout_row))

            dump("fusl", B3)
            vgqs = view(B4, DC, TH)
            proj_fp32(pg, 1024, vln2, TH, ev_cp(vgqs), evac_eng="v", pre=2,
                      pf_next=(wgin[:, 0:1024], 1024, DC))

            vgq = view(B1, DC, TH); vgk = view(B2, DC, TH)
            vgv = view(B5, 8, 16, 65)
            def ev_gq(e, ps, ofc, ts, te):
                return e.activation(out=vgq[:, ofc, ts:te], in_=ps, func=AF.Identity,
                                    scale=0.125, bias=sm("bgq8")[:, ofc:ofc + 1])
            proj_fp32(wgin[:, 0:1024], 1024, vgqs, TH, ev_gq, evac_eng="s", pre=2,
                      pf_next=(wgin[:, 1024:2048], 1024, DC))
            def ev_gk(e, ps, ofc, ts, te):
                return e.activation(out=vgk[:, ofc, ts:te], in_=ps, func=AF.Identity,
                                    bias=sm("bgk")[:, ofc:ofc + 1])
            proj_fp32(wgin[:, 1024:2048], 1024, vgqs, TH, ev_gk, evac_eng="s", pre=2,
                      pf_next=(wgin[:, 2048:3072], 1024, DC))
            with nc.Block("l1_gvo") as blk:
                @blk.vector
                def _(v):
                    v.memset(vgv[:, :, :, 64:65], 1.0)
                    inc(v.drain(), "gv", 1)
            tm_proj(vgqs, vgv, bgv_row, wgin[:, 2048:3072], pre=2,
                    pf_next=(wgout, 1024, DC))

            vet = view(B4, 8, TH)
            b_gsc = val("ga_sc"); b_gse = val("ga_se"); b_gav = val("ga_av")
            b_gnr = val("ga_nr"); b_grb = val("ga_rb"); b_gno = val("ga_no")
            b_gdm = val("dm"); b_gst = val("ga_st")
            with nc.Block("l1_global") as blk:
                @blk.tensor
                def _(t):
                    t.wait_ge(S("pev"), val("pev"))
                    for h in range(16):
                        c = h // 2
                        r0 = (h % 2) * 64

                        def emit_av(jt):
                            last = None
                            for ih in range(2):
                                last = t.matmul(PS[2 + ih].ap()[0:65, 0:512],
                                                vgv[:, jt, h, :],
                                                vet[:, jt, ih * 512:(ih + 1) * 512],
                                                start=(jt == 0), stop=(jt == 7))
                            return last

                        t.wait_ge(S("ga_no"), b_gno + h)
                        for jt in range(8):
                            gidx = h * 8 + jt
                            bank = (jt % 2) * 6
                            if gidx >= 2:
                                t.wait_ge(S("ga_se"), b_gse + (gidx - 1) * 2)
                            last = None
                            for ih in range(2):
                                reg = PS[bank + ih].ap()[:, 0:512]
                                ops = [("sc", None)]
                                dlo = jt * 128
                                s0, s1 = ih * 512, (ih + 1) * 512
                                if dlo > s0:
                                    ops.append(("full", min(dlo, s1) - s0))
                                if dlo < s1 and dlo + 128 > s0:
                                    ops.append(("diag", dlo - s0))
                                nops = len(ops)
                                for oi, (kind, arg) in enumerate(ops):
                                    st = (oi == 0)
                                    sp = (oi == nops - 1)
                                    if kind == "sc":
                                        last = t.matmul(reg,
                                                        vgk[r0:r0 + 64, c, jt * 128:(jt + 1) * 128],
                                                        vgq[r0:r0 + 64, c, ih * 512:(ih + 1) * 512],
                                                        start=st, stop=sp)
                                    elif kind == "full":
                                        last = t.matmul(reg[:, 0:arg], CONC.ap()[0:1, :],
                                                        CONR.ap()[0:1, 0:arg],
                                                        start=st, stop=sp)
                                    else:
                                        last = t.matmul(reg[:, arg:arg + 128], CID.ap(),
                                                        CL1.ap(), start=st, stop=sp)
                            inc(last, "ga_sc", 1)
                        t.wait_ge(S("ga_se"), b_gse + (h * 8 + 8) * 2)
                        for jt in range(8):
                            lav = emit_av(jt)
                        inc(lav, "ga_av", 2)
                        t.wait_ge(S("ga_nr"), b_gnr + h + 1)
                        last = None
                        for ih in range(2):
                            last = t.matmul(PS[4 + ih].ap()[0:64, 0:512], CONB.ap()[64:65, 0:64],
                                            SB2.ap()[64:65, ih * 512:(ih + 1) * 512],
                                            start=True, stop=True)
                        inc(last, "ga_rb", 1)

                @blk.scalar
                def _(s):
                    for h in range(16):
                        for jt in range(8):
                            gidx = h * 8 + jt
                            bank = (jt % 2) * 6
                            s.wait_ge(S("ga_sc"), b_gsc + gidx + 1)
                            s.activation(out=vet[:, jt, 0:1024],
                                         in_=psw(bank, 2), func=AF.Exp)
                            inc(s.drain(), "ga_se", 2)
                        s.wait_ge(S("ga_av"), b_gav + h * 2 + 2)
                        for ih in range(2):
                            s.activation(out=SB2.ap()[64:65, ih * 512:(ih + 1) * 512],
                                         in_=PS[2 + ih].ap()[64:65, 0:512], func=AF.Ln)
                        s.drain()
                        with nc.allow_low_precision(reason="softmax recip row bf16"):
                            s.activation(out=SB2.ap()[64:65, 0:1024],
                                         in_=SB2.ap()[64:65, 0:1024], func=AF.Exp,
                                         scale=-1.0)
                        inc(s.drain(), "ga_nr", 1)

                @blk.vector
                def _(v):
                    wrbc = view(WB, 2048)
                    for h in range(16):
                        c = h // 2
                        v.wait_ge(S("ga_rb"), b_grb + h + 1)
                        if h % 2 == 1:
                            v.wait_ge(S("dm"), b_gdm + 16 * (h // 2))
                        for ih in range(2):
                            v.tensor_copy(out=wrbc[0:64, ih * 512:(ih + 1) * 512],
                                          in_=PS[4 + ih].ap()[0:64, 0:512])
                        v.drain()
                        ei = None
                        for ih in range(2):
                            if h % 2 == 0:
                                dst = vgq[0:64, c, ih * 512:(ih + 1) * 512]
                            else:
                                dst = SB2.ap()[0:64, ih * 512:(ih + 1) * 512]
                            ei = v.tensor_mul(out=dst,
                                              in0=PS[2 + ih].ap()[0:64, 0:512],
                                              in1=wrbc[0:64, ih * 512:(ih + 1) * 512])
                        inc(ei, "ga_no", 1)
                        if h % 2 == 1:
                            inc(v.drain(), "ga_st", 1)

                @blk.sync
                def _(sy):
                    vgq4 = view(B1, DC, TH)
                    for ho in range(8):
                        sy.wait_ge(S("ga_st"), b_gst + ho + 1)
                        inc(sy.dma_start(out=vgq4[64:128, ho, :],
                                         in_=SB2.ap()[0:64, 0:1024]), "dm", 16)

            dump("go", B1)
            def ev_go(e, ps, ofc, ts, te):
                dst = vfus[:, ofc, ts:te]
                return e.scalar_tensor_tensor(out=dst, in0=ps, scalar=sm("w1", 1),
                                              in1=dst, op0=ALU.mult, op1=ALU.add)
            proj_fp32(wgout, 1024, vgq, TH, ev_go, evac_eng="v", bias_row=brow("rowB"),
                      pre=2, row_load=("rowB", bgout_row))

            dump("fusg", B3)
            vd = view(B1, DC, TH)
            ln_1024(None, vfus, vd, None, view(B4, DC, TH),
                    gamma=sm("g1"), beta=sm("b1"))
            with nc.Block("l1_res") as blk:
                @blk.vector
                def _(v):
                    v.wait_ge(S("lnv"), val("lnv"))
                    v.tensor_add(out=xsl, in0=vd, in1=xsl)
                    inc(v.drain(), "gv", 1)

    def l1_ffn():
        def u_ap(hc):
            buf = [B3, B4, B5, B1][hc // 8]
            return view(buf, 8, TH)[:, hc % 8, :]

        for half in range(2):
            t0 = half * TH
            xsl = XRES.ap()[:, :, t0:t0 + TH]
            vy = view(B1, DC, TH)
            ln_1024(xsl, None, vy, view(B3, DC, TH), view(B4, DC, TH))
            with nc.Block("l1f_res1") as blk:
                @blk.vector
                def _(v):
                    v.wait_ge(S("lnv"), val("lnv"))
                    v.tensor_add(out=xsl, in0=vy, in1=xsl)
                    inc(v.drain(), "gv", 1)
            vh = view(B2, DC, TH)

            def pf_fc1(sy):
                for hc in range(2):
                    src = fc1q[:, hc * 128:(hc + 1) * 128].rearrange(
                        "(kc p) of -> p kc of", p=P)
                    inc(sy.dma_start(out=view(WS, 2, DC, 128)[:, hc].bitcast(I32),
                                     in_=src), f"dmp{hc}", 16)
            ln_1024(None, vy, vh, None, view(B4, DC, TH),
                    gamma=sm("g2"), beta=sm("b2"), pf=pf_fc1)

            b_wv = val("fwv"); b_mm = val("fmm")
            b_sl = val("fsl"); b_sr = val("fsr"); b_gt = val("fgt")
            b_p = [val("dmp0") - 16, val("dmp1") - 16]
            with nc.Block("l1f_fc1") as blk:
                @blk.sync
                def _(sy):
                    for hc in range(32):
                        if hc >= 2:
                            sy.wait_ge(S("fwv"), b_wv + hc - 1)
                        if hc >= 2:
                            src = fc1q[:, hc * 128:(hc + 1) * 128].rearrange(
                                "(kc p) of -> p kc of", p=P)
                            inc(sy.dma_start(out=view(WS, 2, DC, 128)[:, hc % 2].bitcast(I32),
                                             in_=src), f"dmp{hc % 2}", 16)
                        inc(sy.dma_start(out=RSTG.ap()[0:1, (hc % 2) * 1024:(hc % 2) * 1024 + 1024],
                                         in_=s1_rows[hc:hc + 1, :]), f"dmp{hc % 2}", 16)
                    sy.wait_ge(S("fsr"), b_sr + 31)
                    src = fc2q[0:128, 0:256]
                    inc(sy.dma_start(out=view(WS, 4, 256)[:, 0].bitcast(I32),
                                     in_=src), "dmq0", 16)
                    inc(sy.dma_start(out=RSTG.ap()[0:1, 2048:2304],
                                     in_=s2_rows[0:1, 0:256]), "dmq0", 16)

                @blk.vector
                def _(v):
                    for i in range(33):
                        if i < 32:
                            hc = i
                            v.wait_ge(S(f"dmp{hc % 2}"), b_p[hc % 2] + 32 * (hc // 2 + 1))
                            if hc >= 2:
                                v.wait_ge(S("fmm"), b_mm + 2 * (hc - 1))
                            v.tensor_scalar_add(out=view(WB, 2, DC, 128)[:, hc % 2],
                                                in0=view(WS, 2, DC, 128)[:, hc % 2].bitcast(I32),
                                                scalar1=-128.0)
                            v.tensor_copy(out=BROW.ap()[0:1, (hc % 2) * 1024:(hc % 2) * 1024 + 1024],
                                          in_=RSTG.ap()[0:1, (hc % 2) * 1024:(hc % 2) * 1024 + 1024])
                            inc(v.drain(), "fsr", 1)
                        if i >= 1:
                            hc = i - 1
                            v.wait_ge(S("fgt"), b_gt + hc + 1)
                            wbd = view(WB, 2, 1024)[:, hc % 2]
                            v.tensor_mul(out=wbd, in0=wbd, in1=psw(6, 2)[:, 0:1024])
                            inc(v.drain(), "fwv", 1)

                @blk.tensor
                def _(t):
                    for hc in range(32):
                        t.wait_ge(S("fsr"), b_sr + hc + 1)
                        if hc >= 1:
                            t.wait_ge(S("fwv"), b_wv + hc)
                        srow = BROW.ap()[0:1, (hc % 2) * 1024:(hc % 2) * 1024 + 1024]
                        for bb in range(2):
                            t.matmul(PS[6 + bb].ap()[:, 0:512],
                                     CONC.ap()[0:1, :], srow[:, bb * 512:(bb + 1) * 512],
                                     start=True, stop=True)
                        inc(t.drain(), "fgt", 1)
                        t.wait_ge(S("fwv"), b_wv + hc + 1)
                        if hc >= 2:
                            t.wait_ge(S("fsl"), b_sl + 2 * (hc - 1))
                        last = None
                        for tb in range(2):
                            for kc in range(DC):
                                last = t.matmul(PS[2 * (hc % 2) + tb].ap()[:, 0:512],
                                                view(WB, 2, DC, 128)[:, hc % 2, kc, :],
                                                vh[:, kc, tb * 512:(tb + 1) * 512],
                                                start=(kc == 0), stop=(kc == DC - 1))
                            inc(last, "fmm", 1)

                @blk.scalar
                def _(s):
                    for hc in range(32):
                        for tb in range(2):
                            s.wait_ge(S("fmm"), b_mm + 2 * hc + tb + 1)
                            s.activation(out=u_ap(hc)[:, tb * 512:(tb + 1) * 512],
                                         in_=PS[2 * (hc % 2) + tb].ap()[:, 0:512], func=AF.Silu,
                                         bias=sm("fb1", 32)[:, hc:hc + 1])
                            inc(s.drain(), "fsl", 1)

            for ofq in range(4):
                ocs0 = ofq * 256
                b_wv = val("fwv"); b_mm = val("fmm")
                b_sr = val("fsr"); b_gt = val("fgt"); b_sl = val("fsl")
                b_pe = val("fpe")
                b_q = [val("dmq0") - 32] + [val(f"dmq{j}") for j in range(1, 4)]
                with nc.Block("l1f_fc2") as blk:
                    @blk.sync
                    def _(sy, ofq=ofq):
                        for kc in range(1, 32):
                            if kc >= 4:
                                sy.wait_ge(S("fwv"), b_wv + kc - 3)
                            src = fc2q[kc * 128:(kc + 1) * 128, ocs0:ocs0 + 256]
                            inc(sy.dma_start(out=view(WS, 4, 256)[:, kc % 4].bitcast(I32),
                                             in_=src), f"dmq{kc % 4}", 16)
                            inc(sy.dma_start(out=RSTG.ap()[0:1, 2048 + (kc % 4) * 256:2048 + (kc % 4) * 256 + 256],
                                             in_=s2_rows[kc:kc + 1, ocs0:ocs0 + 256]),
                                f"dmq{kc % 4}", 16)
                        if ofq < 3:
                            sy.wait_ge(S("fsr"), b_sr + 29)
                            nx0 = (ofq + 1) * 256
                            src = fc2q[0:128, nx0:nx0 + 256]
                            inc(sy.dma_start(out=view(WS, 4, 256)[:, 0].bitcast(I32),
                                             in_=src), "dmq0", 16)
                            inc(sy.dma_start(out=RSTG.ap()[0:1, 2048:2304],
                                             in_=s2_rows[0:1, nx0:nx0 + 256]), "dmq0", 16)

                    @blk.vector
                    def _(v):
                        for i in range(33):
                            if i < 32:
                                kc = i
                                v.wait_ge(S(f"dmq{kc % 4}"), b_q[kc % 4] + 32 * (kc // 4 + 1))
                                if kc >= 4:
                                    v.wait_ge(S("fpe"), b_pe + kc - 3)
                                v.tensor_scalar_add(out=view(WB, 4, 256, off=4096)[:, kc % 4],
                                                    in0=view(WS, 4, 256)[:, kc % 4].bitcast(I32),
                                                    scalar1=-128.0)
                                v.tensor_copy(out=BROW.ap()[0:1, (kc % 4) * 256:(kc % 4) * 256 + 256],
                                              in_=RSTG.ap()[0:1, 2048 + (kc % 4) * 256:2048 + (kc % 4) * 256 + 256])
                                inc(v.drain(), "fsr", 1)
                            if i >= 1:
                                kc = i - 1
                                v.wait_ge(S("fgt"), b_gt + kc + 1)
                                wbd = view(WB, 4, 256, off=4096)[:, kc % 4]
                                v.tensor_mul(out=wbd, in0=wbd,
                                             in1=PS[6 + (kc % 4) // 2].ap()[:, (kc % 2) * 256:(kc % 2) * 256 + 256])
                                inc(v.drain(), "fwv", 1)

                    @blk.tensor
                    def _(t):
                        for kc in range(32):
                            t.wait_ge(S("fsr"), b_sr + kc + 1)
                            if kc >= 4:
                                t.wait_ge(S("fwv"), b_wv + kc - 3)
                            srow = BROW.ap()[0:1, (kc % 4) * 256:(kc % 4) * 256 + 256]
                            t.matmul(PS[6 + (kc % 4) // 2].ap()[:, (kc % 2) * 256:(kc % 2) * 256 + 256],
                                     CONC.ap()[0:1, :], srow, start=True, stop=True)
                            inc(t.drain(), "fgt", 1)
                            t.wait_ge(S("fwv"), b_wv + kc + 1)
                            last = None
                            for jj2 in range(2):
                                for tb in range(2):
                                    last = t.matmul(PS[2 * jj2 + tb].ap()[:, 0:512],
                                                    view(WB, 4, 256, off=4096)[:, kc % 4, jj2 * 128:(jj2 + 1) * 128],
                                                    u_ap(kc)[:, tb * 512:(tb + 1) * 512],
                                                    start=(kc == 0), stop=(kc == 31))
                            inc(last, "fpe", 1)
                        inc(t.drain(), "fmm", 1)

                    @blk.scalar
                    def _(s, ofq=ofq):
                        s.wait_ge(S("fmm"), b_mm + 1)
                        ei = None
                        for jj2 in range(2):
                            ofc = ofq * 2 + jj2
                            for tb in range(2):
                                ei = s.activation(out=SB2.ap()[:, (2 * jj2 + tb) * 512:(2 * jj2 + tb + 1) * 512],
                                                  in_=PS[2 * jj2 + tb].ap()[:, 0:512],
                                                  func=AF.Identity, scale=0.5,
                                                  bias=sm("hfb2", 8)[:, ofc:ofc + 1])
                        inc(s.drain(), "fsl", 1)

                    @blk.vector
                    def _(v, ofq=ofq):
                        v.wait_ge(S("fsl"), b_sl + 1)
                        ei = None
                        for jj2 in range(2):
                            ofc = ofq * 2 + jj2
                            for tb in range(2):
                                dst = XRES.ap()[:, ofc, t0 + tb * 512:t0 + (tb + 1) * 512]
                                ei = v.tensor_add(out=dst,
                                                  in0=SB2.ap()[:, (2 * jj2 + tb) * 512:(2 * jj2 + tb + 1) * 512],
                                                  in1=dst)
                        inc(ei, "fev", 1)

            if half == 0:
                with nc.Block("sthalf") as blk:
                    @blk.sync
                    def _(sy):
                        yv = y_out.rearrange("(c p) t -> p c t", p=P)
                        inc(sy.dma_start(out=yv[:, :, 0:TH],
                                         in_=XRES.ap()[:, :, 0:TH]), "dm", 16)

    seq = [("L0A", l0_attn), ("L0F", l0_ffn), ("L1A", l1_attn), ("L1F", l1_ffn)]
    try:
        for name, fn in seq:
            fn()
            if stop_after == name:
                break
    except StopBuild:
        return nc

    with nc.Block("store") as blk:
        @blk.sync
        def _(sy):
            yv = y_out.rearrange("(c p) t -> p c t", p=P)
            inc(sy.dma_start(out=yv[:, :, TH:2 * TH],
                             in_=XRES.ap()[:, :, TH:2 * TH]), "dm", 16)
            sy.wait_ge(S("dm"), val("dm"))
            sy.wait_ge(S("dmx"), 16)

    return nc


def _prep_weights(inp):
    C = np.ascontiguousarray
    d = {}
    d["wq"] = C(inp["L0_attn_Wq"].T)
    d["wk"] = C(inp["L0_attn_Wk"].T)
    d["wv"] = C(inp["L0_attn_Wv"].T)
    d["wo"] = C(inp["L0_attn_Wo"].T)
    d["wg1"] = C(inp["L0_attn_Wg1"].T)
    d["wg2"] = C(inp["L0_attn_Wg2"].T)
    d["gn_col"] = C(inp["L0_attn_g_norm_w"].reshape(8, 128).T)
    d["w0g"] = C(inp["L0_ffn_w_gate"].T)
    d["w0u"] = C(inp["L0_ffn_w_up"].T)
    d["w0d"] = C(inp["L0_ffn_w_down"].T)
    d["pl"] = C(inp["L1_attn_proj_local"])
    d["pg"] = C(inp["L1_attn_proj_global"])
    d["wlin"] = C(inp["L1_attn_Wl_in"].T)
    d["wgin"] = C(inp["L1_attn_Wg_in"].T)
    d["wlout"] = C(inp["L1_attn_Wl_out"].T)
    d["wgout"] = C(inp["L1_attn_Wg_out"].T)
    bl = inp["L1_attn_bl_in"]; bg = inp["L1_attn_bg_in"]
    d["blq_col"] = C(bl[0:1024].reshape(8, 128).T)
    d["blk_col"] = C(bl[1024:2048].reshape(8, 128).T)
    d["blv_row"] = C(bl[2048:3072].reshape(1, 1024))
    d["bgq_col"] = C(bg[0:1024].reshape(8, 128).T)
    d["bgk_col"] = C(bg[1024:2048].reshape(8, 128).T)
    d["bgv_row"] = C(bg[2048:3072].reshape(1, 1024))
    d["blout_row"] = C(inp["L1_attn_bl_out"].reshape(1, 1024))
    d["bgout_row"] = C(inp["L1_attn_bg_out"].reshape(1, 1024))
    d["fusw"] = C(inp["L1_attn_fusion_w"].reshape(1, 2))
    d["g1_col"] = C(inp["L1_attn_ln_gamma"].reshape(8, 128).T)
    d["b1_col"] = C(inp["L1_attn_ln_beta"].reshape(8, 128).T)
    d["g2_col"] = C(inp["L1_ffn_ln_gamma"].reshape(8, 128).T)
    d["b2_col"] = C(inp["L1_ffn_ln_beta"].reshape(8, 128).T)
    d["fc1q"] = C(inp["L1_ffn_fc1_q"])
    d["fc2q"] = C(inp["L1_ffn_fc2_q"])
    s1 = inp["L1_ffn_fc1_scales"]
    d["s1_rows"] = C(np.stack([s1[:, hc * 128:(hc + 1) * 128].ravel()
                               for hc in range(32)]))
    d["s2_rows"] = C(inp["L1_ffn_fc2_scales"])
    d["fb1_col"] = C(inp["L1_ffn_fc1_bias"].reshape(32, 128).T)
    d["fb2_row"] = C(inp["L1_ffn_fc2_bias"].reshape(1, 1024))
    d["fb2_col"] = C(inp["L1_ffn_fc2_bias"].reshape(8, 128).T)
    return d


def kernel(_stop_after=None, _cores=None, **inputs):
    x = np.asarray(inputs["L0_attn_hidden_states"], dtype=np.float32)
    wd = _prep_weights({k: np.asarray(v) for k, v in inputs.items()})
    nc = build_program(stop_after=_stop_after)
    ncores = NCORES if _cores is None else _cores
    in_maps = []
    for b in range(ncores):
        m = dict(wd)
        m["x"] = np.ascontiguousarray(x[b].T)
        in_maps.append(m)
    res = run_bass_kernel_spmd(nc, in_maps, core_ids=list(range(ncores)))
    out = np.stack([np.asarray(r["y"], dtype=np.float32).T for r in res.results])
    if ncores < NCORES:
        pad = np.zeros((NCORES - ncores, T, 1024), np.float32)
        out = np.concatenate([out, pad], axis=0)
    return out
